# version 01
# baseline (speedup 1.0000x reference)
import contextlib
import os
from collections import defaultdict

import numpy as np
import ml_dtypes
import concourse.bass as bass
import concourse.mybir as mybir
from concourse.bass_utils import run_bass_kernel_spmd

F32 = mybir.dt.float32
BF16 = mybir.dt.bfloat16
I8 = mybir.dt.int8
AF = mybir.ActivationFunctionType
ALU = mybir.AluOpType
BFD = ml_dtypes.bfloat16

ENGINES = ("sync", "gpsimd", "scalar", "vector", "tensor")

E = 2048
S = 8192
H = 16
D = 128
NCH = 16
NCORES = 8
LN_EPS = 1e-5
L2_EPS = 1e-24
L0_SHIFT = -8.0
KILL = -1e30
RSCALE = float(1.0 / np.sqrt(D))
AQBIAS = float(-0.25 * np.log(2048.0))


def core_tiles(c):
    return [4 * c + j for j in range(4)] + [60 - 4 * c + j for j in range(4)]


RANK_TILES = [core_tiles(c) for c in range(NCORES)]


class Sem:
    def __init__(self, nc, stack, name):
        self.h = stack.enter_context(nc.semaphore(name))
        self.n = 0

    def inc(self, k=1):
        self.n += k
        return self.n


class Kb:
    def __init__(self, nc, stack, block, dbg=()):
        self.nc = nc
        self.stack = stack
        self.block = block
        self.dbg = set(dbg)
        self.dbg_tensors = {}
        self.pe = Sem(nc, stack, "s_pe")
        self.act = Sem(nc, stack, "s_act")
        self.dve = Sem(nc, stack, "s_dve")
        self.wdma = Sem(nc, stack, "s_wdma")
        self.ldr = Sem(nc, stack, "s_ldr")
        self.stg = Sem(nc, stack, "s_stg")
        self.coll = Sem(nc, stack, "s_coll")
        self.out = Sem(nc, stack, "s_out")
        self.w_s = [Sem(nc, stack, f"s_w{i}") for i in range(4)]
        self.ks_s = [Sem(nc, stack, f"s_ks{i}") for i in range(2)]
        self.vs_s = [Sem(nc, stack, f"s_vs{i}") for i in range(2)]
        self.stg_s = [Sem(nc, stack, f"s_stg{i}") for i in range(2)]
        self.rb_s = Sem(nc, stack, "s_rb")
        self.wslab = 0
        self.wfree = []
        self.ops = None
        self.ps_free = [None] * 4
        self.ps_rot = 0
        self.psw0_free = None
        self.psw1_free = None

    def sb(self, name, shape, dtype):
        return self.stack.enter_context(self.nc.sbuf_tensor(name, shape, dtype))

    def psum(self, name, shape, dtype=F32):
        return self.stack.enter_context(self.nc.psum_tensor(name, shape, dtype))

    def begin(self):
        assert self.ops is None
        self.ops = defaultdict(list)

    def end(self):
        ops = self.ops
        self.ops = None
        for name in ENGINES:
            lst = ops.get(name)
            if lst:
                def run(eng, lst=lst):
                    for f in lst:
                        f(eng)
                getattr(self.block, name)(run)

    def on(self, engine, f):
        self.ops[engine].append(f)

    def wload(self, src_ap_fn, nsub=None):
        k = self.wslab
        self.wslab += 1
        buf = k % 4
        free_ms = self.wfree[k - 4] if k >= 4 else None
        sem = self.w_s[buf]
        wv = sem.inc(16)

        def f(g, buf=buf, free_ms=free_ms, src_ap_fn=src_ap_fn, nsub=nsub, sem=sem):
            if free_ms:
                g.wait_ge(self.pe.h, free_ms)
            dst = self.W[buf][:, :]
            if nsub:
                dst = dst.rearrange("p (c f) -> p c f", c=nsub)
            g.dma_start(out=dst, in_=src_ap_fn()).then_inc(sem.h, 16)
        self.on("gpsimd", f)
        return buf, (sem, wv)

    def wconsumed(self, pe_ms):
        self.wfree.append(pe_ms)

    def next_ps(self):
        i = self.ps_rot % 4
        self.ps_rot += 1
        return i, self.ps_free[i]

    def R(self, slot, width=512):
        return self.ROWS[0:1, slot * 512: slot * 512 + width]

    def dump(self, tag, ap_fn, shape, dtype, waits):
        if tag not in self.dbg:
            return
        t = self.nc.dram_tensor("dbg_" + tag, shape, dtype, kind="ExternalOutput").ap()
        self.dbg_tensors[tag] = t
        tgt = self.out.inc(16)

        def f(sy, t=t, ap_fn=ap_fn, waits=list(waits)):
            for s, v in waits:
                sy.wait_ge(s.h, v)
            sy.dma_start(out=t[:, :], in_=ap_fn()).then_inc(self.out.h, 16)
        self.on("sync", f)


def emit_load(kb, dr):
    names = [("X32", "x"), ("COS0", "cos0"), ("SIN0", "sin0"),
             ("TRI", "tri"), ("R0T", "r0t"),
             ("ONEC32", "onec32"), ("ONECBF", "onecbf"),
             ("ONER32", "oner32"), ("ONERBF", "onerbf"),
             ("BT0", "biastab0"), ("BT1", "biastab1"),
             ("CV1", "cosv1"), ("SV1", "sinv1"), ("CONSTS", "consts")]
    small = {"l0attn": [("BQ", "l0_bq"), ("BK", "l0_bk"), ("BO", "l0_bo")],
             "l1attn": [("BQKV", "l1_bqkv"), ("BOUT", "l1_bout")],
             "l1ffn": [("SGT", "l1_sg"), ("SUT", "l1_su"), ("SDT", "l1_sd")]}
    for ph in kb.phases:
        names += small.get(ph, [])
    n_dma = len(names) + (4 if "l1attn" in kb.phases else 0)

    def f(sy, names=list(names)):
        for attr, key in names:
            sy.dma_start(out=getattr(kb, attr)[:, :], in_=dr[key][:, :]).then_inc(kb.ldr.h, 16)
        if "l1attn" in kb.phases:
            for i in range(4):
                sy.dma_start(out=kb.L1R[i][:, :], in_=dr["l1rot"][i]).then_inc(kb.ldr.h, 16)
    kb.on("sync", f)
    kb.loads_done = kb.ldr.inc(16 * n_dma)


def emit_ln_pre(kb, src, dst, entry):
    apply_ms = None
    for half in range(2):
        sq_ms = {}
        sumsq_ms = {}
        for c in range(NCH):
            slot = (c % 2) * 512

            def fsq(sc, c=c, slot=slot, half=half,
                    ent=list(entry) if (half == 0 and c == 0) else None,
                    w=sumsq_ms.get(c - 2)):
                if ent:
                    for s, v in ent:
                        sc.wait_ge(s.h, v)
                if w:
                    sc.wait_ge(kb.pe.h, w)
                sc.activation(kb.APL[:, slot:slot + 512],
                              src[:, c * 1024 + half * 512: c * 1024 + half * 512 + 512],
                              AF.Square).then_inc(kb.act.h, 1)
            kb.on("scalar", fsq)
            sq_ms[c] = kb.act.inc()

            def fmm(t, c=c, slot=slot, half=half, sqv=sq_ms[c],
                    ent=list(entry) if (half == 0 and c == 0) else None,
                    pw=kb.psw1_free if c == 0 else None):
                if ent:
                    for s, v in ent:
                        t.wait_ge(s.h, v)
                    t.wait_ge(kb.ldr.h, kb.loads_done)
                if pw:
                    t.wait_ge(pw[0].h, pw[1])
                t.matmul(kb.PSW1[0:1, 0:512], kb.ONEC32[:, 0:1],
                         src[:, c * 1024 + half * 512: c * 1024 + half * 512 + 512],
                         start=(c == 0), stop=(c == NCH - 1))
                t.wait_ge(kb.act.h, sqv)
                t.matmul(kb.PSW1[0:1, 512:1024], kb.ONEC32[:, 0:1],
                         kb.APL[:, slot:slot + 512],
                         start=(c == 0), stop=(c == NCH - 1)).then_inc(kb.pe.h, 1)
            kb.on("tensor", fmm)
            sumsq_ms[c] = kb.pe.inc()
        stats_ms = sumsq_ms[NCH - 1]

        def frows(v, stats_ms=stats_ms):
            v.wait_ge(kb.pe.h, stats_ms)
            v.tensor_scalar_mul(kb.R(0), kb.PSW1[0:1, 0:512], 1.0 / E)
            v.tensor_mul(kb.R(2), kb.R(0), kb.R(0))
            v.scalar_tensor_tensor(kb.R(1), kb.PSW1[0:1, 512:1024], 1.0 / E,
                                   kb.R(2), op0=ALU.mult, op1=ALU.subtract).then_inc(kb.dve.h, 1)
        kb.on("vector", frows)
        rows1_ms = kb.dve.inc()
        kb.psw1_free = (kb.dve, rows1_ms)

        def facts(sc, rows1_ms=rows1_ms):
            sc.wait_ge(kb.dve.h, rows1_ms)
            sc.activation(kb.R(2), kb.R(1), AF.Ln, bias=kb.CONSTS[0:1, 0:1])
            sc.activation(kb.R(1), kb.R(2), AF.Exp, scale=-0.5).then_inc(kb.act.h, 1)
        kb.on("scalar", facts)
        rstd_ms = kb.act.inc()

        def fc2(v, rstd_ms=rstd_ms):
            v.wait_ge(kb.act.h, rstd_ms)
            v.scalar_tensor_tensor(kb.R(2), kb.R(0), -1.0, kb.R(1),
                                   op0=ALU.mult, op1=ALU.mult).then_inc(kb.dve.h, 1)
        kb.on("vector", fc2)
        c2_ms = kb.dve.inc()

        def fbc(t, rstd_ms=rstd_ms, c2_ms=c2_ms, pw=kb.psw0_free):
            t.wait_ge(kb.act.h, rstd_ms)
            t.wait_ge(kb.dve.h, c2_ms)
            if pw:
                t.wait_ge(pw[0].h, pw[1])
            t.matmul(kb.PSW0[:, 0:512], kb.ONER32[0:1, :], kb.R(1), start=True, stop=True)
            t.matmul(kb.PSW0[:, 512:1024], kb.ONER32[0:1, :], kb.R(2),
                     start=True, stop=True).then_inc(kb.pe.h, 1)
        kb.on("tensor", fbc)
        bc_ms = kb.pe.inc()

        for c in range(NCH):
            def fap(v, c=c, half=half, bc_ms=bc_ms):
                if c == 0:
                    v.wait_ge(kb.pe.h, bc_ms)
                xs = src[:, c * 1024 + half * 512: c * 1024 + half * 512 + 512]
                v.tensor_mul(kb.SCR[:, 0:512], xs, kb.PSW0[:, 0:512])
                mm = v.tensor_add(dst[:, c * 1024 + half * 512: c * 1024 + half * 512 + 512],
                                  kb.SCR[:, 0:512], kb.PSW0[:, 512:1024])
                if c == NCH - 1:
                    mm.then_inc(kb.dve.h, 1)
            kb.on("vector", fap)
        apply_ms = kb.dve.inc()
        kb.psw0_free = (kb.dve, apply_ms)
    return apply_ms


def emit_ln_post_res(kb, ntok, dst_ap_fn, pe_ready):
    sq_act = {}
    mm_ms = {}
    for c in range(NCH):
        slot = 512 + (c % 2) * 256

        def fsq(sc, c=c, slot=slot, w=mm_ms.get(c - 2),
                pr=pe_ready if c == 0 else None):
            if pr:
                sc.wait_ge(pr[0].h, pr[1])
            if w:
                sc.wait_ge(kb.pe.h, w)
            sc.activation(kb.APL[:, slot:slot + 256],
                          kb.SCR[:, c * ntok: c * ntok + 256],
                          AF.Square).then_inc(kb.act.h, 1)
        kb.on("scalar", fsq)
        sqv = kb.act.inc()

        def fmm(t, c=c, slot=slot, sqv=sqv, pr=pe_ready if c == 0 else None,
                pw=kb.psw1_free if c == 0 else None):
            if pr:
                t.wait_ge(pr[0].h, pr[1])
            if pw:
                t.wait_ge(pw[0].h, pw[1])
            t.matmul(kb.PSW1[0:1, 0:256], kb.ONEC32[:, 0:1],
                     kb.SCR[:, c * ntok: c * ntok + 256],
                     start=(c == 0), stop=(c == NCH - 1))
            t.wait_ge(kb.act.h, sqv)
            t.matmul(kb.PSW1[0:1, 512:768], kb.ONEC32[:, 0:1],
                     kb.APL[:, slot:slot + 256],
                     start=(c == 0), stop=(c == NCH - 1)).then_inc(kb.pe.h, 1)
        kb.on("tensor", fmm)
        mm_ms[c] = kb.pe.inc()
    stats_ms = mm_ms[NCH - 1]

    def frows(v, stats_ms=stats_ms):
        v.wait_ge(kb.pe.h, stats_ms)
        v.tensor_scalar_mul(kb.R(0, 256), kb.PSW1[0:1, 0:256], 1.0 / E)
        v.tensor_mul(kb.R(2, 256), kb.R(0, 256), kb.R(0, 256))
        v.scalar_tensor_tensor(kb.R(1, 256), kb.PSW1[0:1, 512:768], 1.0 / E,
                               kb.R(2, 256), op0=ALU.mult, op1=ALU.subtract).then_inc(kb.dve.h, 1)
    kb.on("vector", frows)
    rows1_ms = kb.dve.inc()
    kb.psw1_free = (kb.dve, rows1_ms)

    def facts(sc, rows1_ms=rows1_ms):
        sc.wait_ge(kb.dve.h, rows1_ms)
        sc.activation(kb.R(2, 256), kb.R(1, 256), AF.Ln, bias=kb.CONSTS[0:1, 0:1])
        sc.activation(kb.R(1, 256), kb.R(2, 256), AF.Exp, scale=-0.5).then_inc(kb.act.h, 1)
    kb.on("scalar", facts)
    rstd_ms = kb.act.inc()

    def fc2(v, rstd_ms=rstd_ms):
        v.wait_ge(kb.act.h, rstd_ms)
        v.scalar_tensor_tensor(kb.R(2, 256), kb.R(0, 256), -1.0, kb.R(1, 256),
                               op0=ALU.mult, op1=ALU.mult).then_inc(kb.dve.h, 1)
    kb.on("vector", fc2)
    c2_ms = kb.dve.inc()

    def fbc(t, rstd_ms=rstd_ms, c2_ms=c2_ms, pw=kb.psw0_free):
        t.wait_ge(kb.act.h, rstd_ms)
        t.wait_ge(kb.dve.h, c2_ms)
        if pw:
            t.wait_ge(pw[0].h, pw[1])
        t.matmul(kb.PSW0[:, 0:256], kb.ONER32[0:1, :], kb.R(1, 256), start=True, stop=True)
        t.matmul(kb.PSW0[:, 512:768], kb.ONER32[0:1, :], kb.R(2, 256),
                 start=True, stop=True).then_inc(kb.pe.h, 1)
    kb.on("tensor", fbc)
    bc_ms = kb.pe.inc()

    for c in range(NCH):
        def fap(v, c=c, bc_ms=bc_ms):
            if c == 0:
                v.wait_ge(kb.pe.h, bc_ms)
            s_ = kb.SCR[:, c * ntok: c * ntok + 256]
            v.tensor_mul(kb.APL[:, 0:256], s_, kb.PSW0[:, 0:256])
            v.tensor_add(kb.APL[:, 256:512], kb.APL[:, 0:256], kb.PSW0[:, 512:768])
            dst = dst_ap_fn(c)
            mm = v.tensor_add(dst, dst, kb.APL[:, 256:512])
            if c == NCH - 1:
                mm.then_inc(kb.dve.h, 1)
        kb.on("vector", fap)
    apply_ms = kb.dve.inc()
    kb.psw0_free = (kb.dve, apply_ms)
    return apply_ms


def emit_v_proj(kb, dr, L, xn_ms):
    wsrc = dr["l0_wv"] if L == 0 else dr["l1_wqkvt"]
    col0 = 0 if L == 0 else 4096
    bvflat = dr["l0_bv_flat"] if L == 0 else dr["l1_bv_flat"]
    vin = kb.vin[L]
    evac_ms_prev = None
    prev_mm = None
    for idx in range(8):
        half_t, foB = idx // 4, idx % 4
        rb = 0

        def fbias(g, rb=rb, foB=foB, pm=prev_mm):
            if pm:
                g.wait_ge(kb.pe.h, pm)
            g.dma_start(out=kb.ROWSBF[0:1, rb:rb + 512],
                        in_=bvflat[0:1, foB * 512:(foB + 1) * 512]).then_inc(kb.rb_s.h, 16)
        kb.on("gpsimd", fbias)
        bias_v = kb.rb_s.inc(16)

        slabs = [kb.wload(
            lambda cq=cq, foB=foB: wsrc[cq * 512:(cq + 1) * 512,
                                        col0 + foB * 512: col0 + (foB + 1) * 512]
            .rearrange("(s p) f -> p s f", p=128), nsub=4) for cq in range(4)]

        def fmm(t, slabs=slabs, half_t=half_t, foB=foB, bias_v=bias_v, rb=rb,
                xm=xn_ms if idx == 0 else None, ev_prev=evac_ms_prev):
            if xm:
                t.wait_ge(kb.dve.h, xm)
            if ev_prev:
                t.wait_ge(kb.dve.h, ev_prev)
            for cq, (buf, wv) in enumerate(slabs):
                t.wait_ge(wv[0].h, wv[1])
                for c4 in range(4):
                    for tt in range(4):
                        t.matmul(kb.PS[tt][:, :],
                                 kb.A[:, (4 * cq + c4) * 1024 + (half_t * 4 + tt) * 128:
                                      (4 * cq + c4) * 1024 + (half_t * 4 + tt) * 128 + 128],
                                 kb.W[buf][:, c4 * 512:(c4 + 1) * 512],
                                 start=(cq == 0 and c4 == 0), stop=False)
            t.wait_ge(kb.rb_s.h, bias_v)
            for tt in range(4):
                mm = t.matmul(kb.PS[tt][:, :], kb.ONERBF[0:1, :],
                              kb.ROWSBF[0:1, rb:rb + 512], start=False, stop=True)
                if tt == 3:
                    mm.then_inc(kb.pe.h, 1)
        kb.on("tensor", fmm)
        mm_ms = kb.pe.inc()
        for _ in range(4):
            kb.wconsumed(mm_ms)

        sp = idx % 2
        st_w = kb.stg_s[sp].n if idx >= 2 else 0

        def fev(v, mm_ms=mm_ms, sp=sp, st_w=st_w):
            v.wait_ge(kb.pe.h, mm_ms)
            if st_w:
                v.wait_ge(kb.stg_s[sp].h, st_w)
            for tt in range(4):
                mm = v.tensor_copy(kb.STG[:, sp * 2048 + tt * 512: sp * 2048 + (tt + 1) * 512],
                                   kb.PS[tt][:, :])
                if tt == 3:
                    mm.then_inc(kb.dve.h, 1)
        kb.on("vector", fev)
        evac_ms = kb.dve.inc()
        evac_ms_prev = evac_ms
        prev_mm = mm_ms

        def fst(sc, evac_ms=evac_ms, half_t=half_t, foB=foB, sp=sp):
            sc.wait_ge(kb.dve.h, evac_ms)
            for tt in range(4):
                sc.dma_start(
                    out=vin[half_t * 512 + tt * 128: half_t * 512 + (tt + 1) * 128,
                            foB * 512:(foB + 1) * 512],
                    in_=kb.STG[:, sp * 2048 + tt * 512: sp * 2048 + (tt + 1) * 512]
                ).then_inc(kb.stg_s[sp].h, 16)
        kb.on("scalar", fst)
        kb.stg_s[sp].inc(64)
    for i in range(4):
        kb.ps_free[i] = (kb.dve, evac_ms_prev)


def emit_kq_proj(kb, dr, L, which, b_free=None):
    if L == 0:
        wsrc = dr["l0_wq"] if which == "q" else dr["l0_wk"]
        col0 = 0
        bias_t, bias_off = ("BQ" if which == "q" else "BK"), 0
    else:
        wsrc = dr["l1_wqkvt"]
        col0 = 0 if which == "q" else 2048
        bias_t, bias_off = "BQKV", (0 if which == "q" else 16)
    bias = getattr(kb, bias_t)
    ev_ms = {}
    for fo in range(NCH):
        buf, wv = kb.wload(lambda fo=fo: wsrc[:, col0 + fo * 128: col0 + (fo + 1) * 128]
                           .rearrange("(c p) f -> p c f", p=128), nsub=16)
        for half in range(2):
            p, free = kb.next_ps()

            def fmm(t, buf=buf, wv=wv, p=p, free=free, fo=fo, half=half):
                t.wait_ge(wv[0].h, wv[1])
                if free is not None:
                    t.wait_ge(free[0].h, free[1])
                for c in range(NCH):
                    mm = t.matmul(kb.PS[p][:, :], kb.W[buf][:, c * 128:(c + 1) * 128],
                                  kb.A[:, c * 1024 + half * 512: c * 1024 + half * 512 + 512],
                                  start=(c == 0), stop=(c == NCH - 1))
                    if c == NCH - 1:
                        mm.then_inc(kb.pe.h, 1)
            kb.on("tensor", fmm)
            mm_ms = kb.pe.inc()
            if half == 1:
                kb.wconsumed(mm_ms)

            def fev(v, p=p, fo=fo, half=half, mm_ms=mm_ms,
                    bf=b_free if (fo == 0 and half == 0) else None):
                v.wait_ge(kb.pe.h, mm_ms)
                if bf:
                    for s, vv in bf:
                        v.wait_ge(s.h, vv)
                v.tensor_scalar_add(
                    kb.B[:, fo * 1024 + half * 512: fo * 1024 + half * 512 + 512],
                    kb.PS[p][:, :],
                    bias[:, bias_off + fo: bias_off + fo + 1]).then_inc(kb.dve.h, 1)
            kb.on("vector", fev)
            ev = kb.dve.inc()
            ev_ms[(fo, half)] = ev
            kb.ps_free[p] = (kb.dve, ev)
    return ev_ms


def emit_rope_l0(kb, raw_ms, dst, stage, kdma=None):
    out_ms = {}
    st_prev = {}
    for fo in range(NCH):
        sp = fo % 2
        for half in range(2):
            p, free = kb.next_ps()

            def frot(t, p=p, free=free, fo=fo, half=half, rms=raw_ms[(fo, half)]):
                t.wait_ge(kb.dve.h, rms)
                if free is not None:
                    t.wait_ge(free[0].h, free[1])
                t.matmul(kb.PS[p][:, :], kb.R0T[:, :],
                         kb.B[:, fo * 1024 + half * 512: fo * 1024 + half * 512 + 512],
                         start=True, stop=True).then_inc(kb.pe.h, 1)
            kb.on("tensor", frot)
            rot_ms = kb.pe.inc()

            def fcomb(v, p=p, fo=fo, half=half, rot_ms=rot_ms, sp=sp,
                      stw=st_prev.get(fo - 2)):
                v.wait_ge(kb.pe.h, rot_ms)
                if stw:
                    v.wait_ge(kb.stg_s[sp].h, stw)
                hs = half * 512
                v.tensor_mul(kb.APL[:, 0:512], kb.PS[p][:, :], kb.SIN0[:, hs:hs + 512])
                v.tensor_mul(kb.APL[:, 512:1024],
                             kb.B[:, fo * 1024 + hs: fo * 1024 + hs + 512],
                             kb.COS0[:, hs:hs + 512])
                if stage:
                    tgt = kb.STG[:, sp * 2048 + hs: sp * 2048 + hs + 512]
                else:
                    tgt = dst[:, fo * 1024 + hs: fo * 1024 + hs + 512]
                v.tensor_add(tgt, kb.APL[:, 0:512], kb.APL[:, 512:1024]).then_inc(kb.dve.h, 1)
            kb.on("vector", fcomb)
            cm = kb.dve.inc()
            out_ms[(fo, half)] = cm
            kb.ps_free[p] = (kb.dve, cm)
        if stage:
            def fst(sc, fo=fo, sp=sp, cm=out_ms[(fo, 1)]):
                sc.wait_ge(kb.dve.h, cm)
                sc.dma_start(out=kdma[fo * 128:(fo + 1) * 128, :],
                             in_=kb.STG[:, sp * 2048: sp * 2048 + 1024]).then_inc(kb.stg_s[sp].h, 16)
            kb.on("scalar", fst)
            st_prev[fo] = kb.stg_s[sp].inc(16)
    return out_ms


def l1_sources(fo):
    if fo < 8:
        return 2 * fo, 2 * fo + 1, 0, 1
    return 2 * (fo - 8), 2 * (fo - 8) + 1, 2, 3


def emit_rope_l1(kb, raw_ms, to_a, park_aps):
    out_ms = {}
    loc = {}
    for fo in range(NCH):
        s0, s1, p0, p1 = l1_sources(fo)
        p, free = kb.next_ps()
        qb = fo % 2

        def frot(t, p=p, free=free, s0=s0, s1=s1, p0=p0, p1=p1,
                 w=max(raw_ms[(s0, 1)], raw_ms[(s1, 1)])):
            t.wait_ge(kb.dve.h, w)
            if free is not None:
                t.wait_ge(free[0].h, free[1])
            t.matmul(kb.PS[p][:, :], kb.L1R[p0][:, :],
                     kb.B[:, s0 * 1024: s0 * 1024 + 512], start=True, stop=False)
            t.matmul(kb.PS[p][:, :], kb.L1R[p1][:, :],
                     kb.B[:, s1 * 1024: s1 * 1024 + 512], start=False, stop=True).then_inc(kb.pe.h, 1)
        kb.on("tensor", frot)
        rot1 = kb.pe.inc()
        p2, free2 = kb.next_ps()

        def frot2(t, p=p2, free=free2, s0=s0, s1=s1, p0=p0, p1=p1):
            if free is not None:
                t.wait_ge(free[0].h, free[1])
            t.matmul(kb.PS[p][:, :], kb.L1R[p0][:, :],
                     kb.B[:, s0 * 1024 + 512: (s0 + 1) * 1024], start=True, stop=False)
            t.matmul(kb.PS[p][:, :], kb.L1R[p1][:, :],
                     kb.B[:, s1 * 1024 + 512: (s1 + 1) * 1024],
                     start=False, stop=True).then_inc(kb.pe.h, 1)
        kb.on("tensor", frot2)
        rot2 = kb.pe.inc()

        def fqcos(sc, fo=fo, qb=qb, rm=raw_ms[(fo, 1)]):
            sc.wait_ge(kb.dve.h, rm)
            sc.activation(kb.STG[:, qb * 1024:(qb + 1) * 1024],
                          kb.B[:, fo * 1024:(fo + 1) * 1024],
                          AF.Copy, scale=kb.CV1[:, fo:fo + 1]).then_inc(kb.act.h, 1)
        kb.on("scalar", fqcos)
        qc_ms = kb.act.inc()

        if to_a:
            tgt = kb.A[:, fo * 1024:(fo + 1) * 1024]
        elif fo < 8:
            tgt = park_aps[fo]
        else:
            tgt = kb.B[:, 2 * (fo - 8) * 1024: (2 * (fo - 8) + 1) * 1024]
        loc[fo] = tgt

        def fcomb(v, p=p, p2=p2, fo=fo, qb=qb, rot2=rot2, qc_ms=qc_ms, tgt=tgt):
            v.wait_ge(kb.pe.h, rot2)
            v.wait_ge(kb.act.h, qc_ms)
            v.scalar_tensor_tensor(tgt[:, 0:512], kb.PS[p][:, :], kb.SV1[:, fo:fo + 1],
                                   kb.STG[:, qb * 1024: qb * 1024 + 512],
                                   op0=ALU.mult, op1=ALU.add)
            v.scalar_tensor_tensor(tgt[:, 512:1024], kb.PS[p2][:, :], kb.SV1[:, fo:fo + 1],
                                   kb.STG[:, qb * 1024 + 512: (qb + 1) * 1024],
                                   op0=ALU.mult, op1=ALU.add).then_inc(kb.dve.h, 1)
        kb.on("vector", fcomb)
        cm = kb.dve.inc()
        out_ms[fo] = cm
        kb.ps_free[p] = (kb.dve, cm)
        kb.ps_free[p2] = (kb.dve, cm)
    return out_ms, loc


def emit_l2_scale(kb, out_ms, loc, stage_dma, kdma):
    sq_pe = {}
    for fo in range(NCH):
        qb = 2 + (fo % 2)

        def fsq(sc, fo=fo, qb=qb, cm=out_ms[fo], w=sq_pe.get(fo - 2)):
            sc.wait_ge(kb.dve.h, cm)
            if w:
                sc.wait_ge(kb.pe.h, w)
            sc.activation(kb.STG[:, qb * 1024:(qb + 1) * 1024], loc[fo],
                          AF.Square).then_inc(kb.act.h, 1)
        kb.on("scalar", fsq)
        sqv = kb.act.inc()

        def fmm(t, fo=fo, qb=qb, sqv=sqv, pw=kb.psw1_free if fo == 0 else None):
            if pw:
                t.wait_ge(pw[0].h, pw[1])
            t.wait_ge(kb.act.h, sqv)
            t.matmul(kb.PSW1[0:1, 0:512], kb.ONECBF[:, 0:1],
                     kb.STG[:, qb * 1024: qb * 1024 + 512],
                     start=(fo == 0), stop=(fo == NCH - 1))
            t.matmul(kb.PSW1[0:1, 512:1024], kb.ONECBF[:, 0:1],
                     kb.STG[:, qb * 1024 + 512: (qb + 1) * 1024],
                     start=(fo == 0), stop=(fo == NCH - 1)).then_inc(kb.pe.h, 1)
        kb.on("tensor", fmm)
        sq_pe[fo] = kb.pe.inc()
    ss_ms = sq_pe[NCH - 1]

    def facts(sc, ss_ms=ss_ms):
        sc.wait_ge(kb.pe.h, ss_ms)
        sc.activation(kb.R(0), kb.PSW1[0:1, 0:512], AF.Ln, bias=kb.CONSTS[0:1, 1:2])
        sc.activation(kb.R(1), kb.R(0), AF.Exp, scale=-0.5, bias=kb.CONSTS[0:1, 2:3])
        sc.activation(kb.R(2), kb.PSW1[0:1, 512:1024], AF.Ln, bias=kb.CONSTS[0:1, 1:2])
        sc.activation(kb.R(0), kb.R(2), AF.Exp, scale=-0.5, bias=kb.CONSTS[0:1, 2:3]).then_inc(kb.act.h, 1)
    kb.on("scalar", facts)
    a_ms = kb.act.inc()
    kb.psw1_free = (kb.act, a_ms)

    def fbc(t, a_ms=a_ms, pw=kb.psw0_free):
        t.wait_ge(kb.act.h, a_ms)
        if pw:
            t.wait_ge(pw[0].h, pw[1])
        t.matmul(kb.PSW0[:, 0:512], kb.ONER32[0:1, :], kb.R(1), start=True, stop=True)
        t.matmul(kb.PSW0[:, 512:1024], kb.ONER32[0:1, :], kb.R(0),
                 start=True, stop=True).then_inc(kb.pe.h, 1)
    kb.on("tensor", fbc)
    bc_ms = kb.pe.inc()

    last = None
    for fo in range(NCH):
        def fsc(v, fo=fo, bc_ms=bc_ms):
            if fo == 0:
                v.wait_ge(kb.pe.h, bc_ms)
            t = loc[fo]
            v.tensor_mul(t[:, 0:512], t[:, 0:512], kb.PSW0[:, 0:512])
            v.tensor_mul(t[:, 512:1024], t[:, 512:1024],
                         kb.PSW0[:, 512:1024]).then_inc(kb.dve.h, 1)
        kb.on("vector", fsc)
        last = kb.dve.inc()
        if stage_dma:
            def fst(sc, fo=fo, last=last):
                sc.wait_ge(kb.dve.h, last)
                sc.dma_start(out=kdma[fo * 128:(fo + 1) * 128, :],
                             in_=loc[fo]).then_inc(kb.stg_s[fo % 2].h, 16)
            kb.on("scalar", fst)
            kb.stg_s[fo % 2].inc(16)
    kb.psw0_free = (kb.dve, last)
    return last


def emit_gather(kb, L):
    stg_v = (kb.stg_s[0].n, kb.stg_s[1].n)

    def f(g, stg_v=stg_v, L=L):
        if stg_v[0]:
            g.wait_ge(kb.stg_s[0].h, stg_v[0])
        if stg_v[1]:
            g.wait_ge(kb.stg_s[1].h, stg_v[1])
        g.collective_compute("AllGather", ALU.bypass,
                             replica_groups=[list(range(NCORES))],
                             ins=[kb.kin[L][:, :]], outs=[kb.kout[L][:, :]]).then_inc(kb.coll.h, 1)
        g.collective_compute("AllGather", ALU.bypass,
                             replica_groups=[list(range(NCORES))],
                             ins=[kb.vin[L][:, :]], outs=[kb.vout[L][:, :]]).then_inc(kb.coll.h, 1)
    kb.on("gpsimd", f)
    return kb.coll.inc(2)


def build_head_pairs():
    pairs = []
    for r in range(8):
        tiles = RANK_TILES[r]
        a_items = [(loc, 0, tiles[loc]) for loc in range(4) if tiles[loc] < 28]
        b_items = [(loc, 1, tiles[loc]) for loc in range(8) if tiles[loc] < 60]
        for i in range(0, len(a_items), 2):
            pairs.append({"kind": "past", "r": r, "items": a_items[i:i + 2]})
        for i in range(0, len(b_items), 2):
            pairs.append({"kind": "past", "r": r, "items": b_items[i:i + 2]})
    for rho in range(4):
        pairs.append({"kind": "diag", "rho": rho})
    return pairs


HEAD_PAIRS = build_head_pairs()


def select_pairs():
    sel = os.environ.get("K_PAIRSEL", "all")
    if sel == "all":
        return HEAD_PAIRS
    if sel == "diag":
        return [p for p in HEAD_PAIRS if p["kind"] == "diag"]
    if sel.startswith("first:"):
        return HEAD_PAIRS[:int(sel.split(":")[1])] +             [p for p in HEAD_PAIRS if p["kind"] == "diag"]
    if sel.startswith("rank:"):
        rs = set(int(x) for x in sel.split(":")[1].split("-"))
        return [p for p in HEAD_PAIRS if p["kind"] == "diag" or p["r"] in rs]
    if sel == "grpA":
        return [p for p in HEAD_PAIRS if p["kind"] == "diag"
                or p["items"][0][1] == 0]
    if sel == "grpB":
        return [p for p in HEAD_PAIRS if p["kind"] == "diag"
                or p["items"][0][1] == 1]
    raise ValueError(sel)


SEL_PAIRS = select_pairs()
NPAIRS = len(SEL_PAIRS)


def emit_attention(kb, L, entry_coll):
    scale = RSCALE if L == 0 else 1.0
    biasc = L0_SHIFT if L == 0 else 0.0
    BT = kb.BT0 if L == 0 else kb.BT1
    kout, vout = kb.kout[L], kb.vout[L]
    kin, vin = kb.kin[L], kb.vin[L]

    ks_last = {}
    vs_last = {}
    ks_lds = {}
    vs_lds = {}
    ks_owner = {}
    vs_owner = {}
    epi_dve = {0: kb.ps_free[0], 1: kb.ps_free[1]}
    epi_cp = {0: kb.ps_free[2], 1: kb.ps_free[3]}
    epi_bc_ms = {0: None, 1: None}

    def ensure_k(h, r):
        key = h * 9 + r
        if key in ks_lds:
            return
        kbuf = key % 2
        prev = ks_owner.get(kbuf)
        ks_owner[kbuf] = key

        def fk(sy, r=r, h=h, kbuf=kbuf, w=ks_last.get(prev), ec=entry_coll):
            sy.wait_ge(kb.coll.h, ec)
            if w:
                sy.wait_ge(kb.pe.h, w)
            if r < 8:
                src = kout[r * 2048 + h * 128: r * 2048 + (h + 1) * 128, :]
            else:
                src = kin[h * 128:(h + 1) * 128, :]
            sy.dma_start(out=kb.KS[kbuf][:, :], in_=src).then_inc(kb.ks_s[kbuf].h, 16)
        kb.on("sync", fk)
        ks_lds[key] = (kb.ks_s[kbuf], kb.ks_s[kbuf].inc(16))

    def ensure_v(h, r):
        key = h * 9 + r
        if key in vs_lds:
            return
        vbuf = key % 2
        prev = vs_owner.get(vbuf)
        vs_owner[vbuf] = key

        def fv(sy, r=r, h=h, vbuf=vbuf, w=vs_last.get(prev), ec=entry_coll):
            sy.wait_ge(kb.coll.h, ec)
            if w:
                sy.wait_ge(kb.pe.h, w)
            if r < 8:
                src = vout[r * 1024:(r + 1) * 1024,
                           h * 128:(h + 1) * 128].rearrange("(j p) d -> p j d", p=128)
            else:
                src = vin[:, h * 128:(h + 1) * 128].rearrange("(j p) d -> p j d", p=128)
            sy.dma_start(out=kb.VS[vbuf][:, :].rearrange("p (j d) -> p j d", j=8),
                         in_=src).then_inc(kb.vs_s[vbuf].h, 16)
        kb.on("sync", fv)
        vs_lds[key] = (kb.vs_s[vbuf], kb.vs_s[vbuf].inc(16))

    for h in range(H):
        hp = h // 2
        exp_ms = {}
        qk_ms = {}
        pv_ms = {}
        mask_ms = {}
        pv_started = {0: False, 1: False}

        def pair_rank(pi):
            return SEL_PAIRS[pi].get("r", 8)

        def pair_items(pi):
            pr = SEL_PAIRS[pi]
            if pr["kind"] == "past":
                return [(it[1], it[2], it[0], 0) for it in pr["items"]]
            rho = pr["rho"]
            return [(0, None, rho, rho * 128), (1, None, 4 + rho, rho * 128)]

        def emit_pv(pi):
            items = pair_items(pi)
            r = pair_rank(pi)
            sw = pi % 2
            vbuf = (h * 9 + r) % 2
            em = exp_ms[pi]
            dm = mask_ms.get(pi)
            starts = []
            for (grp, kt, loc, q0) in items:
                starts.append(not pv_started[grp])
                pv_started[grp] = True

            def fpv(t, items=items, sw=sw, r=r, vbuf=vbuf, h=h, em=em, dm=dm,
                    starts=tuple(starts), vld=vs_lds[h * 9 + r],
                    _unused=None,
                    epi_d=(epi_dve[0], epi_dve[1]),
                    epi_c=(epi_cp[0], epi_cp[1])):
                t.wait_ge(kb.act.h, em)
                if dm:
                    t.wait_ge(kb.dve.h, dm)
                t.wait_ge(vld[0].h, vld[1])
                last_mm = None
                for ix, (grp, kt, loc, q0) in enumerate(items):
                    st = starts[ix]
                    if st and epi_d[grp]:
                        t.wait_ge(epi_d[grp][0].h, epi_d[grp][1])
                    if st and epi_c[grp]:
                        t.wait_ge(epi_c[grp][0].h, epi_c[grp][1])
                    n = 512 - q0
                    vch = kb.VS[vbuf][:, loc * 128:(loc + 1) * 128]
                    pt = kb.PT[sw][:, ix * 512: ix * 512 + n]
                    t.matmul(kb.PS[grp][:, q0:512], vch, pt, start=st, stop=False)
                    last_mm = t.matmul(kb.PS[2 + grp][0:1, q0:512], kb.ONECBF[:, 0:1], pt,
                                       start=st, stop=False)
                last_mm.then_inc(kb.pe.h, 1)
            kb.on("tensor", fpv)
            ms = kb.pe.inc()
            vs_last[h * 9 + r] = ms
            return ms

        for pi in range(NPAIRS):
            pr = SEL_PAIRS[pi]
            r = pair_rank(pi)
            ensure_k(h, r)
            ensure_v(h, r)
            sw = pi % 2
            psw = kb.PSW0 if sw == 0 else kb.PSW1
            items = pair_items(pi)
            if pr["kind"] == "past":
                bcol = (0 if items[0][0] == 0 else 32) + items[0][1] // 2
            else:
                bcol = None
            kbuf = (h * 9 + r) % 2

            ser = (os.environ.get("K_SERIAL") and pi == 0 and h > 0)

            def fqk(t, items=items, sw=sw, psw=psw, kbuf=kbuf, h=h,
                    kld=ks_lds[h * 9 + r], expw=exp_ms.get(pi - 2),
                    pw=(kb.psw0_free if sw == 0 else kb.psw1_free)
                    if (h == 0 and pi < 2) else None,
                    serw=(epi_cp[0], epi_cp[1]) if ser else None):
                t.wait_ge(kld[0].h, kld[1])
                if serw:
                    for s in serw:
                        if s:
                            t.wait_ge(s[0].h, s[1])
                if expw:
                    t.wait_ge(kb.act.h, expw)
                if pw:
                    t.wait_ge(pw[0].h, pw[1])
                for ix, (grp, kt, loc, q0) in enumerate(items):
                    n = 512 - q0
                    kch = kb.KS[kbuf][:, loc * 128:(loc + 1) * 128]
                    q = kb.A[:, h * 1024 + grp * 512 + q0: h * 1024 + (grp + 1) * 512]
                    mm = t.matmul(psw[:, ix * 512: ix * 512 + n], kch, q,
                                  start=True, stop=True)
                    if ix == len(items) - 1:
                        mm.then_inc(kb.pe.h, 1)
            kb.on("tensor", fqk)
            qk_ms[pi] = kb.pe.inc()
            ks_last[h * 9 + r] = qk_ms[pi]

            wid = 512 + (512 - items[-1][3])

            def fexp(sc, sw=sw, psw=psw, wid=wid, bcol=bcol,
                     qv=qk_ms[pi], pvw=pv_ms.get(pi - 2)):
                sc.wait_ge(kb.pe.h, qv)
                if pvw:
                    sc.wait_ge(kb.pe.h, pvw)
                bias = BT[:, bcol:bcol + 1] if bcol is not None else BT[:, 62:63]
                sc.activation(kb.PT[sw][:, 0:wid], psw[:, 0:wid], AF.Exp,
                              scale=scale, bias=bias).then_inc(kb.act.h, 1)
            kb.on("scalar", fexp)
            exp_ms[pi] = kb.act.inc()

            if pr["kind"] == "diag":
                def fmask(v, sw=sw, ev=exp_ms[pi]):
                    v.wait_ge(kb.act.h, ev)
                    v.tensor_mul(kb.PT[sw][:, 0:128], kb.PT[sw][:, 0:128], kb.TRI[:, :])
                    v.tensor_mul(kb.PT[sw][:, 512:640], kb.PT[sw][:, 512:640],
                                 kb.TRI[:, :]).then_inc(kb.dve.h, 1)
                kb.on("vector", fmask)
                mask_ms[pi] = kb.dve.inc()

            if os.environ.get("K_NOLOOK"):
                pv_ms[pi] = emit_pv(pi)
            elif pi >= 1:
                pv_ms[pi - 1] = emit_pv(pi - 1)
            if pi + 1 < NPAIRS:
                ensure_k(h, pair_rank(pi + 1))
                ensure_v(h, pair_rank(pi + 1))
        if not os.environ.get("K_NOLOOK"):
            pv_ms[NPAIRS - 1] = emit_pv(NPAIRS - 1)
        head_end = pv_ms[NPAIRS - 1]

        for grp in range(2):
            rs0, rs1 = (0, 1) if grp == 0 else (2, 0)

            def fep_act(sc, grp=grp, rs0=rs0, rs1=rs1, he=head_end,
                        bw=epi_bc_ms[grp],
                        aw=epi_bc_ms[0] if grp == 1 else None):
                sc.wait_ge(kb.pe.h, he)
                if bw:
                    sc.wait_ge(kb.pe.h, bw)
                if aw:
                    sc.wait_ge(kb.pe.h, aw)
                sc.activation(kb.R(rs0), kb.PS[2 + grp][0:1, :], AF.Ln)
                sc.activation(kb.R(rs1), kb.R(rs0), AF.Exp, scale=-1.0).then_inc(kb.act.h, 1)
            kb.on("scalar", fep_act)
            rec_ms = kb.act.inc()

            def fep_bc(t, grp=grp, rs1=rs1, rec_ms=rec_ms):
                t.wait_ge(kb.act.h, rec_ms)
                t.matmul(kb.PS[2 + grp][:, :], kb.ONER32[0:1, :], kb.R(rs1),
                         start=True, stop=True).then_inc(kb.pe.h, 1)
            kb.on("tensor", fep_bc)
            epi_bc_ms[grp] = kb.pe.inc()

            def fep_dve(v, grp=grp, h=h, bc=epi_bc_ms[grp]):
                v.wait_ge(kb.pe.h, bc)
                v.tensor_copy(kb.APL[:, grp * 512:(grp + 1) * 512], kb.PS[2 + grp][:, :])
                v.tensor_mul(kb.B[:, h * 1024 + grp * 512: h * 1024 + (grp + 1) * 512],
                             kb.PS[grp][:, :],
                             kb.APL[:, grp * 512:(grp + 1) * 512]).then_inc(kb.dve.h, 1)
            kb.on("vector", fep_dve)
            cp = kb.dve.inc()
            epi_cp[grp] = (kb.dve, cp)
            epi_dve[grp] = (kb.dve, cp)
    for i in range(4):
        kb.ps_free[i] = epi_cp[i % 2]
    kb.psw0_free = (kb.act, exp_ms[NPAIRS - 2])
    kb.psw1_free = (kb.act, exp_ms[NPAIRS - 1])
    return kb.dve.n


def emit_outproj(kb, dr, L, ao_ms):
    wsrc = dr["l0_wo"] if L == 0 else dr["l1_woutt"]
    bias = kb.BO if L == 0 else kb.BOUT
    res_ms = None
    for g in range(4):
        ev_last = None
        for fo in range(NCH):
            buf, wv = kb.wload(lambda fo=fo: wsrc[:, fo * 128:(fo + 1) * 128]
                               .rearrange("(c p) f -> p c f", p=128), nsub=16)
            p, free = kb.next_ps()

            def fmm(t, buf=buf, wv=wv, p=p, free=free, g=g, fo=fo,
                    am=ao_ms if (g == 0 and fo == 0) else None):
                if am:
                    t.wait_ge(kb.dve.h, am)
                t.wait_ge(wv[0].h, wv[1])
                if free is not None:
                    t.wait_ge(free[0].h, free[1])
                for c in range(NCH):
                    mm = t.matmul(kb.PS[p][:, 0:256], kb.W[buf][:, c * 128:(c + 1) * 128],
                                  kb.B[:, c * 1024 + g * 256: c * 1024 + (g + 1) * 256],
                                  start=(c == 0), stop=(c == NCH - 1))
                    if c == NCH - 1:
                        mm.then_inc(kb.pe.h, 1)
            kb.on("tensor", fmm)
            mm_ms = kb.pe.inc()
            kb.wconsumed(mm_ms)

            def fev(v, p=p, fo=fo, mm_ms=mm_ms):
                v.wait_ge(kb.pe.h, mm_ms)
                v.tensor_scalar_add(kb.SCR[:, fo * 256:(fo + 1) * 256],
                                    kb.PS[p][:, 0:256],
                                    bias[:, fo:fo + 1]).then_inc(kb.dve.h, 1)
            kb.on("vector", fev)
            ev_last = kb.dve.inc()
            kb.ps_free[p] = (kb.dve, ev_last)
        res_ms = emit_ln_post_res(
            kb, 256,
            lambda c, g=g: kb.X32[:, c * 1024 + g * 256: c * 1024 + (g + 1) * 256],
            (kb.dve, ev_last))
    return res_ms


def emit_ffn_l0(kb, dr, xn_ms):
    gu, dw = dr["l0_gu"], dr["l0_dw"]
    res_ms = None
    for e in range(4):
        int_ms = {}
        for f in range(NCH):
            bufg, wvg = kb.wload(lambda e=e, f=f: gu[e, :, f * 128:(f + 1) * 128]
                                 .rearrange("(c p) d -> p c d", p=128), nsub=16)
            pg, freeg = kb.next_ps()

            def fmmg(t, buf=bufg, wv=wvg, p=pg, free=freeg, e=e,
                     xm=xn_ms if (e == 0 and f == 0) else None):
                if xm:
                    t.wait_ge(kb.dve.h, xm)
                t.wait_ge(wv[0].h, wv[1])
                if free is not None:
                    t.wait_ge(free[0].h, free[1])
                for c in range(NCH):
                    mm = t.matmul(kb.PS[p][:, 0:256], kb.W[buf][:, c * 128:(c + 1) * 128],
                                  kb.A[:, c * 1024 + e: c * 1024 + 1024: 4],
                                  start=(c == 0), stop=(c == NCH - 1))
                    if c == NCH - 1:
                        mm.then_inc(kb.pe.h, 1)
            kb.on("tensor", fmmg)
            msg = kb.pe.inc()
            kb.wconsumed(msg)

            bufu, wvu = kb.wload(lambda e=e, f=f: gu[e, :, 2048 + f * 128: 2048 + (f + 1) * 128]
                                 .rearrange("(c p) d -> p c d", p=128), nsub=16)
            pu, freeu = kb.next_ps()

            def fmmu(t, buf=bufu, wv=wvu, p=pu, free=freeu, e=e):
                t.wait_ge(wv[0].h, wv[1])
                if free is not None:
                    t.wait_ge(free[0].h, free[1])
                for c in range(NCH):
                    mm = t.matmul(kb.PS[p][:, 0:256], kb.W[buf][:, c * 128:(c + 1) * 128],
                                  kb.A[:, c * 1024 + e: c * 1024 + 1024: 4],
                                  start=(c == 0), stop=(c == NCH - 1))
                    if c == NCH - 1:
                        mm.then_inc(kb.pe.h, 1)
            kb.on("tensor", fmmu)
            msu = kb.pe.inc()
            kb.wconsumed(msu)

            sb_ = f % 2

            def fsil(sc, p=pg, sb_=sb_, msg=msg, w=int_ms.get(f - 2)):
                sc.wait_ge(kb.pe.h, msg)
                if w:
                    sc.wait_ge(kb.dve.h, w)
                sc.activation(kb.STG[:, sb_ * 256:(sb_ + 1) * 256], kb.PS[p][:, 0:256],
                              AF.Silu).then_inc(kb.act.h, 1)
            kb.on("scalar", fsil)
            sil_ms = kb.act.inc()
            kb.ps_free[pg] = (kb.act, sil_ms)

            def fint(v, p=pu, sb_=sb_, f=f, msu=msu, sil_ms=sil_ms):
                v.wait_ge(kb.pe.h, msu)
                v.wait_ge(kb.act.h, sil_ms)
                v.tensor_mul(kb.B[:, f * 256:(f + 1) * 256],
                             kb.STG[:, sb_ * 256:(sb_ + 1) * 256],
                             kb.PS[p][:, 0:256]).then_inc(kb.dve.h, 1)
            kb.on("vector", fint)
            mi = kb.dve.inc()
            int_ms[f] = mi
            kb.ps_free[pu] = (kb.dve, mi)
        last_int = kb.dve.n

        ev_last = None
        for hh in range(NCH):
            buf, wv = kb.wload(lambda e=e, hh=hh: dw[e, :, hh * 128:(hh + 1) * 128]
                               .rearrange("(c p) d -> p c d", p=128), nsub=16)
            p, free = kb.next_ps()

            def fmmd(t, buf=buf, wv=wv, p=p, free=free, last_int=last_int):
                t.wait_ge(kb.dve.h, last_int)
                t.wait_ge(wv[0].h, wv[1])
                if free is not None:
                    t.wait_ge(free[0].h, free[1])
                for i in range(NCH):
                    mm = t.matmul(kb.PS[p][:, 0:256], kb.W[buf][:, i * 128:(i + 1) * 128],
                                  kb.B[:, i * 256:(i + 1) * 256],
                                  start=(i == 0), stop=(i == NCH - 1))
                    if i == NCH - 1:
                        mm.then_inc(kb.pe.h, 1)
            kb.on("tensor", fmmd)
            md = kb.pe.inc()
            kb.wconsumed(md)

            def fev(v, p=p, hh=hh, md=md):
                v.wait_ge(kb.pe.h, md)
                v.tensor_copy(kb.SCR[:, hh * 256:(hh + 1) * 256],
                              kb.PS[p][:, 0:256]).then_inc(kb.dve.h, 1)
            kb.on("vector", fev)
            ev_last = kb.dve.inc()
            kb.ps_free[p] = (kb.dve, ev_last)

        res_ms = emit_ln_post_res(
            kb, 256,
            lambda c, e=e: kb.X32[:, c * 1024 + e: c * 1024 + 1024: 4],
            (kb.dve, ev_last))
    return res_ms


def emit_ffn_l1(kb, dr, xn_ms):
    wg, wu, wd = dr["l1_wg"], dr["l1_wu"], dr["l1_wd"]
    res_ms = None
    for g in range(4):
        for ihalf in range(2):
            int_ms = {}
            for f in range(32):
                gi = ihalf * 32 + f
                bufg, wvg = kb.wload(lambda gi=gi: wg[gi])
                pg, freeg = kb.next_ps()

                def fmmg(t, buf=bufg, wv=wvg, p=pg, free=freeg, g=g,
                         xm=xn_ms if (g == 0 and ihalf == 0 and f == 0) else None):
                    if xm:
                        t.wait_ge(kb.dve.h, xm)
                    t.wait_ge(wv[0].h, wv[1])
                    if free is not None:
                        t.wait_ge(free[0].h, free[1])
                    for c in range(NCH):
                        mm = t.matmul(kb.PS[p][:, 0:256], kb.W[buf][:, c * 128:(c + 1) * 128],
                                      kb.A[:, c * 1024 + g * 256: c * 1024 + (g + 1) * 256],
                                      start=(c == 0), stop=(c == NCH - 1))
                        if c == NCH - 1:
                            mm.then_inc(kb.pe.h, 1)
                kb.on("tensor", fmmg)
                msg = kb.pe.inc()
                kb.wconsumed(msg)

                bufu, wvu = kb.wload(lambda gi=gi: wu[gi])
                pu, freeu = kb.next_ps()

                def fmmu(t, buf=bufu, wv=wvu, p=pu, free=freeu, g=g):
                    t.wait_ge(wv[0].h, wv[1])
                    if free is not None:
                        t.wait_ge(free[0].h, free[1])
                    for c in range(NCH):
                        mm = t.matmul(kb.PS[p][:, 0:256], kb.W[buf][:, c * 128:(c + 1) * 128],
                                      kb.A[:, c * 1024 + g * 256: c * 1024 + (g + 1) * 256],
                                      start=(c == 0), stop=(c == NCH - 1))
                        if c == NCH - 1:
                            mm.then_inc(kb.pe.h, 1)
                kb.on("tensor", fmmu)
                msu = kb.pe.inc()
                kb.wconsumed(msu)

                sb_ = f % 2

                def fsil(sc, p=pg, sb_=sb_, gi=gi, msg=msg, w=int_ms.get(f - 2)):
                    sc.wait_ge(kb.pe.h, msg)
                    if w:
                        sc.wait_ge(kb.dve.h, w)
                    sc.activation(kb.STG[:, sb_ * 256:(sb_ + 1) * 256], kb.PS[p][:, 0:256],
                                  AF.Silu, scale=kb.SGT[:, gi:gi + 1]).then_inc(kb.act.h, 1)
                kb.on("scalar", fsil)
                sil_ms = kb.act.inc()
                kb.ps_free[pg] = (kb.act, sil_ms)

                def fint(v, p=pu, sb_=sb_, gi=gi, f=f, msu=msu, sil_ms=sil_ms):
                    v.wait_ge(kb.pe.h, msu)
                    v.wait_ge(kb.act.h, sil_ms)
                    v.scalar_tensor_tensor(kb.B[:, f * 256:(f + 1) * 256], kb.PS[p][:, 0:256],
                                           kb.SUT[:, gi:gi + 1],
                                           kb.STG[:, sb_ * 256:(sb_ + 1) * 256],
                                           op0=ALU.mult, op1=ALU.mult).then_inc(kb.dve.h, 1)
                kb.on("vector", fint)
                mi = kb.dve.inc()
                int_ms[f] = mi
                kb.ps_free[pu] = (kb.dve, mi)
            last_int = kb.dve.n

            ev_last = None
            for hh in range(NCH):
                slabs = [kb.wload(lambda hh=hh, j=j, ihalf=ihalf: wd[hh, ihalf * 2 + j])
                         for j in range(2)]
                p, free = kb.next_ps()

                def fmmd(t, slabs=slabs, p=p, free=free, last_int=last_int):
                    t.wait_ge(kb.dve.h, last_int)
                    if free is not None:
                        t.wait_ge(free[0].h, free[1])
                    for j, (buf, wv) in enumerate(slabs):
                        t.wait_ge(wv[0].h, wv[1])
                        for cc in range(16):
                            f_ = j * 16 + cc
                            mm = t.matmul(kb.PS[p][:, 0:256],
                                          kb.W[buf][:, cc * 128:(cc + 1) * 128],
                                          kb.B[:, f_ * 256:(f_ + 1) * 256],
                                          start=(j == 0 and cc == 0),
                                          stop=(j == 1 and cc == 15))
                            if j == 1 and cc == 15:
                                mm.then_inc(kb.pe.h, 1)
                kb.on("tensor", fmmd)
                md = kb.pe.inc()
                kb.wconsumed(md)
                kb.wconsumed(md)

                def fev(v, p=p, hh=hh, ihalf=ihalf, md=md):
                    v.wait_ge(kb.pe.h, md)
                    dst = kb.SCR[:, hh * 256:(hh + 1) * 256]
                    if ihalf == 0:
                        v.tensor_copy(dst, kb.PS[p][:, 0:256]).then_inc(kb.dve.h, 1)
                    else:
                        v.tensor_add(dst, dst, kb.PS[p][:, 0:256])
                        v.tensor_scalar_mul(dst, dst, kb.SDT[:, hh:hh + 1]).then_inc(kb.dve.h, 1)
                kb.on("vector", fev)
                ev_last = kb.dve.inc()
                kb.ps_free[p] = (kb.dve, ev_last)

        res_ms = emit_ln_post_res(
            kb, 256,
            lambda c, g=g: kb.X32[:, c * 1024 + g * 256: c * 1024 + (g + 1) * 256],
            (kb.dve, ev_last))
    return res_ms


def emit_store(kb, y_d, res_ms):
    tgt = kb.out.inc(16)

    def f(sy, res_ms=res_ms, tgt=tgt):
        sy.wait_ge(kb.dve.h, res_ms)
        sy.dma_start(out=y_d[:, :], in_=kb.X32[:, :]).then_inc(kb.out.h, 16)
        sy.wait_ge(kb.out.h, tgt)
    kb.on("sync", f)


ALL_PHASES = ["l0attn", "l0ffn", "l1attn", "l1ffn"]


def build_program(phases, dbg=()):
    nc = bass.Bass("TRN2", target_bir_lowering=False, debug=False)
    stack = contextlib.ExitStack()
    dr = {}

    def din(name, shape, dtype):
        dr[name] = nc.dram_tensor(name, shape, dtype, kind="ExternalInput").ap()

    din("x", [128, 16 * 1024], F32)
    din("cos0", [128, 1024], BF16)
    din("sin0", [128, 1024], BF16)
    din("tri", [128, 128], BF16)
    din("r0t", [128, 128], BF16)
    din("onec32", [128, 1], F32)
    din("onecbf", [128, 1], BF16)
    din("oner32", [1, 128], F32)
    din("onerbf", [1, 128], BF16)
    din("biastab0", [128, 64], F32)
    din("biastab1", [128, 64], F32)
    din("cosv1", [128, 16], F32)
    din("sinv1", [128, 16], F32)
    din("consts", [1, 4], F32)
    if "l0attn" in phases:
        for n in ("l0_wq", "l0_wk", "l0_wv", "l0_wo"):
            din(n, [2048, 2048], F32)
        for n in ("l0_bq", "l0_bk", "l0_bo"):
            din(n, [128, 16], F32)
        din("l0_bv_flat", [1, 2048], F32)
    if "l0ffn" in phases:
        din("l0_gu", [4, 2048, 4096], F32)
        din("l0_dw", [4, 2048, 2048], F32)
    if "l1attn" in phases:
        din("l1_wqkvt", [2048, 6144], F32)
        din("l1_bqkv", [128, 48], F32)
        din("l1_woutt", [2048, 2048], F32)
        din("l1_bout", [128, 16], F32)
        din("l1_bv_flat", [1, 2048], F32)
        din("l1rot", [4, 128, 128], BF16)
    if "l1ffn" in phases:
        din("l1_wg", [64, 128, 2048], I8)
        din("l1_wu", [64, 128, 2048], I8)
        din("l1_wd", [16, 4, 128, 2048], I8)
        din("l1_sg", [128, 64], F32)
        din("l1_su", [128, 64], F32)
        din("l1_sd", [128, 16], F32)
    y_d = nc.dram_tensor("y", [128, 16 * 1024], F32, kind="ExternalOutput").ap()

    with nc.Block() as block:
        kb = Kb(nc, stack, block, dbg)
        kb.phases = phases
        kb.X32 = kb.sb("X32", [128, 16 * 1024], F32)
        kb.A = kb.sb("Apool", [128, 16 * 1024], BF16)
        kb.B = kb.sb("Bpool", [128, 16 * 1024], BF16)
        kb.W = [kb.sb(f"W{i}", [128, 2048], BF16) for i in range(4)]
        kb.SCR = kb.sb("SCR", [128, 4096], F32)
        kb.STG = kb.sb("STG", [128, 4096], BF16)
        kb.KS = [kb.sb(f"KS{i}", [128, 1024], BF16) for i in range(2)]
        kb.VS = [kb.sb(f"VS{i}", [128, 1024], BF16) for i in range(2)]
        kb.PARK2 = kb.sb("PARK2", [128, 2048], BF16)
        kb.PT = [kb.sb(f"PT{i}", [128, 1024], BF16) for i in range(2)]
        kb.APL = kb.sb("APL", [128, 1024], F32)
        kb.ROWS = kb.sb("ROWS", [1, 1536], F32)
        kb.ROWSBF = kb.sb("ROWSBF", [1, 512], BF16)
        kb.COS0 = kb.sb("COS0", [128, 1024], BF16)
        kb.SIN0 = kb.sb("SIN0", [128, 1024], BF16)
        kb.TRI = kb.sb("TRI", [128, 128], BF16)
        kb.R0T = kb.sb("R0T", [128, 128], BF16)
        kb.ONEC32 = kb.sb("ONEC32", [128, 1], F32)
        kb.ONECBF = kb.sb("ONECBF", [128, 1], BF16)
        kb.ONER32 = kb.sb("ONER32", [1, 128], F32)
        kb.ONERBF = kb.sb("ONERBF", [1, 128], BF16)
        kb.BT0 = kb.sb("BT0", [128, 64], F32)
        kb.BT1 = kb.sb("BT1", [128, 64], F32)
        kb.CV1 = kb.sb("CV1", [128, 16], F32)
        kb.SV1 = kb.sb("SV1", [128, 16], F32)
        kb.CONSTS = kb.sb("CONSTS", [1, 4], F32)
        if "l0attn" in phases:
            kb.BQ = kb.sb("BQ", [128, 16], F32)
            kb.BK = kb.sb("BK", [128, 16], F32)
            kb.BO = kb.sb("BO", [128, 16], F32)
        if "l1attn" in phases:
            kb.BQKV = kb.sb("BQKV", [128, 48], F32)
            kb.BOUT = kb.sb("BOUT", [128, 16], F32)
            kb.L1R = [kb.sb(f"L1R{i}", [128, 128], BF16) for i in range(4)]
        if "l1ffn" in phases:
            kb.SGT = kb.sb("SGT", [128, 64], F32)
            kb.SUT = kb.sb("SUT", [128, 64], F32)
            kb.SDT = kb.sb("SDT", [128, 16], F32)
        kb.PS = [kb.psum(f"PS{i}", [128, 512]) for i in range(4)]
        kb.PSW0 = kb.psum("PSW0", [128, 1024])
        kb.PSW1 = kb.psum("PSW1", [128, 1024])

        kb.kin, kb.kout, kb.vin, kb.vout = {}, {}, {}, {}
        for L, ph in ((0, "l0attn"), (1, "l1attn")):
            if ph in phases:
                kb.kin[L] = nc.dram_tensor(f"kin{L}", [2048, 1024], BF16).ap()
                kb.kout[L] = nc.dram_tensor(f"kout{L}", [16384, 1024], BF16).ap()
                kb.vin[L] = nc.dram_tensor(f"vin{L}", [1024, 2048], BF16).ap()
                kb.vout[L] = nc.dram_tensor(f"vout{L}", [8192, 2048], BF16).ap()

        kb.begin()
        emit_load(kb, dr)
        entry = [(kb.ldr, kb.loads_done)]
        res = 1
        for ph in phases:
            if ph in ("l0attn", "l1attn"):
                L = 0 if ph == "l0attn" else 1
                xn_ms = emit_ln_pre(kb, kb.X32, kb.A, entry)
                kb.dump(f"xn{L}", lambda: kb.A[:, :], [128, 16384], BF16,
                        [(kb.dve, xn_ms)])
                emit_v_proj(kb, dr, L, xn_ms)
                raw_ms = emit_kq_proj(kb, dr, L, "k")
                if L == 0:
                    emit_rope_l0(kb, raw_ms, None, True, kb.kin[0])
                    rawq_ms = emit_kq_proj(kb, dr, L, "q")
                    q_ms = emit_rope_l0(kb, rawq_ms, kb.A, False)
                    q_ready = q_ms[(NCH - 1, 1)]
                else:
                    park = [kb.KS[0][:, :], kb.KS[1][:, :],
                            kb.VS[0][:, :], kb.VS[1][:, :],
                            kb.PARK2[:, 0:1024], kb.PARK2[:, 1024:2048],
                            kb.PT[0][:, :], kb.PT[1][:, :]]
                    km, kloc = emit_rope_l1(kb, raw_ms, False, park)
                    emit_l2_scale(kb, km, kloc, True, kb.kin[1])
                    rawq_ms = emit_kq_proj(kb, dr, L, "q",
                                           b_free=[(kb.stg_s[0], kb.stg_s[0].n),
                                                   (kb.stg_s[1], kb.stg_s[1].n)])
                    qm, qloc = emit_rope_l1(kb, rawq_ms, True, None)
                    q_ready = emit_l2_scale(kb, qm, qloc, False, None)
                coll_v = emit_gather(kb, L)
                kb.dump(f"kin{L}", lambda L=L: kb.kin[L][:, :], [2048, 1024], BF16,
                        [(kb.stg, kb.stg.n)])
                kb.dump(f"vin{L}", lambda L=L: kb.vin[L][:, :], [1024, 2048], BF16,
                        [(kb.stg, kb.stg.n)])
                kb.dump(f"kout{L}", lambda L=L: kb.kout[L][:, :], [16384, 1024], BF16,
                        [(kb.coll, coll_v)])
                kb.dump(f"vout{L}", lambda L=L: kb.vout[L][:, :], [8192, 2048], BF16,
                        [(kb.coll, coll_v)])
                kb.dump(f"q{L}", lambda: kb.A[:, :], [128, 16384], BF16,
                        [(kb.dve, q_ready)])
                ao_ms = emit_attention(kb, L, coll_v)
                kb.dump(f"ao{L}", lambda: kb.B[:, :], [128, 16384], BF16,
                        [(kb.dve, ao_ms)])
                res = emit_outproj(kb, dr, L, ao_ms)
            else:
                xn_ms = emit_ln_pre(kb, kb.X32, kb.A, entry)
                if ph == "l0ffn":
                    res = emit_ffn_l0(kb, dr, xn_ms)
                else:
                    res = emit_ffn_l1(kb, dr, xn_ms)
            entry = [(kb.dve, res)]
        emit_store(kb, y_d, res)
        kb.end()
    stack.close()
    return nc, set(dr.keys()), kb.dbg_tensors


def stage_inputs(inputs):
    f32 = np.float32
    shared = {}
    x = np.asarray(inputs["L0_attn_x"][0], f32)

    def chunked_cols(a):
        return np.ascontiguousarray(np.asarray(a, f32).reshape(-1, 128).T)

    shared["l0_wq"] = np.asarray(inputs["L0_attn_Wq"], f32)
    shared["l0_wk"] = np.asarray(inputs["L0_attn_Wk"], f32)
    shared["l0_wv"] = np.asarray(inputs["L0_attn_Wv"], f32)
    shared["l0_wo"] = np.asarray(inputs["L0_attn_Wo"], f32)
    shared["l0_bq"] = chunked_cols(inputs["L0_attn_bq"])
    shared["l0_bk"] = chunked_cols(inputs["L0_attn_bk"])
    shared["l0_bo"] = chunked_cols(inputs["L0_attn_bo"])
    shared["l0_bv_flat"] = np.asarray(inputs["L0_attn_bv"], f32).reshape(1, 2048)
    shared["l0_gu"] = np.asarray(inputs["L0_ffn_gate_up_proj"], f32)
    shared["l0_dw"] = np.asarray(inputs["L0_ffn_down_proj"], f32)
    shared["l1_wqkvt"] = np.ascontiguousarray(np.asarray(inputs["L1_attn_Wqkv"], f32).T)
    bqkv = np.asarray(inputs["L1_attn_bqkv"], f32)
    shared["l1_bqkv"] = chunked_cols(bqkv)
    shared["l1_bv_flat"] = np.ascontiguousarray(bqkv[4096:6144].reshape(1, 2048))
    shared["l1_woutt"] = np.ascontiguousarray(np.asarray(inputs["L1_attn_Wout"], f32).T)
    shared["l1_bout"] = chunked_cols(inputs["L1_attn_bout"])
    wg = np.asarray(inputs["L1_ffn_w_gate"])
    wu = np.asarray(inputs["L1_ffn_w_up"])
    wd = np.asarray(inputs["L1_ffn_w_down"])
    shared["l1_wg"] = np.ascontiguousarray(
        wg.reshape(16, 128, 64, 128).transpose(2, 1, 0, 3).reshape(64, 128, 2048))
    shared["l1_wu"] = np.ascontiguousarray(
        wu.reshape(16, 128, 64, 128).transpose(2, 1, 0, 3).reshape(64, 128, 2048))
    shared["l1_wd"] = np.ascontiguousarray(
        wd.reshape(4, 16, 128, 16, 128).transpose(3, 0, 2, 1, 4).reshape(16, 4, 128, 2048))
    shared["l1_sg"] = chunked_cols(inputs["L1_ffn_s_gate"])
    shared["l1_su"] = chunked_cols(inputs["L1_ffn_s_up"])
    shared["l1_sd"] = chunked_cols(inputs["L1_ffn_s_down"])

    shared["tri"] = np.ascontiguousarray(np.triu(np.ones((128, 128), f32)).astype(BFD))
    r0t = np.zeros((128, 128), f32)
    for i in range(64):
        r0t[2 * i, 2 * i + 1] = 1.0
        r0t[2 * i + 1, 2 * i] = -1.0
    shared["r0t"] = r0t.astype(BFD)
    l1r = np.zeros((4, 128, 128), f32)
    for m in range(64):
        l1r[0, 2 * m + 1, m] = -1.0
        l1r[1, 2 * m + 1, m + 64] = -1.0
        l1r[2, 2 * m, m] = 1.0
        l1r[3, 2 * m, m + 64] = 1.0
    shared["l1rot"] = l1r.astype(BFD)
    shared["onec32"] = np.ones((128, 1), f32)
    shared["onecbf"] = np.ones((128, 1), BFD)
    shared["oner32"] = np.ones((1, 128), f32)
    shared["onerbf"] = np.ones((1, 128), BFD)
    inv = np.power(f32(10000.0), -(np.arange(0, E, 2, dtype=f32) / E))
    emb = np.repeat(inv, 2)
    shared["cosv1"] = chunked_cols(np.cos(f32(S) * emb))
    shared["sinv1"] = chunked_cols(np.sin(f32(S) * emb))
    shared["consts"] = np.array([[LN_EPS, L2_EPS, AQBIAS, 0.0]], f32)

    per_core = []
    for c in range(NCORES):
        d = dict(shared)
        tiles = RANK_TILES[c]
        rows = np.concatenate([x[g * 128:(g + 1) * 128] for g in tiles], 0)
        d["x"] = np.ascontiguousarray(
            rows.T.reshape(16, 128, 1024).transpose(1, 0, 2).reshape(128, 16 * 1024))
        pos = np.concatenate([np.arange(g * 128, (g + 1) * 128) for g in tiles]).astype(f32)
        exps = -np.arange(0, D, 2, dtype=f32) / D
        thetas = np.repeat(np.power(f32(10000.0), exps), 2)
        ang = thetas[:, None] * pos[None, :]
        d["cos0"] = np.ascontiguousarray(np.cos(ang).astype(BFD))
        d["sin0"] = np.ascontiguousarray(np.sin(ang).astype(BFD))
        for L, base in ((0, L0_SHIFT), (1, 0.0)):
            bt = np.full((128, 64), KILL, f32)
            for p in range(14):
                bt[:, p] = base if p < 2 * c else KILL
            for p in range(30):
                bt[:, 32 + p] = base if p < 30 - 2 * c else KILL
            bt[:, 62] = base
            d[f"biastab{L}"] = bt
        per_core.append(d)
    return per_core


LAUNCH_PLAN = [ALL_PHASES]


def kernel(**inputs):
    per_core = stage_inputs(inputs)
    for phases in LAUNCH_PLAN:
        nc, innames, _ = build_program(phases)
        in_maps = [{k: v for k, v in d.items() if k in innames} for d in per_core]
        res = run_bass_kernel_spmd(nc, in_maps, core_ids=list(range(NCORES)))
        for c in range(NCORES):
            per_core[c]["x"] = np.ascontiguousarray(res.results[c]["y"])
    out = np.zeros((S, E), np.float32)
    for c in range(NCORES):
        xc = per_core[c]["x"].reshape(128, 16, 1024).transpose(1, 0, 2).reshape(2048, 1024)
        for j, g in enumerate(RANK_TILES[c]):
            out[g * 128:(g + 1) * 128] = xc[:, j * 128:(j + 1) * 128].T
    return out[None]


# version 02
# speedup vs baseline: 1.0464x; 1.0464x over previous
import contextlib
import os
from collections import defaultdict

import numpy as np
import ml_dtypes
import concourse.bass as bass
import concourse.mybir as mybir
from concourse.bass_utils import run_bass_kernel_spmd

F32 = mybir.dt.float32
BF16 = mybir.dt.bfloat16
I8 = mybir.dt.int8
AF = mybir.ActivationFunctionType
ALU = mybir.AluOpType
BFD = ml_dtypes.bfloat16

ENGINES = ("sync", "gpsimd", "scalar", "vector", "tensor")

E = 2048
S = 8192
H = 16
D = 128
NCH = 16
NCORES = 8
LN_EPS = 1e-5
L2_EPS = 1e-24
L0_SHIFT = -8.0
KILL = -1e30
RSCALE = float(1.0 / np.sqrt(D))
AQBIAS = float(-0.25 * np.log(2048.0))


def core_tiles(c):
    return [4 * c + j for j in range(4)] + [60 - 4 * c + j for j in range(4)]


RANK_TILES = [core_tiles(c) for c in range(NCORES)]


class Sem:
    def __init__(self, nc, stack, name):
        self.h = stack.enter_context(nc.semaphore(name))
        self.n = 0

    def inc(self, k=1):
        self.n += k
        return self.n


class Kb:
    def __init__(self, nc, stack, block, dbg=()):
        self.nc = nc
        self.stack = stack
        self.block = block
        self.dbg = set(dbg)
        self.dbg_tensors = {}
        self.pe = Sem(nc, stack, "s_pe")
        self.act = Sem(nc, stack, "s_act")
        self.dve = Sem(nc, stack, "s_dve")
        self.wdma = Sem(nc, stack, "s_wdma")
        self.ldr = Sem(nc, stack, "s_ldr")
        self.stg = Sem(nc, stack, "s_stg")
        self.coll = Sem(nc, stack, "s_coll")
        self.out = Sem(nc, stack, "s_out")
        self.w_s = [Sem(nc, stack, f"s_w{i}") for i in range(4)]
        self.ks_s = [Sem(nc, stack, f"s_ks{i}") for i in range(2)]
        self.vs_s = [Sem(nc, stack, f"s_vs{i}") for i in range(2)]
        self.stg_s = [Sem(nc, stack, f"s_stg{i}") for i in range(2)]
        self.rb_s = Sem(nc, stack, "s_rb")
        self.wslab = 0
        self.wfree = []
        self.ops = None
        self.ps_free = [None] * 4
        self.ps_rot = 0
        self.psw0_free = None
        self.psw1_free = None

    def sb(self, name, shape, dtype):
        return self.stack.enter_context(self.nc.sbuf_tensor(name, shape, dtype))

    def psum(self, name, shape, dtype=F32):
        return self.stack.enter_context(self.nc.psum_tensor(name, shape, dtype))

    def begin(self):
        assert self.ops is None
        self.ops = defaultdict(list)

    def end(self):
        ops = self.ops
        self.ops = None
        for name in ENGINES:
            lst = ops.get(name)
            if lst:
                def run(eng, lst=lst):
                    for f in lst:
                        f(eng)
                getattr(self.block, name)(run)

    def on(self, engine, f):
        self.ops[engine].append(f)

    def wload(self, src_ap_fn, nsub=None):
        k = self.wslab
        self.wslab += 1
        buf = k % 4
        free_ms = self.wfree[k - 4] if k >= 4 else None
        sem = self.w_s[buf]
        wv = sem.inc(16)

        def f(g, buf=buf, free_ms=free_ms, src_ap_fn=src_ap_fn, nsub=nsub, sem=sem):
            if free_ms:
                g.wait_ge(self.pe.h, free_ms)
            dst = self.W[buf][:, :]
            if nsub:
                dst = dst.rearrange("p (c f) -> p c f", c=nsub)
            g.dma_start(out=dst, in_=src_ap_fn()).then_inc(sem.h, 16)
        self.on("gpsimd", f)
        return buf, (sem, wv)

    def wconsumed(self, pe_ms):
        self.wfree.append(pe_ms)

    def next_ps(self):
        i = self.ps_rot % 4
        self.ps_rot += 1
        return i, self.ps_free[i]

    def R(self, slot, width=512):
        return self.ROWS[0:1, slot * 512: slot * 512 + width]

    def dump(self, tag, ap_fn, shape, dtype, waits):
        if tag not in self.dbg:
            return
        t = self.nc.dram_tensor("dbg_" + tag, shape, dtype, kind="ExternalOutput").ap()
        self.dbg_tensors[tag] = t
        tgt = self.out.inc(16)

        def f(sy, t=t, ap_fn=ap_fn, waits=list(waits)):
            for s, v in waits:
                sy.wait_ge(s.h, v)
            sy.dma_start(out=t[:, :], in_=ap_fn()).then_inc(self.out.h, 16)
        self.on("sync", f)


def emit_load(kb, dr):
    names = [("X32", "x"), ("COS0", "cos0"), ("SIN0", "sin0"),
             ("TRI", "tri"), ("R0T", "r0t"),
             ("ONEC32", "onec32"), ("ONECBF", "onecbf"),
             ("ONER32", "oner32"), ("ONERBF", "onerbf"),
             ("ONES128F", "ones128f"), ("ONES128B", "ones128b"),
             ("BT0", "biastab0"), ("BT1", "biastab1"),
             ("CV1", "cosv1"), ("SV1", "sinv1"), ("CONSTS", "consts")]
    small = {"l0attn": [("BQ", "l0_bq"), ("BK", "l0_bk"), ("BO", "l0_bo")],
             "l1attn": [("BQKV", "l1_bqkv"), ("BOUT", "l1_bout")],
             "l1ffn": [("SGT", "l1_sg"), ("SUT", "l1_su"), ("SDT", "l1_sd")]}
    for ph in kb.phases:
        names += small.get(ph, [])
    n_dma = len(names) + (4 if "l1attn" in kb.phases else 0)

    def f(sy, names=list(names)):
        for attr, key in names:
            sy.dma_start(out=getattr(kb, attr)[:, :], in_=dr[key][:, :]).then_inc(kb.ldr.h, 16)
        if "l1attn" in kb.phases:
            for i in range(4):
                sy.dma_start(out=kb.L1R[i][:, :], in_=dr["l1rot"][i]).then_inc(kb.ldr.h, 16)
    kb.on("sync", f)
    kb.loads_done = kb.ldr.inc(16 * n_dma)


def emit_ln_pre(kb, src, dst, entry, pack4=False):
    apply_ms = None
    for half in range(2):
        sq_ms = {}
        sumsq_ms = {}
        for c in range(NCH):
            slot = 2048 + (c % 2) * 512

            def fsq(sc, c=c, slot=slot, half=half,
                    ent=list(entry) if (half == 0 and c == 0) else None,
                    w=sumsq_ms.get(c - 2)):
                if ent:
                    for s, v in ent:
                        sc.wait_ge(s.h, v)
                if w:
                    sc.wait_ge(kb.pe.h, w)
                sc.activation(kb.SCR[:, slot:slot + 512],
                              src[:, c * 1024 + half * 512: c * 1024 + half * 512 + 512],
                              AF.Square).then_inc(kb.act.h, 1)
            kb.on("scalar", fsq)
            sq_ms[c] = kb.act.inc()

            def fmm(t, c=c, slot=slot, half=half, sqv=sq_ms[c],
                    ent=list(entry) if (half == 0 and c == 0) else None,
                    pw=kb.psw1_free if c == 0 else None):
                if ent:
                    for s, v in ent:
                        t.wait_ge(s.h, v)
                    t.wait_ge(kb.ldr.h, kb.loads_done)
                if pw:
                    t.wait_ge(pw[0].h, pw[1])
                t.matmul(kb.PSW1[:, 0:512], kb.ONES128F[:, :],
                         src[:, c * 1024 + half * 512: c * 1024 + half * 512 + 512],
                         start=(c == 0), stop=(c == NCH - 1))
                t.wait_ge(kb.act.h, sqv)
                t.matmul(kb.PSW1[:, 512:1024], kb.ONES128F[:, :],
                         kb.SCR[:, slot:slot + 512],
                         start=(c == 0), stop=(c == NCH - 1)).then_inc(kb.pe.h, 1)
            kb.on("tensor", fmm)
            sumsq_ms[c] = kb.pe.inc()
        stats_ms = sumsq_ms[NCH - 1]

        def frows(v, stats_ms=stats_ms):
            v.wait_ge(kb.pe.h, stats_ms)
            v.tensor_scalar_mul(kb.APL[:, 0:512], kb.PSW1[:, 0:512], 1.0 / E)
            v.tensor_mul(kb.SCR[:, 512:1024], kb.APL[:, 0:512], kb.APL[:, 0:512])
            v.scalar_tensor_tensor(kb.APL[:, 512:1024], kb.PSW1[:, 512:1024], 1.0 / E,
                                   kb.SCR[:, 512:1024],
                                   op0=ALU.mult, op1=ALU.subtract).then_inc(kb.dve.h, 1)
        kb.on("vector", frows)
        rows1_ms = kb.dve.inc()
        kb.psw1_free = (kb.dve, rows1_ms)

        def facts(sc, rows1_ms=rows1_ms):
            sc.wait_ge(kb.dve.h, rows1_ms)
            sc.activation(kb.SCR[:, 0:512], kb.APL[:, 512:1024], AF.Ln,
                          bias=kb.CONSTS[:, 0:1])
            sc.activation(kb.APL[:, 512:1024], kb.SCR[:, 0:512], AF.Exp,
                          scale=-0.5).then_inc(kb.act.h, 1)
        kb.on("scalar", facts)
        rstd_ms = kb.act.inc()

        def fc2(v, rstd_ms=rstd_ms):
            v.wait_ge(kb.act.h, rstd_ms)
            v.scalar_tensor_tensor(kb.APL[:, 0:512], kb.APL[:, 0:512], -1.0,
                                   kb.APL[:, 512:1024],
                                   op0=ALU.mult, op1=ALU.mult).then_inc(kb.dve.h, 1)
        kb.on("vector", fc2)
        c2_ms = kb.dve.inc()

        for c in range(NCH):
            def fap(v, c=c, half=half):
                xs = src[:, c * 1024 + half * 512: c * 1024 + half * 512 + 512]
                v.tensor_mul(kb.SCR[:, 0:512], xs, kb.APL[:, 512:1024])
                if pack4:
                    out = dst[:, c * 1024:(c + 1) * 1024].rearrange(
                        "p (e g) -> p g e", e=4)[:, half * 128:(half + 1) * 128, :]
                    i0 = kb.SCR[:, 0:512].rearrange("p (g e) -> p g e", e=4)
                    i1 = kb.APL[:, 0:512].rearrange("p (g e) -> p g e", e=4)
                    mm = v.tensor_add(out, i0, i1)
                else:
                    mm = v.tensor_add(dst[:, c * 1024 + half * 512: c * 1024 + half * 512 + 512],
                                      kb.SCR[:, 0:512], kb.APL[:, 0:512])
                if c == NCH - 1:
                    mm.then_inc(kb.dve.h, 1)
            kb.on("vector", fap)
        apply_ms = kb.dve.inc()
    return apply_ms


def emit_ln_post_res(kb, ntok, dst_ap_fn, pe_ready):
    mm_ms = {}
    for c in range(NCH):
        slot = 512 + (c % 2) * 256

        def fsq(sc, c=c, slot=slot, w=mm_ms.get(c - 2),
                pr=pe_ready if c == 0 else None):
            if pr:
                sc.wait_ge(pr[0].h, pr[1])
            if w:
                sc.wait_ge(kb.pe.h, w)
            sc.activation(kb.APL[:, slot:slot + 256],
                          kb.SCR[:, c * ntok: c * ntok + 256],
                          AF.Square).then_inc(kb.act.h, 1)
        kb.on("scalar", fsq)
        sqv = kb.act.inc()

        def fmm(t, c=c, slot=slot, sqv=sqv, pr=pe_ready if c == 0 else None,
                pw=kb.psw1_free if c == 0 else None):
            if pr:
                t.wait_ge(pr[0].h, pr[1])
            if pw:
                t.wait_ge(pw[0].h, pw[1])
            t.matmul(kb.PSW1[:, 0:256], kb.ONES128F[:, :],
                     kb.SCR[:, c * ntok: c * ntok + 256],
                     start=(c == 0), stop=(c == NCH - 1))
            t.wait_ge(kb.act.h, sqv)
            t.matmul(kb.PSW1[:, 512:768], kb.ONES128F[:, :],
                     kb.APL[:, slot:slot + 256],
                     start=(c == 0), stop=(c == NCH - 1)).then_inc(kb.pe.h, 1)
        kb.on("tensor", fmm)
        mm_ms[c] = kb.pe.inc()
    stats_ms = mm_ms[NCH - 1]

    def frows(v, stats_ms=stats_ms):
        v.wait_ge(kb.pe.h, stats_ms)
        v.tensor_scalar_mul(kb.APL[:, 0:256], kb.PSW1[:, 0:256], 1.0 / E)
        v.tensor_mul(kb.APL[:, 256:512], kb.APL[:, 0:256], kb.APL[:, 0:256])
        v.scalar_tensor_tensor(kb.APL[:, 256:512], kb.PSW1[:, 512:768], 1.0 / E,
                               kb.APL[:, 256:512],
                               op0=ALU.mult, op1=ALU.subtract).then_inc(kb.dve.h, 1)
    kb.on("vector", frows)
    rows1_ms = kb.dve.inc()
    kb.psw1_free = (kb.dve, rows1_ms)

    def facts(sc, rows1_ms=rows1_ms):
        sc.wait_ge(kb.dve.h, rows1_ms)
        sc.activation(kb.APL[:, 512:768], kb.APL[:, 256:512], AF.Ln,
                      bias=kb.CONSTS[:, 0:1])
        sc.activation(kb.APL[:, 256:512], kb.APL[:, 512:768], AF.Exp,
                      scale=-0.5).then_inc(kb.act.h, 1)
    kb.on("scalar", facts)
    rstd_ms = kb.act.inc()

    def fc2(v, rstd_ms=rstd_ms):
        v.wait_ge(kb.act.h, rstd_ms)
        v.scalar_tensor_tensor(kb.APL[:, 0:256], kb.APL[:, 0:256], -1.0,
                               kb.APL[:, 256:512],
                               op0=ALU.mult, op1=ALU.mult).then_inc(kb.dve.h, 1)
    kb.on("vector", fc2)
    c2_ms = kb.dve.inc()

    for c in range(NCH):
        def fap(v, c=c):
            s_ = kb.SCR[:, c * ntok: c * ntok + 256]
            v.tensor_mul(kb.APL[:, 512:768], s_, kb.APL[:, 256:512])
            v.tensor_add(kb.APL[:, 768:1024], kb.APL[:, 512:768], kb.APL[:, 0:256])
            dst = dst_ap_fn(c)
            mm = v.tensor_add(dst, dst, kb.APL[:, 768:1024])
            if c == NCH - 1:
                mm.then_inc(kb.dve.h, 1)
        kb.on("vector", fap)
    apply_ms = kb.dve.inc()
    return apply_ms


def emit_v_proj(kb, dr, L, xn_ms):
    wsrc = dr["l0_wv"] if L == 0 else dr["l1_wqkvt"]
    col0 = 0 if L == 0 else 4096
    bvflat = dr["l0_bv_flat"] if L == 0 else dr["l1_bv_flat"]
    vin = kb.vin[L]
    evac_ms_prev = None
    prev_mm = None
    for idx in range(8):
        half_t, foB = idx // 4, idx % 4
        rb = 0

        def fbias(g, rb=rb, foB=foB, pm=prev_mm):
            if pm:
                g.wait_ge(kb.pe.h, pm)
            g.dma_start(out=kb.ROWSBF[0:1, rb:rb + 512],
                        in_=bvflat[0:1, foB * 512:(foB + 1) * 512]).then_inc(kb.rb_s.h, 16)
        kb.on("gpsimd", fbias)
        bias_v = kb.rb_s.inc(16)

        slabs = [kb.wload(
            lambda cq=cq, foB=foB: wsrc[cq * 512:(cq + 1) * 512,
                                        col0 + foB * 512: col0 + (foB + 1) * 512]
            .rearrange("(s p) f -> p s f", p=128), nsub=4) for cq in range(4)]

        def fmm(t, slabs=slabs, half_t=half_t, foB=foB, bias_v=bias_v, rb=rb,
                xm=xn_ms if idx == 0 else None, ev_prev=evac_ms_prev):
            if xm:
                t.wait_ge(kb.dve.h, xm)
            if ev_prev:
                t.wait_ge(kb.dve.h, ev_prev)
            for cq, (buf, wv) in enumerate(slabs):
                t.wait_ge(wv[0].h, wv[1])
                for c4 in range(4):
                    for tt in range(4):
                        t.matmul(kb.PS[tt][:, :],
                                 kb.A[:, (4 * cq + c4) * 1024 + (half_t * 4 + tt) * 128:
                                      (4 * cq + c4) * 1024 + (half_t * 4 + tt) * 128 + 128],
                                 kb.W[buf][:, c4 * 512:(c4 + 1) * 512],
                                 start=(cq == 0 and c4 == 0), stop=False)
            t.wait_ge(kb.rb_s.h, bias_v)
            for tt in range(4):
                mm = t.matmul(kb.PS[tt][:, :], kb.ONERBF[0:1, :],
                              kb.ROWSBF[0:1, rb:rb + 512], start=False, stop=True)
                if tt == 3:
                    mm.then_inc(kb.pe.h, 1)
        kb.on("tensor", fmm)
        mm_ms = kb.pe.inc()
        for _ in range(4):
            kb.wconsumed(mm_ms)

        sp = idx % 2
        st_w = kb.stg_s[sp].n if idx >= 2 else 0

        def fev(v, mm_ms=mm_ms, sp=sp, st_w=st_w):
            v.wait_ge(kb.pe.h, mm_ms)
            if st_w:
                v.wait_ge(kb.stg_s[sp].h, st_w)
            for tt in range(4):
                mm = v.tensor_copy(kb.STG[:, sp * 2048 + tt * 512: sp * 2048 + (tt + 1) * 512],
                                   kb.PS[tt][:, :])
                if tt == 3:
                    mm.then_inc(kb.dve.h, 1)
        kb.on("vector", fev)
        evac_ms = kb.dve.inc()
        evac_ms_prev = evac_ms
        prev_mm = mm_ms

        def fst(sc, evac_ms=evac_ms, half_t=half_t, foB=foB, sp=sp):
            sc.wait_ge(kb.dve.h, evac_ms)
            for tt in range(4):
                sc.dma_start(
                    out=vin[half_t * 512 + tt * 128: half_t * 512 + (tt + 1) * 128,
                            foB * 512:(foB + 1) * 512],
                    in_=kb.STG[:, sp * 2048 + tt * 512: sp * 2048 + (tt + 1) * 512]
                ).then_inc(kb.stg_s[sp].h, 16)
        kb.on("scalar", fst)
        kb.stg_s[sp].inc(64)
    for i in range(4):
        kb.ps_free[i] = (kb.dve, evac_ms_prev)


def emit_kq_proj(kb, dr, L, which, b_free=None):
    if L == 0:
        wsrc = dr["l0_wq"] if which == "q" else dr["l0_wk"]
        col0 = 0
        bias_t, bias_off = ("BQ" if which == "q" else "BK"), 0
    else:
        wsrc = dr["l1_wqkvt"]
        col0 = 0 if which == "q" else 2048
        bias_t, bias_off = "BQKV", (0 if which == "q" else 16)
    bias = getattr(kb, bias_t)
    ev_ms = {}
    for fo in range(NCH):
        buf, wv = kb.wload(lambda fo=fo: wsrc[:, col0 + fo * 128: col0 + (fo + 1) * 128]
                           .rearrange("(c p) f -> p c f", p=128), nsub=16)
        for half in range(2):
            p, free = kb.next_ps()

            def fmm(t, buf=buf, wv=wv, p=p, free=free, fo=fo, half=half):
                t.wait_ge(wv[0].h, wv[1])
                if free is not None:
                    t.wait_ge(free[0].h, free[1])
                for c in range(NCH):
                    mm = t.matmul(kb.PS[p][:, :], kb.W[buf][:, c * 128:(c + 1) * 128],
                                  kb.A[:, c * 1024 + half * 512: c * 1024 + half * 512 + 512],
                                  start=(c == 0), stop=(c == NCH - 1))
                    if c == NCH - 1:
                        mm.then_inc(kb.pe.h, 1)
            kb.on("tensor", fmm)
            mm_ms = kb.pe.inc()
            if half == 1:
                kb.wconsumed(mm_ms)

            def fev(v, p=p, fo=fo, half=half, mm_ms=mm_ms,
                    bf=b_free if (fo == 0 and half == 0) else None):
                v.wait_ge(kb.pe.h, mm_ms)
                if bf:
                    for s, vv in bf:
                        v.wait_ge(s.h, vv)
                v.tensor_scalar_add(
                    kb.B[:, fo * 1024 + half * 512: fo * 1024 + half * 512 + 512],
                    kb.PS[p][:, :],
                    bias[:, bias_off + fo: bias_off + fo + 1]).then_inc(kb.dve.h, 1)
            kb.on("vector", fev)
            ev = kb.dve.inc()
            ev_ms[(fo, half)] = ev
            kb.ps_free[p] = (kb.dve, ev)
    return ev_ms


def emit_rope_l0(kb, raw_ms, dst, stage, kdma=None):
    out_ms = {}
    st_prev = {}
    for fo in range(NCH):
        sp = fo % 2
        for half in range(2):
            p, free = kb.next_ps()

            def frot(t, p=p, free=free, fo=fo, half=half, rms=raw_ms[(fo, half)]):
                t.wait_ge(kb.dve.h, rms)
                if free is not None:
                    t.wait_ge(free[0].h, free[1])
                t.matmul(kb.PS[p][:, :], kb.R0T[:, :],
                         kb.B[:, fo * 1024 + half * 512: fo * 1024 + half * 512 + 512],
                         start=True, stop=True).then_inc(kb.pe.h, 1)
            kb.on("tensor", frot)
            rot_ms = kb.pe.inc()

            def fcomb(v, p=p, fo=fo, half=half, rot_ms=rot_ms, sp=sp,
                      stw=st_prev.get(fo - 2)):
                v.wait_ge(kb.pe.h, rot_ms)
                if stw:
                    v.wait_ge(kb.stg_s[sp].h, stw)
                hs = half * 512
                v.tensor_mul(kb.APL[:, 0:512], kb.PS[p][:, :], kb.SIN0[:, hs:hs + 512])
                v.tensor_mul(kb.APL[:, 512:1024],
                             kb.B[:, fo * 1024 + hs: fo * 1024 + hs + 512],
                             kb.COS0[:, hs:hs + 512])
                if stage:
                    tgt = kb.STG[:, sp * 2048 + hs: sp * 2048 + hs + 512]
                else:
                    tgt = dst[:, fo * 1024 + hs: fo * 1024 + hs + 512]
                v.tensor_add(tgt, kb.APL[:, 0:512], kb.APL[:, 512:1024]).then_inc(kb.dve.h, 1)
            kb.on("vector", fcomb)
            cm = kb.dve.inc()
            out_ms[(fo, half)] = cm
            kb.ps_free[p] = (kb.dve, cm)
        if stage:
            def fst(sc, fo=fo, sp=sp, cm=out_ms[(fo, 1)]):
                sc.wait_ge(kb.dve.h, cm)
                sc.dma_start(out=kdma[fo * 128:(fo + 1) * 128, :],
                             in_=kb.STG[:, sp * 2048: sp * 2048 + 1024]).then_inc(kb.stg_s[sp].h, 16)
            kb.on("scalar", fst)
            st_prev[fo] = kb.stg_s[sp].inc(16)
    return out_ms


def l1_sources(fo):
    if fo < 8:
        return 2 * fo, 2 * fo + 1, 0, 1
    return 2 * (fo - 8), 2 * (fo - 8) + 1, 2, 3


def emit_rope_l1(kb, raw_ms, to_a, park_aps):
    out_ms = {}
    loc = {}
    for fo in range(NCH):
        s0, s1, p0, p1 = l1_sources(fo)
        p, free = kb.next_ps()
        qb = fo % 2

        def frot(t, p=p, free=free, s0=s0, s1=s1, p0=p0, p1=p1,
                 w=max(raw_ms[(s0, 1)], raw_ms[(s1, 1)])):
            t.wait_ge(kb.dve.h, w)
            if free is not None:
                t.wait_ge(free[0].h, free[1])
            t.matmul(kb.PS[p][:, :], kb.L1R[p0][:, :],
                     kb.B[:, s0 * 1024: s0 * 1024 + 512], start=True, stop=False)
            t.matmul(kb.PS[p][:, :], kb.L1R[p1][:, :],
                     kb.B[:, s1 * 1024: s1 * 1024 + 512], start=False, stop=True).then_inc(kb.pe.h, 1)
        kb.on("tensor", frot)
        rot1 = kb.pe.inc()
        p2, free2 = kb.next_ps()

        def frot2(t, p=p2, free=free2, s0=s0, s1=s1, p0=p0, p1=p1):
            if free is not None:
                t.wait_ge(free[0].h, free[1])
            t.matmul(kb.PS[p][:, :], kb.L1R[p0][:, :],
                     kb.B[:, s0 * 1024 + 512: (s0 + 1) * 1024], start=True, stop=False)
            t.matmul(kb.PS[p][:, :], kb.L1R[p1][:, :],
                     kb.B[:, s1 * 1024 + 512: (s1 + 1) * 1024],
                     start=False, stop=True).then_inc(kb.pe.h, 1)
        kb.on("tensor", frot2)
        rot2 = kb.pe.inc()

        def fqcos(sc, fo=fo, qb=qb, rm=raw_ms[(fo, 1)]):
            sc.wait_ge(kb.dve.h, rm)
            sc.activation(kb.STG[:, qb * 1024:(qb + 1) * 1024],
                          kb.B[:, fo * 1024:(fo + 1) * 1024],
                          AF.Copy, scale=kb.CV1[:, fo:fo + 1]).then_inc(kb.act.h, 1)
        kb.on("scalar", fqcos)
        qc_ms = kb.act.inc()

        if to_a:
            tgt = kb.A[:, fo * 1024:(fo + 1) * 1024]
        elif fo < 8:
            tgt = park_aps[fo]
        else:
            tgt = kb.B[:, 2 * (fo - 8) * 1024: (2 * (fo - 8) + 1) * 1024]
        loc[fo] = tgt

        def fcomb(v, p=p, p2=p2, fo=fo, qb=qb, rot2=rot2, qc_ms=qc_ms, tgt=tgt):
            v.wait_ge(kb.pe.h, rot2)
            v.wait_ge(kb.act.h, qc_ms)
            v.scalar_tensor_tensor(tgt[:, 0:512], kb.PS[p][:, :], kb.SV1[:, fo:fo + 1],
                                   kb.STG[:, qb * 1024: qb * 1024 + 512],
                                   op0=ALU.mult, op1=ALU.add)
            v.scalar_tensor_tensor(tgt[:, 512:1024], kb.PS[p2][:, :], kb.SV1[:, fo:fo + 1],
                                   kb.STG[:, qb * 1024 + 512: (qb + 1) * 1024],
                                   op0=ALU.mult, op1=ALU.add).then_inc(kb.dve.h, 1)
        kb.on("vector", fcomb)
        cm = kb.dve.inc()
        out_ms[fo] = cm
        kb.ps_free[p] = (kb.dve, cm)
        kb.ps_free[p2] = (kb.dve, cm)
    return out_ms, loc


def emit_l2_scale(kb, out_ms, loc, stage_dma, kdma):
    sq_pe = {}
    for fo in range(NCH):
        qb = 2 + (fo % 2)

        def fsq(sc, fo=fo, qb=qb, cm=out_ms[fo], w=sq_pe.get(fo - 2)):
            sc.wait_ge(kb.dve.h, cm)
            if w:
                sc.wait_ge(kb.pe.h, w)
            sc.activation(kb.STG[:, qb * 1024:(qb + 1) * 1024], loc[fo],
                          AF.Square).then_inc(kb.act.h, 1)
        kb.on("scalar", fsq)
        sqv = kb.act.inc()

        def fmm(t, fo=fo, qb=qb, sqv=sqv, pw=kb.psw1_free if fo == 0 else None):
            if pw:
                t.wait_ge(pw[0].h, pw[1])
            t.wait_ge(kb.act.h, sqv)
            t.matmul(kb.PSW1[:, 0:512], kb.ONES128B[:, :],
                     kb.STG[:, qb * 1024: qb * 1024 + 512],
                     start=(fo == 0), stop=(fo == NCH - 1))
            t.matmul(kb.PSW1[:, 512:1024], kb.ONES128B[:, :],
                     kb.STG[:, qb * 1024 + 512: (qb + 1) * 1024],
                     start=(fo == 0), stop=(fo == NCH - 1)).then_inc(kb.pe.h, 1)
        kb.on("tensor", fmm)
        sq_pe[fo] = kb.pe.inc()
    ss_ms = sq_pe[NCH - 1]

    def facts(sc, ss_ms=ss_ms):
        sc.wait_ge(kb.pe.h, ss_ms)
        sc.activation(kb.SCR[:, 0:512], kb.PSW1[:, 0:512], AF.Ln,
                      bias=kb.CONSTS[:, 1:2])
        sc.activation(kb.SCR[:, 1024:1536], kb.SCR[:, 0:512], AF.Exp,
                      scale=-0.5, bias=kb.CONSTS[:, 2:3])
        sc.activation(kb.SCR[:, 512:1024], kb.PSW1[:, 512:1024], AF.Ln,
                      bias=kb.CONSTS[:, 1:2])
        sc.activation(kb.SCR[:, 1536:2048], kb.SCR[:, 512:1024], AF.Exp,
                      scale=-0.5, bias=kb.CONSTS[:, 2:3]).then_inc(kb.act.h, 1)
    kb.on("scalar", facts)
    a_ms = kb.act.inc()
    kb.psw1_free = (kb.act, a_ms)

    last = None
    for fo in range(NCH):
        def fsc(v, fo=fo, a_ms=a_ms):
            if fo == 0:
                v.wait_ge(kb.act.h, a_ms)
            t = loc[fo]
            v.tensor_mul(t[:, 0:512], t[:, 0:512], kb.SCR[:, 1024:1536])
            v.tensor_mul(t[:, 512:1024], t[:, 512:1024],
                         kb.SCR[:, 1536:2048]).then_inc(kb.dve.h, 1)
        kb.on("vector", fsc)
        last = kb.dve.inc()
        if stage_dma:
            def fst(sc, fo=fo, last=last):
                sc.wait_ge(kb.dve.h, last)
                sc.dma_start(out=kdma[fo * 128:(fo + 1) * 128, :],
                             in_=loc[fo]).then_inc(kb.stg_s[fo % 2].h, 16)
            kb.on("scalar", fst)
            kb.stg_s[fo % 2].inc(16)
    return last


def emit_gather(kb, L):
    stg_v = (kb.stg_s[0].n, kb.stg_s[1].n)

    def f(g, stg_v=stg_v, L=L):
        if stg_v[0]:
            g.wait_ge(kb.stg_s[0].h, stg_v[0])
        if stg_v[1]:
            g.wait_ge(kb.stg_s[1].h, stg_v[1])
        g.collective_compute("AllGather", ALU.bypass,
                             replica_groups=[list(range(NCORES))],
                             ins=[kb.kin[L][:, :]], outs=[kb.kout[L][:, :]]).then_inc(kb.coll.h, 1)
        g.collective_compute("AllGather", ALU.bypass,
                             replica_groups=[list(range(NCORES))],
                             ins=[kb.vin[L][:, :]], outs=[kb.vout[L][:, :]]).then_inc(kb.coll.h, 1)
    kb.on("gpsimd", f)
    return kb.coll.inc(2)


def build_head_pairs():
    pairs = []
    for r in range(8):
        tiles = RANK_TILES[r]
        a_items = [(loc, 0, tiles[loc]) for loc in range(4) if tiles[loc] < 28]
        b_items = [(loc, 1, tiles[loc]) for loc in range(8) if tiles[loc] < 60]
        for i in range(0, len(a_items), 2):
            pairs.append({"kind": "past", "r": r, "items": a_items[i:i + 2]})
        for i in range(0, len(b_items), 2):
            pairs.append({"kind": "past", "r": r, "items": b_items[i:i + 2]})
    for rho in range(4):
        pairs.append({"kind": "diag", "rho": rho})
    return pairs


HEAD_PAIRS = build_head_pairs()


def select_pairs():
    sel = os.environ.get("K_PAIRSEL", "all")
    if sel == "all":
        return HEAD_PAIRS
    if sel == "diag":
        return [p for p in HEAD_PAIRS if p["kind"] == "diag"]
    if sel.startswith("first:"):
        return HEAD_PAIRS[:int(sel.split(":")[1])] +             [p for p in HEAD_PAIRS if p["kind"] == "diag"]
    if sel.startswith("rank:"):
        rs = set(int(x) for x in sel.split(":")[1].split("-"))
        return [p for p in HEAD_PAIRS if p["kind"] == "diag" or p["r"] in rs]
    if sel == "grpA":
        return [p for p in HEAD_PAIRS if p["kind"] == "diag"
                or p["items"][0][1] == 0]
    if sel == "grpB":
        return [p for p in HEAD_PAIRS if p["kind"] == "diag"
                or p["items"][0][1] == 1]
    raise ValueError(sel)


SEL_PAIRS = select_pairs()
NPAIRS = len(SEL_PAIRS)


def emit_attention(kb, L, entry_coll):
    scale = RSCALE if L == 0 else 1.0
    biasc = L0_SHIFT if L == 0 else 0.0
    BT = kb.BT0 if L == 0 else kb.BT1
    kout, vout = kb.kout[L], kb.vout[L]
    kin, vin = kb.kin[L], kb.vin[L]

    ks_last = {}
    vs_last = {}
    ks_lds = {}
    vs_lds = {}
    ks_owner = {}
    vs_owner = {}
    epi_dve = {0: kb.ps_free[0], 1: kb.ps_free[1]}
    epi_cp = {0: kb.ps_free[2], 1: kb.ps_free[3]}
    epi_prev_mul = {0: None, 1: None}

    def ensure_k(h, r):
        key = h * 9 + r
        if key in ks_lds:
            return
        kbuf = key % 2
        prev = ks_owner.get(kbuf)
        ks_owner[kbuf] = key

        def fk(sy, r=r, h=h, kbuf=kbuf, w=ks_last.get(prev), ec=entry_coll):
            sy.wait_ge(kb.coll.h, ec)
            if w:
                sy.wait_ge(kb.pe.h, w)
            if r < 8:
                src = kout[r * 2048 + h * 128: r * 2048 + (h + 1) * 128, :]
            else:
                src = kin[h * 128:(h + 1) * 128, :]
            sy.dma_start(out=kb.KS[kbuf][:, :], in_=src).then_inc(kb.ks_s[kbuf].h, 16)
        kb.on("sync", fk)
        ks_lds[key] = (kb.ks_s[kbuf], kb.ks_s[kbuf].inc(16))

    def ensure_v(h, r):
        key = h * 9 + r
        if key in vs_lds:
            return
        vbuf = key % 2
        prev = vs_owner.get(vbuf)
        vs_owner[vbuf] = key

        def fv(sy, r=r, h=h, vbuf=vbuf, w=vs_last.get(prev), ec=entry_coll):
            sy.wait_ge(kb.coll.h, ec)
            if w:
                sy.wait_ge(kb.pe.h, w)
            if r < 8:
                src = vout[r * 1024:(r + 1) * 1024,
                           h * 128:(h + 1) * 128].rearrange("(j p) d -> p j d", p=128)
            else:
                src = vin[:, h * 128:(h + 1) * 128].rearrange("(j p) d -> p j d", p=128)
            sy.dma_start(out=kb.VS[vbuf][:, :].rearrange("p (j d) -> p j d", j=8),
                         in_=src).then_inc(kb.vs_s[vbuf].h, 16)
        kb.on("sync", fv)
        vs_lds[key] = (kb.vs_s[vbuf], kb.vs_s[vbuf].inc(16))

    for h in range(H):
        hp = h // 2
        exp_ms = {}
        qk_ms = {}
        pv_ms = {}
        mask_ms = {}
        pv_started = {0: False, 1: False}

        def pair_rank(pi):
            return SEL_PAIRS[pi].get("r", 8)

        def pair_items(pi):
            pr = SEL_PAIRS[pi]
            if pr["kind"] == "past":
                return [(it[1], it[2], it[0], 0) for it in pr["items"]]
            rho = pr["rho"]
            return [(0, None, rho, rho * 128), (1, None, 4 + rho, rho * 128)]

        def emit_pv(pi):
            items = pair_items(pi)
            r = pair_rank(pi)
            sw = pi % 2
            vbuf = (h * 9 + r) % 2
            em = exp_ms[pi]
            dm = mask_ms.get(pi)
            starts = []
            for (grp, kt, loc, q0) in items:
                starts.append(not pv_started[grp])
                pv_started[grp] = True

            def fpv(t, items=items, sw=sw, r=r, vbuf=vbuf, h=h, em=em, dm=dm,
                    starts=tuple(starts), vld=vs_lds[h * 9 + r],
                    _unused=None,
                    epi_d=(epi_dve[0], epi_dve[1]),
                    epi_c=(epi_cp[0], epi_cp[1])):
                t.wait_ge(kb.act.h, em)
                if dm:
                    t.wait_ge(kb.dve.h, dm)
                t.wait_ge(vld[0].h, vld[1])
                last_mm = None
                for ix, (grp, kt, loc, q0) in enumerate(items):
                    st = starts[ix]
                    if st and epi_d[grp]:
                        t.wait_ge(epi_d[grp][0].h, epi_d[grp][1])
                    if st and epi_c[grp]:
                        t.wait_ge(epi_c[grp][0].h, epi_c[grp][1])
                    n = 512 - q0
                    vch = kb.VS[vbuf][:, loc * 128:(loc + 1) * 128]
                    pt = kb.PT[sw][:, ix * 512: ix * 512 + n]
                    t.matmul(kb.PS[grp][:, q0:512], vch, pt, start=st, stop=False)
                    last_mm = t.matmul(kb.PS[2 + grp][:, q0:512], kb.ONES128B[:, :], pt,
                                       start=st, stop=False)
                last_mm.then_inc(kb.pe.h, 1)
            kb.on("tensor", fpv)
            ms = kb.pe.inc()
            vs_last[h * 9 + r] = ms
            return ms

        for pi in range(NPAIRS):
            pr = SEL_PAIRS[pi]
            r = pair_rank(pi)
            ensure_k(h, r)
            ensure_v(h, r)
            sw = pi % 2
            psw = kb.PSW0 if sw == 0 else kb.PSW1
            items = pair_items(pi)
            if pr["kind"] == "past":
                bcol = (0 if items[0][0] == 0 else 32) + items[0][1] // 2
            else:
                bcol = None
            kbuf = (h * 9 + r) % 2

            ser = (os.environ.get("K_SERIAL") and pi == 0 and h > 0)

            def fqk(t, items=items, sw=sw, psw=psw, kbuf=kbuf, h=h,
                    kld=ks_lds[h * 9 + r], expw=exp_ms.get(pi - 2),
                    pw=(kb.psw0_free if sw == 0 else kb.psw1_free)
                    if (h == 0 and pi < 2) else None,
                    serw=(epi_cp[0], epi_cp[1]) if ser else None):
                t.wait_ge(kld[0].h, kld[1])
                if serw:
                    for s in serw:
                        if s:
                            t.wait_ge(s[0].h, s[1])
                if expw:
                    t.wait_ge(kb.act.h, expw)
                if pw:
                    t.wait_ge(pw[0].h, pw[1])
                for ix, (grp, kt, loc, q0) in enumerate(items):
                    n = 512 - q0
                    kch = kb.KS[kbuf][:, loc * 128:(loc + 1) * 128]
                    q = kb.A[:, h * 1024 + grp * 512 + q0: h * 1024 + (grp + 1) * 512]
                    mm = t.matmul(psw[:, ix * 512: ix * 512 + n], kch, q,
                                  start=True, stop=True)
                    if ix == len(items) - 1:
                        mm.then_inc(kb.pe.h, 1)
            kb.on("tensor", fqk)
            qk_ms[pi] = kb.pe.inc()
            ks_last[h * 9 + r] = qk_ms[pi]

            wid = 512 + (512 - items[-1][3])

            def fexp(sc, sw=sw, psw=psw, wid=wid, bcol=bcol,
                     qv=qk_ms[pi], pvw=pv_ms.get(pi - 2)):
                sc.wait_ge(kb.pe.h, qv)
                if pvw:
                    sc.wait_ge(kb.pe.h, pvw)
                bias = BT[:, bcol:bcol + 1] if bcol is not None else BT[:, 62:63]
                sc.activation(kb.PT[sw][:, 0:wid], psw[:, 0:wid], AF.Exp,
                              scale=scale, bias=bias).then_inc(kb.act.h, 1)
            kb.on("scalar", fexp)
            exp_ms[pi] = kb.act.inc()

            if pr["kind"] == "diag":
                def fmask(v, sw=sw, ev=exp_ms[pi]):
                    v.wait_ge(kb.act.h, ev)
                    v.tensor_mul(kb.PT[sw][:, 0:128], kb.PT[sw][:, 0:128], kb.TRI[:, :])
                    v.tensor_mul(kb.PT[sw][:, 512:640], kb.PT[sw][:, 512:640],
                                 kb.TRI[:, :]).then_inc(kb.dve.h, 1)
                kb.on("vector", fmask)
                mask_ms[pi] = kb.dve.inc()

            if os.environ.get("K_NOLOOK"):
                pv_ms[pi] = emit_pv(pi)
            elif pi >= 1:
                pv_ms[pi - 1] = emit_pv(pi - 1)
            if pi + 1 < NPAIRS:
                ensure_k(h, pair_rank(pi + 1))
                ensure_v(h, pair_rank(pi + 1))
        if not os.environ.get("K_NOLOOK"):
            pv_ms[NPAIRS - 1] = emit_pv(NPAIRS - 1)
        head_end = pv_ms[NPAIRS - 1]

        for grp in range(2):
            def fep_act(sc, grp=grp, he=head_end,
                        pw=epi_prev_mul[grp]):
                sc.wait_ge(kb.pe.h, he)
                if pw:
                    sc.wait_ge(pw[0].h, pw[1])
                sc.activation(kb.SCR[:, grp * 512:(grp + 1) * 512],
                              kb.PS[2 + grp][:, :], AF.Ln)
                sc.activation(kb.APL[:, grp * 512:(grp + 1) * 512],
                              kb.SCR[:, grp * 512:(grp + 1) * 512],
                              AF.Exp, scale=-1.0).then_inc(kb.act.h, 1)
            kb.on("scalar", fep_act)
            rec_ms = kb.act.inc()
            epi_cp[grp] = (kb.act, rec_ms)

            def fep_dve(v, grp=grp, h=h, rec_ms=rec_ms):
                v.wait_ge(kb.act.h, rec_ms)
                v.tensor_mul(kb.B[:, h * 1024 + grp * 512: h * 1024 + (grp + 1) * 512],
                             kb.PS[grp][:, :],
                             kb.APL[:, grp * 512:(grp + 1) * 512]).then_inc(kb.dve.h, 1)
            kb.on("vector", fep_dve)
            cp = kb.dve.inc()
            epi_dve[grp] = (kb.dve, cp)
            epi_prev_mul[grp] = (kb.dve, cp)
    for i in range(4):
        kb.ps_free[i] = epi_dve[i % 2] if i < 2 else epi_cp[i % 2]
    kb.psw0_free = (kb.act, exp_ms[NPAIRS - 2])
    kb.psw1_free = (kb.act, exp_ms[NPAIRS - 1])
    return kb.dve.n


def emit_outproj(kb, dr, L, ao_ms):
    wsrc = dr["l0_wo"] if L == 0 else dr["l1_woutt"]
    bias = kb.BO if L == 0 else kb.BOUT
    res_ms = None
    for g in range(4):
        ev_last = None
        for fo in range(NCH):
            buf, wv = kb.wload(lambda fo=fo: wsrc[:, fo * 128:(fo + 1) * 128]
                               .rearrange("(c p) f -> p c f", p=128), nsub=16)
            p, free = kb.next_ps()

            def fmm(t, buf=buf, wv=wv, p=p, free=free, g=g, fo=fo,
                    am=ao_ms if (g == 0 and fo == 0) else None):
                if am:
                    t.wait_ge(kb.dve.h, am)
                t.wait_ge(wv[0].h, wv[1])
                if free is not None:
                    t.wait_ge(free[0].h, free[1])
                for c in range(NCH):
                    mm = t.matmul(kb.PS[p][:, 0:256], kb.W[buf][:, c * 128:(c + 1) * 128],
                                  kb.B[:, c * 1024 + g * 256: c * 1024 + (g + 1) * 256],
                                  start=(c == 0), stop=(c == NCH - 1))
                    if c == NCH - 1:
                        mm.then_inc(kb.pe.h, 1)
            kb.on("tensor", fmm)
            mm_ms = kb.pe.inc()
            kb.wconsumed(mm_ms)

            def fev(v, p=p, fo=fo, mm_ms=mm_ms):
                v.wait_ge(kb.pe.h, mm_ms)
                v.tensor_scalar_add(kb.SCR[:, fo * 256:(fo + 1) * 256],
                                    kb.PS[p][:, 0:256],
                                    bias[:, fo:fo + 1]).then_inc(kb.dve.h, 1)
            kb.on("vector", fev)
            ev_last = kb.dve.inc()
            kb.ps_free[p] = (kb.dve, ev_last)
        res_ms = emit_ln_post_res(
            kb, 256,
            lambda c, g=g: kb.X32[:, c * 1024 + g * 256: c * 1024 + (g + 1) * 256],
            (kb.dve, ev_last))
    return res_ms


def emit_ffn_l0(kb, dr, xn_ms):
    gu, dw = dr["l0_gu"], dr["l0_dw"]
    res_ms = None
    for e in range(4):
        int_ms = {}
        for f in range(NCH):
            bufg, wvg = kb.wload(lambda e=e, f=f: gu[e, :, f * 128:(f + 1) * 128]
                                 .rearrange("(c p) d -> p c d", p=128), nsub=16)
            pg, freeg = kb.next_ps()

            def fmmg(t, buf=bufg, wv=wvg, p=pg, free=freeg, e=e,
                     xm=xn_ms if (e == 0 and f == 0) else None):
                if xm:
                    t.wait_ge(kb.dve.h, xm)
                t.wait_ge(wv[0].h, wv[1])
                if free is not None:
                    t.wait_ge(free[0].h, free[1])
                for c in range(NCH):
                    mm = t.matmul(kb.PS[p][:, 0:256], kb.W[buf][:, c * 128:(c + 1) * 128],
                                  kb.A[:, c * 1024 + e * 256: c * 1024 + (e + 1) * 256],
                                  start=(c == 0), stop=(c == NCH - 1))
                    if c == NCH - 1:
                        mm.then_inc(kb.pe.h, 1)
            kb.on("tensor", fmmg)
            msg = kb.pe.inc()
            kb.wconsumed(msg)

            bufu, wvu = kb.wload(lambda e=e, f=f: gu[e, :, 2048 + f * 128: 2048 + (f + 1) * 128]
                                 .rearrange("(c p) d -> p c d", p=128), nsub=16)
            pu, freeu = kb.next_ps()

            def fmmu(t, buf=bufu, wv=wvu, p=pu, free=freeu, e=e):
                t.wait_ge(wv[0].h, wv[1])
                if free is not None:
                    t.wait_ge(free[0].h, free[1])
                for c in range(NCH):
                    mm = t.matmul(kb.PS[p][:, 0:256], kb.W[buf][:, c * 128:(c + 1) * 128],
                                  kb.A[:, c * 1024 + e * 256: c * 1024 + (e + 1) * 256],
                                  start=(c == 0), stop=(c == NCH - 1))
                    if c == NCH - 1:
                        mm.then_inc(kb.pe.h, 1)
            kb.on("tensor", fmmu)
            msu = kb.pe.inc()
            kb.wconsumed(msu)

            sb_ = f % 2

            def fsil(sc, p=pg, sb_=sb_, msg=msg, w=int_ms.get(f - 2)):
                sc.wait_ge(kb.pe.h, msg)
                if w:
                    sc.wait_ge(kb.dve.h, w)
                sc.activation(kb.STG[:, sb_ * 256:(sb_ + 1) * 256], kb.PS[p][:, 0:256],
                              AF.Silu).then_inc(kb.act.h, 1)
            kb.on("scalar", fsil)
            sil_ms = kb.act.inc()
            kb.ps_free[pg] = (kb.act, sil_ms)

            def fint(v, p=pu, sb_=sb_, f=f, msu=msu, sil_ms=sil_ms):
                v.wait_ge(kb.pe.h, msu)
                v.wait_ge(kb.act.h, sil_ms)
                v.tensor_mul(kb.B[:, f * 256:(f + 1) * 256],
                             kb.STG[:, sb_ * 256:(sb_ + 1) * 256],
                             kb.PS[p][:, 0:256]).then_inc(kb.dve.h, 1)
            kb.on("vector", fint)
            mi = kb.dve.inc()
            int_ms[f] = mi
            kb.ps_free[pu] = (kb.dve, mi)
        last_int = kb.dve.n

        ev_last = None
        for hh in range(NCH):
            buf, wv = kb.wload(lambda e=e, hh=hh: dw[e, :, hh * 128:(hh + 1) * 128]
                               .rearrange("(c p) d -> p c d", p=128), nsub=16)
            p, free = kb.next_ps()

            def fmmd(t, buf=buf, wv=wv, p=p, free=free, last_int=last_int):
                t.wait_ge(kb.dve.h, last_int)
                t.wait_ge(wv[0].h, wv[1])
                if free is not None:
                    t.wait_ge(free[0].h, free[1])
                for i in range(NCH):
                    mm = t.matmul(kb.PS[p][:, 0:256], kb.W[buf][:, i * 128:(i + 1) * 128],
                                  kb.B[:, i * 256:(i + 1) * 256],
                                  start=(i == 0), stop=(i == NCH - 1))
                    if i == NCH - 1:
                        mm.then_inc(kb.pe.h, 1)
            kb.on("tensor", fmmd)
            md = kb.pe.inc()
            kb.wconsumed(md)

            def fev(v, p=p, hh=hh, md=md):
                v.wait_ge(kb.pe.h, md)
                v.tensor_copy(kb.SCR[:, hh * 256:(hh + 1) * 256],
                              kb.PS[p][:, 0:256]).then_inc(kb.dve.h, 1)
            kb.on("vector", fev)
            ev_last = kb.dve.inc()
            kb.ps_free[p] = (kb.dve, ev_last)

        res_ms = emit_ln_post_res(
            kb, 256,
            lambda c, e=e: kb.X32[:, c * 1024 + e: c * 1024 + 1024: 4],
            (kb.dve, ev_last))
    return res_ms


def emit_ffn_l1(kb, dr, xn_ms):
    wg, wu, wd = dr["l1_wg"], dr["l1_wu"], dr["l1_wd"]
    res_ms = None
    for g in range(4):
        for ihalf in range(2):
            int_ms = {}
            for f in range(32):
                gi = ihalf * 32 + f
                bufg, wvg = kb.wload(lambda gi=gi: wg[gi])
                pg, freeg = kb.next_ps()

                def fmmg(t, buf=bufg, wv=wvg, p=pg, free=freeg, g=g,
                         xm=xn_ms if (g == 0 and ihalf == 0 and f == 0) else None):
                    if xm:
                        t.wait_ge(kb.dve.h, xm)
                    t.wait_ge(wv[0].h, wv[1])
                    if free is not None:
                        t.wait_ge(free[0].h, free[1])
                    for c in range(NCH):
                        mm = t.matmul(kb.PS[p][:, 0:256], kb.W[buf][:, c * 128:(c + 1) * 128],
                                      kb.A[:, c * 1024 + g * 256: c * 1024 + (g + 1) * 256],
                                      start=(c == 0), stop=(c == NCH - 1))
                        if c == NCH - 1:
                            mm.then_inc(kb.pe.h, 1)
                kb.on("tensor", fmmg)
                msg = kb.pe.inc()
                kb.wconsumed(msg)

                bufu, wvu = kb.wload(lambda gi=gi: wu[gi])
                pu, freeu = kb.next_ps()

                def fmmu(t, buf=bufu, wv=wvu, p=pu, free=freeu, g=g):
                    t.wait_ge(wv[0].h, wv[1])
                    if free is not None:
                        t.wait_ge(free[0].h, free[1])
                    for c in range(NCH):
                        mm = t.matmul(kb.PS[p][:, 0:256], kb.W[buf][:, c * 128:(c + 1) * 128],
                                      kb.A[:, c * 1024 + g * 256: c * 1024 + (g + 1) * 256],
                                      start=(c == 0), stop=(c == NCH - 1))
                        if c == NCH - 1:
                            mm.then_inc(kb.pe.h, 1)
                kb.on("tensor", fmmu)
                msu = kb.pe.inc()
                kb.wconsumed(msu)

                sb_ = f % 2

                def fsil(sc, p=pg, sb_=sb_, gi=gi, msg=msg, w=int_ms.get(f - 2)):
                    sc.wait_ge(kb.pe.h, msg)
                    if w:
                        sc.wait_ge(kb.dve.h, w)
                    sc.activation(kb.STG[:, sb_ * 256:(sb_ + 1) * 256], kb.PS[p][:, 0:256],
                                  AF.Silu, scale=kb.SGT[:, gi:gi + 1]).then_inc(kb.act.h, 1)
                kb.on("scalar", fsil)
                sil_ms = kb.act.inc()
                kb.ps_free[pg] = (kb.act, sil_ms)

                def fint(v, p=pu, sb_=sb_, gi=gi, f=f, msu=msu, sil_ms=sil_ms):
                    v.wait_ge(kb.pe.h, msu)
                    v.wait_ge(kb.act.h, sil_ms)
                    v.scalar_tensor_tensor(kb.B[:, f * 256:(f + 1) * 256], kb.PS[p][:, 0:256],
                                           kb.SUT[:, gi:gi + 1],
                                           kb.STG[:, sb_ * 256:(sb_ + 1) * 256],
                                           op0=ALU.mult, op1=ALU.mult).then_inc(kb.dve.h, 1)
                kb.on("vector", fint)
                mi = kb.dve.inc()
                int_ms[f] = mi
                kb.ps_free[pu] = (kb.dve, mi)
            last_int = kb.dve.n

            ev_last = None
            for hh in range(NCH):
                slabs = [kb.wload(lambda hh=hh, j=j, ihalf=ihalf: wd[hh, ihalf * 2 + j])
                         for j in range(2)]
                p, free = kb.next_ps()

                def fmmd(t, slabs=slabs, p=p, free=free, last_int=last_int):
                    t.wait_ge(kb.dve.h, last_int)
                    if free is not None:
                        t.wait_ge(free[0].h, free[1])
                    for j, (buf, wv) in enumerate(slabs):
                        t.wait_ge(wv[0].h, wv[1])
                        for cc in range(16):
                            f_ = j * 16 + cc
                            mm = t.matmul(kb.PS[p][:, 0:256],
                                          kb.W[buf][:, cc * 128:(cc + 1) * 128],
                                          kb.B[:, f_ * 256:(f_ + 1) * 256],
                                          start=(j == 0 and cc == 0),
                                          stop=(j == 1 and cc == 15))
                            if j == 1 and cc == 15:
                                mm.then_inc(kb.pe.h, 1)
                kb.on("tensor", fmmd)
                md = kb.pe.inc()
                kb.wconsumed(md)
                kb.wconsumed(md)

                def fev(v, p=p, hh=hh, ihalf=ihalf, md=md):
                    v.wait_ge(kb.pe.h, md)
                    dst = kb.SCR[:, hh * 256:(hh + 1) * 256]
                    if ihalf == 0:
                        v.tensor_copy(dst, kb.PS[p][:, 0:256]).then_inc(kb.dve.h, 1)
                    else:
                        v.tensor_add(dst, dst, kb.PS[p][:, 0:256])
                        v.tensor_scalar_mul(dst, dst, kb.SDT[:, hh:hh + 1]).then_inc(kb.dve.h, 1)
                kb.on("vector", fev)
                ev_last = kb.dve.inc()
                kb.ps_free[p] = (kb.dve, ev_last)

        res_ms = emit_ln_post_res(
            kb, 256,
            lambda c, g=g: kb.X32[:, c * 1024 + g * 256: c * 1024 + (g + 1) * 256],
            (kb.dve, ev_last))
    return res_ms


def emit_store(kb, y_d, res_ms):
    tgt = kb.out.inc(16)

    def f(sy, res_ms=res_ms, tgt=tgt):
        sy.wait_ge(kb.dve.h, res_ms)
        sy.dma_start(out=y_d[:, :], in_=kb.X32[:, :]).then_inc(kb.out.h, 16)
        sy.wait_ge(kb.out.h, tgt)
    kb.on("sync", f)


ALL_PHASES = ["l0attn", "l0ffn", "l1attn", "l1ffn"]


def build_program(phases, dbg=()):
    nc = bass.Bass("TRN2", target_bir_lowering=False, debug=False)
    stack = contextlib.ExitStack()
    dr = {}

    def din(name, shape, dtype):
        dr[name] = nc.dram_tensor(name, shape, dtype, kind="ExternalInput").ap()

    din("x", [128, 16 * 1024], F32)
    din("cos0", [128, 1024], BF16)
    din("sin0", [128, 1024], BF16)
    din("tri", [128, 128], BF16)
    din("r0t", [128, 128], BF16)
    din("onec32", [128, 1], F32)
    din("onecbf", [128, 1], BF16)
    din("oner32", [1, 128], F32)
    din("ones128f", [128, 128], F32)
    din("ones128b", [128, 128], BF16)
    din("onerbf", [1, 128], BF16)
    din("biastab0", [128, 64], F32)
    din("biastab1", [128, 64], F32)
    din("cosv1", [128, 16], F32)
    din("sinv1", [128, 16], F32)
    din("consts", [128, 4], F32)
    if "l0attn" in phases:
        for n in ("l0_wq", "l0_wk", "l0_wv", "l0_wo"):
            din(n, [2048, 2048], F32)
        for n in ("l0_bq", "l0_bk", "l0_bo"):
            din(n, [128, 16], F32)
        din("l0_bv_flat", [1, 2048], F32)
    if "l0ffn" in phases:
        din("l0_gu", [4, 2048, 4096], F32)
        din("l0_dw", [4, 2048, 2048], F32)
    if "l1attn" in phases:
        din("l1_wqkvt", [2048, 6144], F32)
        din("l1_bqkv", [128, 48], F32)
        din("l1_woutt", [2048, 2048], F32)
        din("l1_bout", [128, 16], F32)
        din("l1_bv_flat", [1, 2048], F32)
        din("l1rot", [4, 128, 128], BF16)
    if "l1ffn" in phases:
        din("l1_wg", [64, 128, 2048], I8)
        din("l1_wu", [64, 128, 2048], I8)
        din("l1_wd", [16, 4, 128, 2048], I8)
        din("l1_sg", [128, 64], F32)
        din("l1_su", [128, 64], F32)
        din("l1_sd", [128, 16], F32)
    y_d = nc.dram_tensor("y", [128, 16 * 1024], F32, kind="ExternalOutput").ap()

    with nc.Block() as block:
        kb = Kb(nc, stack, block, dbg)
        kb.phases = phases
        kb.X32 = kb.sb("X32", [128, 16 * 1024], F32)
        kb.A = kb.sb("Apool", [128, 16 * 1024], BF16)
        kb.B = kb.sb("Bpool", [128, 16 * 1024], BF16)
        kb.W = [kb.sb(f"W{i}", [128, 2048], BF16) for i in range(4)]
        kb.SCR = kb.sb("SCR", [128, 4096], F32)
        kb.STG = kb.sb("STG", [128, 4096], BF16)
        kb.KS = [kb.sb(f"KS{i}", [128, 1024], BF16) for i in range(2)]
        kb.VS = [kb.sb(f"VS{i}", [128, 1024], BF16) for i in range(2)]
        kb.PARK2 = kb.sb("PARK2", [128, 2048], BF16)
        kb.PT = [kb.sb(f"PT{i}", [128, 1024], BF16) for i in range(2)]
        kb.APL = kb.sb("APL", [128, 1024], F32)
        kb.ROWS = kb.sb("ROWS", [1, 1536], F32)
        kb.ROWSBF = kb.sb("ROWSBF", [1, 512], BF16)
        kb.COS0 = kb.sb("COS0", [128, 1024], BF16)
        kb.SIN0 = kb.sb("SIN0", [128, 1024], BF16)
        kb.TRI = kb.sb("TRI", [128, 128], BF16)
        kb.R0T = kb.sb("R0T", [128, 128], BF16)
        kb.ONEC32 = kb.sb("ONEC32", [128, 1], F32)
        kb.ONECBF = kb.sb("ONECBF", [128, 1], BF16)
        kb.ONER32 = kb.sb("ONER32", [1, 128], F32)
        kb.ONES128F = kb.sb("ONES128F", [128, 128], F32)
        kb.ONES128B = kb.sb("ONES128B", [128, 128], BF16)
        kb.ONERBF = kb.sb("ONERBF", [1, 128], BF16)
        kb.BT0 = kb.sb("BT0", [128, 64], F32)
        kb.BT1 = kb.sb("BT1", [128, 64], F32)
        kb.CV1 = kb.sb("CV1", [128, 16], F32)
        kb.SV1 = kb.sb("SV1", [128, 16], F32)
        kb.CONSTS = kb.sb("CONSTS", [128, 4], F32)
        if "l0attn" in phases:
            kb.BQ = kb.sb("BQ", [128, 16], F32)
            kb.BK = kb.sb("BK", [128, 16], F32)
            kb.BO = kb.sb("BO", [128, 16], F32)
        if "l1attn" in phases:
            kb.BQKV = kb.sb("BQKV", [128, 48], F32)
            kb.BOUT = kb.sb("BOUT", [128, 16], F32)
            kb.L1R = [kb.sb(f"L1R{i}", [128, 128], BF16) for i in range(4)]
        if "l1ffn" in phases:
            kb.SGT = kb.sb("SGT", [128, 64], F32)
            kb.SUT = kb.sb("SUT", [128, 64], F32)
            kb.SDT = kb.sb("SDT", [128, 16], F32)
        kb.PS = [kb.psum(f"PS{i}", [128, 512]) for i in range(4)]
        kb.PSW0 = kb.psum("PSW0", [128, 1024])
        kb.PSW1 = kb.psum("PSW1", [128, 1024])

        kb.kin, kb.kout, kb.vin, kb.vout = {}, {}, {}, {}
        for L, ph in ((0, "l0attn"), (1, "l1attn")):
            if ph in phases:
                kb.kin[L] = nc.dram_tensor(f"kin{L}", [2048, 1024], BF16).ap()
                kb.kout[L] = nc.dram_tensor(f"kout{L}", [16384, 1024], BF16).ap()
                kb.vin[L] = nc.dram_tensor(f"vin{L}", [1024, 2048], BF16).ap()
                kb.vout[L] = nc.dram_tensor(f"vout{L}", [8192, 2048], BF16).ap()

        kb.begin()
        emit_load(kb, dr)
        entry = [(kb.ldr, kb.loads_done)]
        res = 1
        for ph in phases:
            if ph in ("l0attn", "l1attn"):
                L = 0 if ph == "l0attn" else 1
                xn_ms = emit_ln_pre(kb, kb.X32, kb.A, entry)
                kb.dump(f"xn{L}", lambda: kb.A[:, :], [128, 16384], BF16,
                        [(kb.dve, xn_ms)])
                emit_v_proj(kb, dr, L, xn_ms)
                raw_ms = emit_kq_proj(kb, dr, L, "k")
                if L == 0:
                    emit_rope_l0(kb, raw_ms, None, True, kb.kin[0])
                    rawq_ms = emit_kq_proj(kb, dr, L, "q")
                    q_ms = emit_rope_l0(kb, rawq_ms, kb.A, False)
                    q_ready = q_ms[(NCH - 1, 1)]
                else:
                    park = [kb.KS[0][:, :], kb.KS[1][:, :],
                            kb.VS[0][:, :], kb.VS[1][:, :],
                            kb.PARK2[:, 0:1024], kb.PARK2[:, 1024:2048],
                            kb.PT[0][:, :], kb.PT[1][:, :]]
                    km, kloc = emit_rope_l1(kb, raw_ms, False, park)
                    emit_l2_scale(kb, km, kloc, True, kb.kin[1])
                    rawq_ms = emit_kq_proj(kb, dr, L, "q",
                                           b_free=[(kb.stg_s[0], kb.stg_s[0].n),
                                                   (kb.stg_s[1], kb.stg_s[1].n)])
                    qm, qloc = emit_rope_l1(kb, rawq_ms, True, None)
                    q_ready = emit_l2_scale(kb, qm, qloc, False, None)
                coll_v = emit_gather(kb, L)
                kb.dump(f"kin{L}", lambda L=L: kb.kin[L][:, :], [2048, 1024], BF16,
                        [(kb.stg, kb.stg.n)])
                kb.dump(f"vin{L}", lambda L=L: kb.vin[L][:, :], [1024, 2048], BF16,
                        [(kb.stg, kb.stg.n)])
                kb.dump(f"kout{L}", lambda L=L: kb.kout[L][:, :], [16384, 1024], BF16,
                        [(kb.coll, coll_v)])
                kb.dump(f"vout{L}", lambda L=L: kb.vout[L][:, :], [8192, 2048], BF16,
                        [(kb.coll, coll_v)])
                kb.dump(f"q{L}", lambda: kb.A[:, :], [128, 16384], BF16,
                        [(kb.dve, q_ready)])
                ao_ms = emit_attention(kb, L, coll_v)
                kb.dump(f"ao{L}", lambda: kb.B[:, :], [128, 16384], BF16,
                        [(kb.dve, ao_ms)])
                res = emit_outproj(kb, dr, L, ao_ms)
            else:
                xn_ms = emit_ln_pre(kb, kb.X32, kb.A, entry, pack4=(ph == "l0ffn"))
                if ph == "l0ffn":
                    res = emit_ffn_l0(kb, dr, xn_ms)
                else:
                    res = emit_ffn_l1(kb, dr, xn_ms)
            entry = [(kb.dve, res)]
        emit_store(kb, y_d, res)
        kb.end()
    stack.close()
    return nc, set(dr.keys()), kb.dbg_tensors


def stage_inputs(inputs):
    f32 = np.float32
    shared = {}
    x = np.asarray(inputs["L0_attn_x"][0], f32)

    def chunked_cols(a):
        return np.ascontiguousarray(np.asarray(a, f32).reshape(-1, 128).T)

    shared["l0_wq"] = np.asarray(inputs["L0_attn_Wq"], f32)
    shared["l0_wk"] = np.asarray(inputs["L0_attn_Wk"], f32)
    shared["l0_wv"] = np.asarray(inputs["L0_attn_Wv"], f32)
    shared["l0_wo"] = np.asarray(inputs["L0_attn_Wo"], f32)
    shared["l0_bq"] = chunked_cols(inputs["L0_attn_bq"])
    shared["l0_bk"] = chunked_cols(inputs["L0_attn_bk"])
    shared["l0_bo"] = chunked_cols(inputs["L0_attn_bo"])
    shared["l0_bv_flat"] = np.asarray(inputs["L0_attn_bv"], f32).reshape(1, 2048)
    shared["l0_gu"] = np.asarray(inputs["L0_ffn_gate_up_proj"], f32)
    shared["l0_dw"] = np.asarray(inputs["L0_ffn_down_proj"], f32)
    shared["l1_wqkvt"] = np.ascontiguousarray(np.asarray(inputs["L1_attn_Wqkv"], f32).T)
    bqkv = np.asarray(inputs["L1_attn_bqkv"], f32)
    shared["l1_bqkv"] = chunked_cols(bqkv)
    shared["l1_bv_flat"] = np.ascontiguousarray(bqkv[4096:6144].reshape(1, 2048))
    shared["l1_woutt"] = np.ascontiguousarray(np.asarray(inputs["L1_attn_Wout"], f32).T)
    shared["l1_bout"] = chunked_cols(inputs["L1_attn_bout"])
    wg = np.asarray(inputs["L1_ffn_w_gate"])
    wu = np.asarray(inputs["L1_ffn_w_up"])
    wd = np.asarray(inputs["L1_ffn_w_down"])
    shared["l1_wg"] = np.ascontiguousarray(
        wg.reshape(16, 128, 64, 128).transpose(2, 1, 0, 3).reshape(64, 128, 2048))
    shared["l1_wu"] = np.ascontiguousarray(
        wu.reshape(16, 128, 64, 128).transpose(2, 1, 0, 3).reshape(64, 128, 2048))
    shared["l1_wd"] = np.ascontiguousarray(
        wd.reshape(4, 16, 128, 16, 128).transpose(3, 0, 2, 1, 4).reshape(16, 4, 128, 2048))
    shared["l1_sg"] = chunked_cols(inputs["L1_ffn_s_gate"])
    shared["l1_su"] = chunked_cols(inputs["L1_ffn_s_up"])
    shared["l1_sd"] = chunked_cols(inputs["L1_ffn_s_down"])

    shared["tri"] = np.ascontiguousarray(np.triu(np.ones((128, 128), f32)).astype(BFD))
    r0t = np.zeros((128, 128), f32)
    for i in range(64):
        r0t[2 * i, 2 * i + 1] = 1.0
        r0t[2 * i + 1, 2 * i] = -1.0
    shared["r0t"] = r0t.astype(BFD)
    l1r = np.zeros((4, 128, 128), f32)
    for m in range(64):
        l1r[0, 2 * m + 1, m] = -1.0
        l1r[1, 2 * m + 1, m + 64] = -1.0
        l1r[2, 2 * m, m] = 1.0
        l1r[3, 2 * m, m + 64] = 1.0
    shared["l1rot"] = l1r.astype(BFD)
    shared["onec32"] = np.ones((128, 1), f32)
    shared["onecbf"] = np.ones((128, 1), BFD)
    shared["oner32"] = np.ones((1, 128), f32)
    shared["ones128f"] = np.ones((128, 128), f32)
    shared["ones128b"] = np.ones((128, 128), BFD)
    shared["onerbf"] = np.ones((1, 128), BFD)
    inv = np.power(f32(10000.0), -(np.arange(0, E, 2, dtype=f32) / E))
    emb = np.repeat(inv, 2)
    shared["cosv1"] = chunked_cols(np.cos(f32(S) * emb))
    shared["sinv1"] = chunked_cols(np.sin(f32(S) * emb))
    shared["consts"] = np.tile(np.array([[LN_EPS, L2_EPS, AQBIAS, 0.0]], f32), (128, 1))

    per_core = []
    for c in range(NCORES):
        d = dict(shared)
        tiles = RANK_TILES[c]
        rows = np.concatenate([x[g * 128:(g + 1) * 128] for g in tiles], 0)
        d["x"] = np.ascontiguousarray(
            rows.T.reshape(16, 128, 1024).transpose(1, 0, 2).reshape(128, 16 * 1024))
        pos = np.concatenate([np.arange(g * 128, (g + 1) * 128) for g in tiles]).astype(f32)
        exps = -np.arange(0, D, 2, dtype=f32) / D
        thetas = np.repeat(np.power(f32(10000.0), exps), 2)
        ang = thetas[:, None] * pos[None, :]
        d["cos0"] = np.ascontiguousarray(np.cos(ang).astype(BFD))
        d["sin0"] = np.ascontiguousarray(np.sin(ang).astype(BFD))
        for L, base in ((0, L0_SHIFT), (1, 0.0)):
            bt = np.full((128, 64), KILL, f32)
            for p in range(14):
                bt[:, p] = base if p < 2 * c else KILL
            for p in range(30):
                bt[:, 32 + p] = base if p < 30 - 2 * c else KILL
            bt[:, 62] = base
            d[f"biastab{L}"] = bt
        per_core.append(d)
    return per_core


LAUNCH_PLAN = [ALL_PHASES]


def kernel(**inputs):
    per_core = stage_inputs(inputs)
    for phases in LAUNCH_PLAN:
        nc, innames, _ = build_program(phases)
        in_maps = [{k: v for k, v in d.items() if k in innames} for d in per_core]
        res = run_bass_kernel_spmd(nc, in_maps, core_ids=list(range(NCORES)))
        for c in range(NCORES):
            per_core[c]["x"] = np.ascontiguousarray(res.results[c]["y"])
    out = np.zeros((S, E), np.float32)
    for c in range(NCORES):
        xc = per_core[c]["x"].reshape(128, 16, 1024).transpose(1, 0, 2).reshape(2048, 1024)
        for j, g in enumerate(RANK_TILES[c]):
            out[g * 128:(g + 1) * 128] = xc[:, j * 128:(j + 1) * 128].T
    return out[None]


# version 03
# speedup vs baseline: 1.1294x; 1.0793x over previous
import contextlib
import os
from collections import defaultdict

import numpy as np
import ml_dtypes
import concourse.bass as bass
import concourse.mybir as mybir
from concourse.bass_utils import run_bass_kernel_spmd

F32 = mybir.dt.float32
BF16 = mybir.dt.bfloat16
I8 = mybir.dt.int8
AF = mybir.ActivationFunctionType
ALU = mybir.AluOpType
BFD = ml_dtypes.bfloat16

ENGINES = ("sync", "gpsimd", "scalar", "vector", "tensor")

E = 2048
S = 8192
H = 16
D = 128
NCH = 16
NCORES = 8
LN_EPS = 1e-5
L2_EPS = 1e-24
L0_SHIFT = -8.0
KILL = -1e30
RSCALE = float(1.0 / np.sqrt(D))
AQBIAS = float(-0.25 * np.log(2048.0))


def core_tiles(c):
    return [4 * c + j for j in range(4)] + [60 - 4 * c + j for j in range(4)]


RANK_TILES = [core_tiles(c) for c in range(NCORES)]


class Sem:
    def __init__(self, nc, stack, name):
        self.h = stack.enter_context(nc.semaphore(name))
        self.n = 0

    def inc(self, k=1):
        self.n += k
        return self.n


class Kb:
    def __init__(self, nc, stack, block, dbg=()):
        self.nc = nc
        self.stack = stack
        self.block = block
        self.dbg = set(dbg)
        self.dbg_tensors = {}
        self.pe = Sem(nc, stack, "s_pe")
        self.act = Sem(nc, stack, "s_act")
        self.dve = Sem(nc, stack, "s_dve")
        self.wdma = Sem(nc, stack, "s_wdma")
        self.ldr = Sem(nc, stack, "s_ldr")
        self.stg = Sem(nc, stack, "s_stg")
        self.coll = Sem(nc, stack, "s_coll")
        self.out = Sem(nc, stack, "s_out")
        self.w_s = [Sem(nc, stack, f"s_w{i}") for i in range(4)]
        self.ks_s = [Sem(nc, stack, f"s_ks{i}") for i in range(2)]
        self.vs_s = [Sem(nc, stack, f"s_vs{i}") for i in range(2)]
        self.stg_s = [Sem(nc, stack, f"s_stg{i}") for i in range(2)]
        self.rb_s = Sem(nc, stack, "s_rb")
        self.wslab = 0
        self.wfree = []
        self.ops = None
        self.ps_free = [None] * 4
        self.ps_rot = 0
        self.psw0_free = None
        self.psw1_free = None
        self.half_gate = None

    def sb(self, name, shape, dtype):
        return self.stack.enter_context(self.nc.sbuf_tensor(name, shape, dtype))

    def psum(self, name, shape, dtype=F32):
        return self.stack.enter_context(self.nc.psum_tensor(name, shape, dtype))

    def begin(self):
        assert self.ops is None
        self.ops = defaultdict(list)

    def end(self):
        ops = self.ops
        self.ops = None
        for name in ENGINES:
            lst = ops.get(name)
            if lst:
                def run(eng, lst=lst):
                    for f in lst:
                        f(eng)
                getattr(self.block, name)(run)

    def on(self, engine, f):
        self.ops[engine].append(f)

    def wload(self, src_ap_fn, nsub=None):
        k = self.wslab
        self.wslab += 1
        buf = k % 4
        free_ms = self.wfree[k - 4] if k >= 4 else None
        sem = self.w_s[buf]
        wv = sem.inc(16)

        def f(g, buf=buf, free_ms=free_ms, src_ap_fn=src_ap_fn, nsub=nsub, sem=sem):
            if free_ms:
                g.wait_ge(self.pe.h, free_ms)
            dst = self.W[buf][:, :]
            if nsub:
                dst = dst.rearrange("p (c f) -> p c f", c=nsub)
            g.dma_start(out=dst, in_=src_ap_fn()).then_inc(sem.h, 16)
        self.on("gpsimd", f)
        return buf, (sem, wv)

    def wconsumed(self, pe_ms):
        self.wfree.append(pe_ms)

    def next_ps(self):
        i = self.ps_rot % 4
        self.ps_rot += 1
        return i, self.ps_free[i]

    def dump(self, tag, ap_fn, shape, dtype, waits):
        if tag not in self.dbg:
            return
        t = self.nc.dram_tensor("dbg_" + tag, shape, dtype, kind="ExternalOutput").ap()
        self.dbg_tensors[tag] = t
        tgt = self.out.inc(16)

        def f(sy, t=t, ap_fn=ap_fn, waits=list(waits)):
            for s, v in waits:
                sy.wait_ge(s.h, v)
            sy.dma_start(out=t[:, :], in_=ap_fn()).then_inc(self.out.h, 16)
        self.on("sync", f)


def emit_load(kb, dr):
    names = [("X32", "x"), ("COS0", "cos0"), ("SIN0", "sin0"),
             ("TRI", "tri"), ("R0T", "r0t"),
             ("ONEC32", "onec32"), ("ONECBF", "onecbf"),
             ("ONER32", "oner32"), ("ONERBF", "onerbf"),
             ("ONES128F", "ones128f"), ("ONES128B", "ones128b"),
             ("BT0", "biastab0"), ("BT1", "biastab1"),
             ("CV1", "cosv1"), ("SV1", "sinv1"), ("CONSTS", "consts")]
    small = {"l0attn": [("BQ", "l0_bq"), ("BK", "l0_bk"), ("BO", "l0_bo")],
             "l1attn": [("BQKV", "l1_bqkv"), ("BOUT", "l1_bout")],
             "l1ffn": [("SGT", "l1_sg"), ("SUT", "l1_su"), ("SDT", "l1_sd")]}
    for ph in kb.phases:
        names += small.get(ph, [])
    n_dma = len(names) + (4 if "l1attn" in kb.phases else 0)

    def f(sy, names=list(names)):
        for attr, key in names:
            sy.dma_start(out=getattr(kb, attr)[:, :], in_=dr[key][:, :]).then_inc(kb.ldr.h, 16)
        if "l1attn" in kb.phases:
            for i in range(4):
                sy.dma_start(out=kb.L1R[i][:, :], in_=dr["l1rot"][i]).then_inc(kb.ldr.h, 16)
    kb.on("sync", f)
    kb.loads_done = kb.ldr.inc(16 * n_dma)


def emit_ln_pre(kb, src, dst, entry, pack4=False):
    if isinstance(entry, tuple):
        entries = entry
    else:
        entries = (entry, entry)
    apply_ms = None
    for half in range(2):
        entry = entries[half]
        sq_ms = {}
        sumsq_ms = {}
        for c in range(NCH):
            slot = (c % 2) * 512

            def fsq(sc, c=c, slot=slot, half=half,
                    ent=list(entry) if c == 0 else None,
                    w=sumsq_ms.get(c - 2),
                    hg=kb.half_gate if c == 0 else None):
                if ent:
                    for s, v in ent:
                        sc.wait_ge(s.h, v)
                if hg:
                    sc.wait_ge(hg[0].h, hg[1])
                if w:
                    sc.wait_ge(kb.pe.h, w)
                sc.activation(kb.LNS[:, 1024 + slot:1024 + slot + 512],
                              src[:, c * 1024 + half * 512: c * 1024 + half * 512 + 512],
                              AF.Square).then_inc(kb.act.h, 1)
            kb.on("scalar", fsq)
            sq_ms[c] = kb.act.inc()

            def fmm(t, c=c, slot=slot, half=half, sqv=sq_ms[c],
                    ent=list(entry) if c == 0 else None,
                    pw=kb.psw0_free if c == 0 else None):
                if ent:
                    for s, v in ent:
                        t.wait_ge(s.h, v)
                    t.wait_ge(kb.ldr.h, kb.loads_done)
                if pw:
                    t.wait_ge(pw[0].h, pw[1])
                t.matmul(kb.PSW0[:, 0:512], kb.ONES128F[:, :],
                         src[:, c * 1024 + half * 512: c * 1024 + half * 512 + 512],
                         start=(c == 0), stop=(c == NCH - 1))
                t.wait_ge(kb.act.h, sqv)
                t.matmul(kb.PSW0[:, 512:1024], kb.ONES128F[:, :],
                         kb.LNS[:, 1024 + slot:1024 + slot + 512],
                         start=(c == 0), stop=(c == NCH - 1)).then_inc(kb.pe.h, 1)
            kb.on("tensor", fmm)
            sumsq_ms[c] = kb.pe.inc()
        stats_ms = sumsq_ms[NCH - 1]

        def frows(v, stats_ms=stats_ms):
            v.wait_ge(kb.pe.h, stats_ms)
            v.tensor_scalar_mul(kb.LNS[:, 0:512], kb.PSW0[:, 0:512], 1.0 / E)
            v.tensor_mul(kb.LNS[:, 1024:1536], kb.LNS[:, 0:512], kb.LNS[:, 0:512])
            v.scalar_tensor_tensor(kb.LNS[:, 512:1024], kb.PSW0[:, 512:1024], 1.0 / E,
                                   kb.LNS[:, 1024:1536],
                                   op0=ALU.mult, op1=ALU.subtract).then_inc(kb.dve.h, 1)
        kb.on("vector", frows)
        rows1_ms = kb.dve.inc()
        kb.psw0_free = (kb.dve, rows1_ms)
        kb.half_gate = (kb.dve, rows1_ms)

        def facts(sc, rows1_ms=rows1_ms):
            sc.wait_ge(kb.dve.h, rows1_ms)
            sc.activation(kb.LNS[:, 1024:1536], kb.LNS[:, 512:1024], AF.Ln,
                          bias=kb.CONSTS[:, 0:1])
            sc.activation(kb.LNS[:, 512:1024], kb.LNS[:, 1024:1536], AF.Exp,
                          scale=-0.5).then_inc(kb.act.h, 1)
        kb.on("scalar", facts)
        rstd_ms = kb.act.inc()

        def fc2(v, rstd_ms=rstd_ms):
            v.wait_ge(kb.act.h, rstd_ms)
            v.scalar_tensor_tensor(kb.LNS[:, 0:512], kb.LNS[:, 0:512], -1.0,
                                   kb.LNS[:, 512:1024],
                                   op0=ALU.mult, op1=ALU.mult).then_inc(kb.dve.h, 1)
        kb.on("vector", fc2)
        c2_ms = kb.dve.inc()

        for c in range(NCH):
            def fap(v, c=c, half=half):
                xs = src[:, c * 1024 + half * 512: c * 1024 + half * 512 + 512]
                if pack4:
                    out = dst[:, c * 1024:(c + 1) * 1024].rearrange(
                        "p (e g) -> p g e", e=4)[:, half * 128:(half + 1) * 128, :]
                    v.tensor_mul(out, xs.rearrange("p (g e) -> p g e", e=4),
                                 kb.LNS[:, 512:1024].rearrange("p (g e) -> p g e", e=4))
                    mm = v.tensor_add(out, out,
                                      kb.LNS[:, 0:512].rearrange("p (g e) -> p g e", e=4))
                else:
                    out = dst[:, c * 1024 + half * 512: c * 1024 + half * 512 + 512]
                    v.tensor_mul(out, xs, kb.LNS[:, 512:1024])
                    mm = v.tensor_add(out, out, kb.LNS[:, 0:512])
                if c == NCH - 1:
                    mm.then_inc(kb.dve.h, 1)
            kb.on("vector", fap)
        apply_ms = kb.dve.inc()
    return apply_ms


def emit_ln_post_res(kb, ntok, dst_ap_fn, pe_ready):
    mm_ms = {}
    for c in range(NCH):
        slot = 512 + (c % 2) * 256

        def fsq(sc, c=c, slot=slot, w=mm_ms.get(c - 2),
                pr=pe_ready if c == 0 else None):
            if pr:
                sc.wait_ge(pr[0].h, pr[1])
            if w:
                sc.wait_ge(kb.pe.h, w)
            sc.activation(kb.APL[:, slot:slot + 256],
                          kb.SCR[:, c * ntok: c * ntok + 256],
                          AF.Square).then_inc(kb.act.h, 1)
        kb.on("scalar", fsq)
        sqv = kb.act.inc()

        def fmm(t, c=c, slot=slot, sqv=sqv, pr=pe_ready if c == 0 else None,
                pw=kb.psw1_free if c == 0 else None):
            if pr:
                t.wait_ge(pr[0].h, pr[1])
            if pw:
                t.wait_ge(pw[0].h, pw[1])
            t.matmul(kb.PSW1[:, 0:256], kb.ONES128F[:, :],
                     kb.SCR[:, c * ntok: c * ntok + 256],
                     start=(c == 0), stop=(c == NCH - 1))
            t.wait_ge(kb.act.h, sqv)
            t.matmul(kb.PSW1[:, 512:768], kb.ONES128F[:, :],
                     kb.APL[:, slot:slot + 256],
                     start=(c == 0), stop=(c == NCH - 1)).then_inc(kb.pe.h, 1)
        kb.on("tensor", fmm)
        mm_ms[c] = kb.pe.inc()
    stats_ms = mm_ms[NCH - 1]

    def frows(v, stats_ms=stats_ms):
        v.wait_ge(kb.pe.h, stats_ms)
        v.tensor_scalar_mul(kb.APL[:, 0:256], kb.PSW1[:, 0:256], 1.0 / E)
        v.tensor_mul(kb.APL[:, 256:512], kb.APL[:, 0:256], kb.APL[:, 0:256])
        v.scalar_tensor_tensor(kb.APL[:, 256:512], kb.PSW1[:, 512:768], 1.0 / E,
                               kb.APL[:, 256:512],
                               op0=ALU.mult, op1=ALU.subtract).then_inc(kb.dve.h, 1)
    kb.on("vector", frows)
    rows1_ms = kb.dve.inc()
    kb.psw1_free = (kb.dve, rows1_ms)

    def facts(sc, rows1_ms=rows1_ms):
        sc.wait_ge(kb.dve.h, rows1_ms)
        sc.activation(kb.APL[:, 512:768], kb.APL[:, 256:512], AF.Ln,
                      bias=kb.CONSTS[:, 0:1])
        sc.activation(kb.APL[:, 256:512], kb.APL[:, 512:768], AF.Exp,
                      scale=-0.5).then_inc(kb.act.h, 1)
    kb.on("scalar", facts)
    rstd_ms = kb.act.inc()

    def fc2(v, rstd_ms=rstd_ms):
        v.wait_ge(kb.act.h, rstd_ms)
        v.scalar_tensor_tensor(kb.APL[:, 0:256], kb.APL[:, 0:256], -1.0,
                               kb.APL[:, 256:512],
                               op0=ALU.mult, op1=ALU.mult).then_inc(kb.dve.h, 1)
    kb.on("vector", fc2)
    c2_ms = kb.dve.inc()

    for c in range(NCH):
        def fap(v, c=c):
            s_ = kb.SCR[:, c * ntok: c * ntok + 256]
            v.tensor_mul(kb.APL[:, 512:768], s_, kb.APL[:, 256:512])
            v.tensor_add(kb.APL[:, 768:1024], kb.APL[:, 512:768], kb.APL[:, 0:256])
            dst = dst_ap_fn(c)
            mm = v.tensor_add(dst, dst, kb.APL[:, 768:1024])
            if c == NCH - 1:
                mm.then_inc(kb.dve.h, 1)
        kb.on("vector", fap)
    apply_ms = kb.dve.inc()
    return apply_ms


def emit_v_proj(kb, dr, L, xn_ms):
    wsrc = dr["l0_wv_t"] if L == 0 else dr["l1_wv_t"]
    bvflat = dr["l0_bv_flat"] if L == 0 else dr["l1_bv_flat"]
    vin = kb.vin[L]
    evac_ms_prev = None
    prev_mm = None
    for idx in range(8):
        half_t, foB = idx // 4, idx % 4
        rb = 0

        def fbias(g, rb=rb, foB=foB, pm=prev_mm):
            if pm:
                g.wait_ge(kb.pe.h, pm)
            g.dma_start(out=kb.ROWSBF[0:1, rb:rb + 512],
                        in_=bvflat[0:1, foB * 512:(foB + 1) * 512]).then_inc(kb.rb_s.h, 16)
        kb.on("gpsimd", fbias)
        bias_v = kb.rb_s.inc(16)

        slabs = [kb.wload(lambda cq=cq, foB=foB: wsrc[cq, foB]) for cq in range(4)]

        def fmm(t, slabs=slabs, half_t=half_t, foB=foB, bias_v=bias_v, rb=rb,
                xm=xn_ms if idx == 0 else None, ev_prev=evac_ms_prev,
                psf=list(kb.ps_free) if idx == 0 else None):
            if xm:
                t.wait_ge(kb.dve.h, xm)
            if psf:
                for w in psf:
                    if w:
                        t.wait_ge(w[0].h, w[1])
            if ev_prev:
                t.wait_ge(kb.dve.h, ev_prev)
            for cq, (buf, wv) in enumerate(slabs):
                t.wait_ge(wv[0].h, wv[1])
                for c4 in range(4):
                    for tt in range(4):
                        t.matmul(kb.PS[tt][:, :],
                                 kb.A[:, (4 * cq + c4) * 1024 + (half_t * 4 + tt) * 128:
                                      (4 * cq + c4) * 1024 + (half_t * 4 + tt) * 128 + 128],
                                 kb.W[buf][:, c4 * 512:(c4 + 1) * 512],
                                 start=(cq == 0 and c4 == 0), stop=False)
            t.wait_ge(kb.rb_s.h, bias_v)
            for tt in range(4):
                mm = t.matmul(kb.PS[tt][:, :], kb.ONERBF[0:1, :],
                              kb.ROWSBF[0:1, rb:rb + 512], start=False, stop=True)
                if tt == 3:
                    mm.then_inc(kb.pe.h, 1)
        kb.on("tensor", fmm)
        mm_ms = kb.pe.inc()
        for _ in range(4):
            kb.wconsumed(mm_ms)

        sp = idx % 2
        st_w = kb.stg_s[sp].n

        def fev(v, mm_ms=mm_ms, sp=sp, st_w=st_w):
            v.wait_ge(kb.pe.h, mm_ms)
            if st_w:
                v.wait_ge(kb.stg_s[sp].h, st_w)
            for tt in range(4):
                mm = v.tensor_copy(kb.STG[:, sp * 2048 + tt * 512: sp * 2048 + (tt + 1) * 512],
                                   kb.PS[tt][:, :])
                if tt == 3:
                    mm.then_inc(kb.dve.h, 1)
        kb.on("vector", fev)
        evac_ms = kb.dve.inc()
        evac_ms_prev = evac_ms
        prev_mm = mm_ms

        def fst(sc, evac_ms=evac_ms, half_t=half_t, foB=foB, sp=sp):
            sc.wait_ge(kb.dve.h, evac_ms)
            for tt in range(4):
                sc.dma_start(
                    out=vin[half_t * 512 + tt * 128: half_t * 512 + (tt + 1) * 128,
                            foB * 512:(foB + 1) * 512],
                    in_=kb.STG[:, sp * 2048 + tt * 512: sp * 2048 + (tt + 1) * 512]
                ).then_inc(kb.stg_s[sp].h, 16)
        kb.on("scalar", fst)
        kb.stg_s[sp].inc(64)
    for i in range(4):
        kb.ps_free[i] = (kb.dve, evac_ms_prev)


def emit_kq_proj(kb, dr, L, which, b_free=None, xn_ms=None):
    if L == 0:
        wsrc = dr["l0_wq_t"] if which == "q" else dr["l0_wk_t"]
        bias_t, bias_off = ("BQ" if which == "q" else "BK"), 0
    else:
        wsrc = dr["l1_wq_t"] if which == "q" else dr["l1_wk_t"]
        bias_t, bias_off = "BQKV", (0 if which == "q" else 16)
    bias = getattr(kb, bias_t)
    ev_ms = {}
    for fo in range(NCH):
        buf, wv = kb.wload(lambda fo=fo: wsrc[fo])
        for half in range(2):
            p, free = kb.next_ps()

            def fmm(t, buf=buf, wv=wv, p=p, free=free, fo=fo, half=half,
                    xm=xn_ms if (fo == 0 and half == 0) else None):
                if xm:
                    t.wait_ge(kb.dve.h, xm)
                t.wait_ge(wv[0].h, wv[1])
                if free is not None:
                    t.wait_ge(free[0].h, free[1])
                for c in range(NCH):
                    mm = t.matmul(kb.PS[p][:, :], kb.W[buf][:, c * 128:(c + 1) * 128],
                                  kb.A[:, c * 1024 + half * 512: c * 1024 + half * 512 + 512],
                                  start=(c == 0), stop=(c == NCH - 1))
                    if c == NCH - 1:
                        mm.then_inc(kb.pe.h, 1)
            kb.on("tensor", fmm)
            mm_ms = kb.pe.inc()
            if half == 1:
                kb.wconsumed(mm_ms)

            def fev(v, p=p, fo=fo, half=half, mm_ms=mm_ms,
                    bf=b_free if (fo == 0 and half == 0) else None):
                v.wait_ge(kb.pe.h, mm_ms)
                if bf:
                    for s, vv in bf:
                        v.wait_ge(s.h, vv)
                v.tensor_scalar_add(
                    kb.B[:, fo * 1024 + half * 512: fo * 1024 + half * 512 + 512],
                    kb.PS[p][:, :],
                    bias[:, bias_off + fo: bias_off + fo + 1]).then_inc(kb.dve.h, 1)
            kb.on("vector", fev)
            ev = kb.dve.inc()
            ev_ms[(fo, half)] = ev
            kb.ps_free[p] = (kb.dve, ev)
    return ev_ms


def emit_rope_l0(kb, raw_ms, dst, stage, kdma=None):
    out_ms = {}
    st_prev = {}
    for fo in range(NCH):
        sp = fo % 2
        for half in range(2):
            p, free = kb.next_ps()

            def frot(t, p=p, free=free, fo=fo, half=half, rms=raw_ms[(fo, half)]):
                t.wait_ge(kb.dve.h, rms)
                if free is not None:
                    t.wait_ge(free[0].h, free[1])
                t.matmul(kb.PS[p][:, :], kb.R0T[:, :],
                         kb.B[:, fo * 1024 + half * 512: fo * 1024 + half * 512 + 512],
                         start=True, stop=True).then_inc(kb.pe.h, 1)
            kb.on("tensor", frot)
            rot_ms = kb.pe.inc()

            def fcomb(v, p=p, fo=fo, half=half, rot_ms=rot_ms, sp=sp,
                      stw=st_prev.get(fo - 2)):
                v.wait_ge(kb.pe.h, rot_ms)
                if stw:
                    v.wait_ge(kb.stg_s[sp].h, stw)
                hs = half * 512
                v.tensor_mul(kb.APL[:, 0:512], kb.PS[p][:, :], kb.SIN0[:, hs:hs + 512])
                v.tensor_mul(kb.APL[:, 512:1024],
                             kb.B[:, fo * 1024 + hs: fo * 1024 + hs + 512],
                             kb.COS0[:, hs:hs + 512])
                if stage:
                    tgt = kb.STG[:, sp * 2048 + hs: sp * 2048 + hs + 512]
                else:
                    tgt = dst[:, fo * 1024 + hs: fo * 1024 + hs + 512]
                v.tensor_add(tgt, kb.APL[:, 0:512], kb.APL[:, 512:1024]).then_inc(kb.dve.h, 1)
            kb.on("vector", fcomb)
            cm = kb.dve.inc()
            out_ms[(fo, half)] = cm
            kb.ps_free[p] = (kb.dve, cm)
        if stage:
            def fst(sc, fo=fo, sp=sp, cm=out_ms[(fo, 1)]):
                sc.wait_ge(kb.dve.h, cm)
                sc.dma_start(out=kdma[fo * 128:(fo + 1) * 128, :],
                             in_=kb.STG[:, sp * 2048: sp * 2048 + 1024]).then_inc(kb.stg_s[sp].h, 16)
            kb.on("scalar", fst)
            st_prev[fo] = kb.stg_s[sp].inc(16)
    return out_ms


def l1_sources(fo):
    if fo < 8:
        return 2 * fo, 2 * fo + 1, 0, 1
    return 2 * (fo - 8), 2 * (fo - 8) + 1, 2, 3


def emit_rope_l1(kb, raw_ms, to_a, park_aps):
    out_ms = {}
    loc = {}
    for fo in range(NCH):
        s0, s1, p0, p1 = l1_sources(fo)
        p, free = kb.next_ps()
        qb = fo % 2

        def frot(t, p=p, free=free, s0=s0, s1=s1, p0=p0, p1=p1,
                 w=max(raw_ms[(s0, 1)], raw_ms[(s1, 1)])):
            t.wait_ge(kb.dve.h, w)
            if free is not None:
                t.wait_ge(free[0].h, free[1])
            t.matmul(kb.PS[p][:, :], kb.L1R[p0][:, :],
                     kb.B[:, s0 * 1024: s0 * 1024 + 512], start=True, stop=False)
            t.matmul(kb.PS[p][:, :], kb.L1R[p1][:, :],
                     kb.B[:, s1 * 1024: s1 * 1024 + 512], start=False, stop=True).then_inc(kb.pe.h, 1)
        kb.on("tensor", frot)
        rot1 = kb.pe.inc()
        p2, free2 = kb.next_ps()

        def frot2(t, p=p2, free=free2, s0=s0, s1=s1, p0=p0, p1=p1):
            if free is not None:
                t.wait_ge(free[0].h, free[1])
            t.matmul(kb.PS[p][:, :], kb.L1R[p0][:, :],
                     kb.B[:, s0 * 1024 + 512: (s0 + 1) * 1024], start=True, stop=False)
            t.matmul(kb.PS[p][:, :], kb.L1R[p1][:, :],
                     kb.B[:, s1 * 1024 + 512: (s1 + 1) * 1024],
                     start=False, stop=True).then_inc(kb.pe.h, 1)
        kb.on("tensor", frot2)
        rot2 = kb.pe.inc()

        def fqcos(sc, fo=fo, qb=qb, rm=raw_ms[(fo, 1)]):
            sc.wait_ge(kb.dve.h, rm)
            sc.activation(kb.STG[:, qb * 1024:(qb + 1) * 1024],
                          kb.B[:, fo * 1024:(fo + 1) * 1024],
                          AF.Copy, scale=kb.CV1[:, fo:fo + 1]).then_inc(kb.act.h, 1)
        kb.on("scalar", fqcos)
        qc_ms = kb.act.inc()

        if to_a:
            tgt = kb.A[:, fo * 1024:(fo + 1) * 1024]
        elif fo < 8:
            tgt = park_aps[fo]
        else:
            tgt = kb.B[:, 2 * (fo - 8) * 1024: (2 * (fo - 8) + 1) * 1024]
        loc[fo] = tgt

        def fcomb(v, p=p, p2=p2, fo=fo, qb=qb, rot2=rot2, qc_ms=qc_ms, tgt=tgt):
            v.wait_ge(kb.pe.h, rot2)
            v.wait_ge(kb.act.h, qc_ms)
            v.scalar_tensor_tensor(tgt[:, 0:512], kb.PS[p][:, :], kb.SV1[:, fo:fo + 1],
                                   kb.STG[:, qb * 1024: qb * 1024 + 512],
                                   op0=ALU.mult, op1=ALU.add)
            v.scalar_tensor_tensor(tgt[:, 512:1024], kb.PS[p2][:, :], kb.SV1[:, fo:fo + 1],
                                   kb.STG[:, qb * 1024 + 512: (qb + 1) * 1024],
                                   op0=ALU.mult, op1=ALU.add).then_inc(kb.dve.h, 1)
        kb.on("vector", fcomb)
        cm = kb.dve.inc()
        out_ms[fo] = cm
        kb.ps_free[p] = (kb.dve, cm)
        kb.ps_free[p2] = (kb.dve, cm)
    return out_ms, loc


def emit_l2_scale(kb, out_ms, loc, stage_dma, kdma):
    sq_pe = {}
    for fo in range(NCH):
        qb = 2 + (fo % 2)

        def fsq(sc, fo=fo, qb=qb, cm=out_ms[fo], w=sq_pe.get(fo - 2)):
            sc.wait_ge(kb.dve.h, cm)
            if w:
                sc.wait_ge(kb.pe.h, w)
            sc.activation(kb.STG[:, qb * 1024:(qb + 1) * 1024], loc[fo],
                          AF.Square).then_inc(kb.act.h, 1)
        kb.on("scalar", fsq)
        sqv = kb.act.inc()

        def fmm(t, fo=fo, qb=qb, sqv=sqv, pw=kb.psw1_free if fo == 0 else None):
            if pw:
                t.wait_ge(pw[0].h, pw[1])
            t.wait_ge(kb.act.h, sqv)
            t.matmul(kb.PSW1[:, 0:512], kb.ONES128B[:, :],
                     kb.STG[:, qb * 1024: qb * 1024 + 512],
                     start=(fo == 0), stop=(fo == NCH - 1))
            t.matmul(kb.PSW1[:, 512:1024], kb.ONES128B[:, :],
                     kb.STG[:, qb * 1024 + 512: (qb + 1) * 1024],
                     start=(fo == 0), stop=(fo == NCH - 1)).then_inc(kb.pe.h, 1)
        kb.on("tensor", fmm)
        sq_pe[fo] = kb.pe.inc()
    ss_ms = sq_pe[NCH - 1]

    def facts(sc, ss_ms=ss_ms):
        sc.wait_ge(kb.pe.h, ss_ms)
        sc.activation(kb.SCR[:, 0:512], kb.PSW1[:, 0:512], AF.Ln,
                      bias=kb.CONSTS[:, 1:2])
        sc.activation(kb.SCR[:, 1024:1536], kb.SCR[:, 0:512], AF.Exp,
                      scale=-0.5, bias=kb.CONSTS[:, 2:3])
        sc.activation(kb.SCR[:, 512:1024], kb.PSW1[:, 512:1024], AF.Ln,
                      bias=kb.CONSTS[:, 1:2])
        sc.activation(kb.SCR[:, 1536:2048], kb.SCR[:, 512:1024], AF.Exp,
                      scale=-0.5, bias=kb.CONSTS[:, 2:3]).then_inc(kb.act.h, 1)
    kb.on("scalar", facts)
    a_ms = kb.act.inc()
    kb.psw1_free = (kb.act, a_ms)

    last = None
    for fo in range(NCH):
        def fsc(v, fo=fo, a_ms=a_ms):
            if fo == 0:
                v.wait_ge(kb.act.h, a_ms)
            t = loc[fo]
            v.tensor_mul(t[:, 0:512], t[:, 0:512], kb.SCR[:, 1024:1536])
            v.tensor_mul(t[:, 512:1024], t[:, 512:1024],
                         kb.SCR[:, 1536:2048]).then_inc(kb.dve.h, 1)
        kb.on("vector", fsc)
        last = kb.dve.inc()
        if stage_dma:
            def fst(sc, fo=fo, last=last):
                sc.wait_ge(kb.dve.h, last)
                sc.dma_start(out=kdma[fo * 128:(fo + 1) * 128, :],
                             in_=loc[fo]).then_inc(kb.stg_s[fo % 2].h, 16)
            kb.on("scalar", fst)
            kb.stg_s[fo % 2].inc(16)
    return last


def emit_gather_one(kb, L, which):
    stg_v = (kb.stg_s[0].n, kb.stg_s[1].n)
    src_ap = kb.kin[L] if which == "k" else kb.vin[L]
    dst_ap = kb.kout[L] if which == "k" else kb.vout[L]

    def f(g, stg_v=stg_v, src_ap=src_ap, dst_ap=dst_ap):
        if stg_v[0]:
            g.wait_ge(kb.stg_s[0].h, stg_v[0])
        if stg_v[1]:
            g.wait_ge(kb.stg_s[1].h, stg_v[1])
        g.collective_compute("AllGather", ALU.bypass,
                             replica_groups=[list(range(NCORES))],
                             ins=[src_ap[:, :]], outs=[dst_ap[:, :]]).then_inc(kb.coll.h, 1)
    kb.on("gpsimd", f)
    return kb.coll.inc(1)


def build_head_pairs():
    pairs = []
    for r in range(8):
        tiles = RANK_TILES[r]
        a_items = [(loc, 0, tiles[loc]) for loc in range(4) if tiles[loc] < 28]
        b_items = [(loc, 1, tiles[loc]) for loc in range(8) if tiles[loc] < 60]
        for i in range(0, len(a_items), 2):
            pairs.append({"kind": "past", "r": r, "items": a_items[i:i + 2]})
        for i in range(0, len(b_items), 2):
            pairs.append({"kind": "past", "r": r, "items": b_items[i:i + 2]})
    for rho in range(4):
        pairs.append({"kind": "diag", "rho": rho})
    return pairs


HEAD_PAIRS = build_head_pairs()


def select_pairs():
    sel = os.environ.get("K_PAIRSEL", "all")
    if sel == "all":
        return HEAD_PAIRS
    if sel == "diag":
        return [p for p in HEAD_PAIRS if p["kind"] == "diag"]
    if sel.startswith("first:"):
        return HEAD_PAIRS[:int(sel.split(":")[1])] +             [p for p in HEAD_PAIRS if p["kind"] == "diag"]
    if sel.startswith("rank:"):
        rs = set(int(x) for x in sel.split(":")[1].split("-"))
        return [p for p in HEAD_PAIRS if p["kind"] == "diag" or p["r"] in rs]
    if sel == "grpA":
        return [p for p in HEAD_PAIRS if p["kind"] == "diag"
                or p["items"][0][1] == 0]
    if sel == "grpB":
        return [p for p in HEAD_PAIRS if p["kind"] == "diag"
                or p["items"][0][1] == 1]
    raise ValueError(sel)


SEL_PAIRS = select_pairs()
NPAIRS = len(SEL_PAIRS)


def emit_attention(kb, L, coll_k, coll_v):
    scale = RSCALE if L == 0 else 1.0
    biasc = L0_SHIFT if L == 0 else 0.0
    BT = kb.BT0 if L == 0 else kb.BT1
    kout, vout = kb.kout[L], kb.vout[L]
    kin, vin = kb.kin[L], kb.vin[L]

    ks_last = {}
    vs_last = {}
    ks_lds = {}
    vs_lds = {}
    ks_owner = {}
    vs_owner = {}
    epi_dve = {0: kb.ps_free[0], 1: kb.ps_free[1]}
    epi_cp = {0: kb.ps_free[2], 1: kb.ps_free[3]}
    epi_prev_mul = {0: None, 1: None}

    def ensure_k(h, r):
        key = h * 9 + r
        if key in ks_lds:
            return
        kbuf = key % 2
        prev = ks_owner.get(kbuf)
        ks_owner[kbuf] = key

        def fk(sy, r=r, h=h, kbuf=kbuf, w=ks_last.get(prev), ec=coll_k):
            sy.wait_ge(kb.coll.h, ec)
            if w:
                sy.wait_ge(kb.pe.h, w)
            if r < 8:
                src = kout[r * 2048 + h * 128: r * 2048 + (h + 1) * 128, :]
            else:
                src = kin[h * 128:(h + 1) * 128, :]
            sy.dma_start(out=kb.KS[kbuf][:, :], in_=src).then_inc(kb.ks_s[kbuf].h, 16)
        kb.on("sync", fk)
        ks_lds[key] = (kb.ks_s[kbuf], kb.ks_s[kbuf].inc(16))

    def ensure_v(h, r):
        key = h * 9 + r
        if key in vs_lds:
            return
        vbuf = key % 2
        prev = vs_owner.get(vbuf)
        vs_owner[vbuf] = key

        def fv(sy, r=r, h=h, vbuf=vbuf, w=vs_last.get(prev), ec=coll_v):
            sy.wait_ge(kb.coll.h, ec)
            if w:
                sy.wait_ge(kb.pe.h, w)
            if r < 8:
                src = vout[r * 1024:(r + 1) * 1024,
                           h * 128:(h + 1) * 128].rearrange("(j p) d -> p j d", p=128)
            else:
                src = vin[:, h * 128:(h + 1) * 128].rearrange("(j p) d -> p j d", p=128)
            sy.dma_start(out=kb.VS[vbuf][:, :].rearrange("p (j d) -> p j d", j=8),
                         in_=src).then_inc(kb.vs_s[vbuf].h, 16)
        kb.on("sync", fv)
        vs_lds[key] = (kb.vs_s[vbuf], kb.vs_s[vbuf].inc(16))

    for h in range(H):
        hp = h // 2
        exp_ms = {}
        qk_ms = {}
        pv_ms = {}
        mask_ms = {}
        pv_started = {0: False, 1: False}

        def pair_rank(pi):
            return SEL_PAIRS[pi].get("r", 8)

        def pair_items(pi):
            pr = SEL_PAIRS[pi]
            if pr["kind"] == "past":
                return [(it[1], it[2], it[0], 0) for it in pr["items"]]
            rho = pr["rho"]
            return [(0, None, rho, rho * 128), (1, None, 4 + rho, rho * 128)]

        def emit_pv(pi):
            items = pair_items(pi)
            r = pair_rank(pi)
            sw = pi % 2
            vbuf = (h * 9 + r) % 2
            em = exp_ms[pi]
            dm = mask_ms.get(pi)
            starts = []
            for (grp, kt, loc, q0) in items:
                starts.append(not pv_started[grp])
                pv_started[grp] = True

            def fpv(t, items=items, sw=sw, r=r, vbuf=vbuf, h=h, em=em, dm=dm,
                    starts=tuple(starts), vld=vs_lds[h * 9 + r],
                    _unused=None,
                    epi_d=(epi_dve[0], epi_dve[1]),
                    epi_c=(epi_cp[0], epi_cp[1])):
                t.wait_ge(kb.act.h, em)
                if dm:
                    t.wait_ge(kb.dve.h, dm)
                t.wait_ge(vld[0].h, vld[1])
                last_mm = None
                for ix, (grp, kt, loc, q0) in enumerate(items):
                    st = starts[ix]
                    if st and epi_d[grp]:
                        t.wait_ge(epi_d[grp][0].h, epi_d[grp][1])
                    if st and epi_c[grp]:
                        t.wait_ge(epi_c[grp][0].h, epi_c[grp][1])
                    n = 512 - q0
                    vch = kb.VS[vbuf][:, loc * 128:(loc + 1) * 128]
                    pt = kb.PT[sw][:, ix * 512: ix * 512 + n]
                    t.matmul(kb.PS[grp][:, q0:512], vch, pt, start=st, stop=False)
                    last_mm = t.matmul(kb.PS[2 + grp][:, q0:512], kb.ONES128B[:, :], pt,
                                       start=st, stop=False)
                last_mm.then_inc(kb.pe.h, 1)
            kb.on("tensor", fpv)
            ms = kb.pe.inc()
            vs_last[h * 9 + r] = ms
            return ms

        for pi in range(NPAIRS):
            pr = SEL_PAIRS[pi]
            r = pair_rank(pi)
            ensure_k(h, r)
            ensure_v(h, r)
            sw = pi % 2
            psw = kb.PSW0 if sw == 0 else kb.PSW1
            items = pair_items(pi)
            if pr["kind"] == "past":
                bcol = (0 if items[0][0] == 0 else 32) + items[0][1] // 2
            else:
                bcol = None
            kbuf = (h * 9 + r) % 2

            ser = (os.environ.get("K_SERIAL") and pi == 0 and h > 0)

            def fqk(t, items=items, sw=sw, psw=psw, kbuf=kbuf, h=h,
                    kld=ks_lds[h * 9 + r], expw=exp_ms.get(pi - 2),
                    pw=(kb.psw0_free if sw == 0 else kb.psw1_free)
                    if (h == 0 and pi < 2) else None,
                    serw=(epi_cp[0], epi_cp[1]) if ser else None):
                t.wait_ge(kld[0].h, kld[1])
                if serw:
                    for s in serw:
                        if s:
                            t.wait_ge(s[0].h, s[1])
                if expw:
                    t.wait_ge(kb.act.h, expw)
                if pw:
                    t.wait_ge(pw[0].h, pw[1])
                for ix, (grp, kt, loc, q0) in enumerate(items):
                    n = 512 - q0
                    kch = kb.KS[kbuf][:, loc * 128:(loc + 1) * 128]
                    q = kb.A[:, h * 1024 + grp * 512 + q0: h * 1024 + (grp + 1) * 512]
                    mm = t.matmul(psw[:, ix * 512: ix * 512 + n], kch, q,
                                  start=True, stop=True)
                    if ix == len(items) - 1:
                        mm.then_inc(kb.pe.h, 1)
            kb.on("tensor", fqk)
            qk_ms[pi] = kb.pe.inc()
            ks_last[h * 9 + r] = qk_ms[pi]

            wid = 512 + (512 - items[-1][3])

            def fexp(sc, sw=sw, psw=psw, wid=wid, bcol=bcol,
                     qv=qk_ms[pi], pvw=pv_ms.get(pi - 2)):
                sc.wait_ge(kb.pe.h, qv)
                if pvw:
                    sc.wait_ge(kb.pe.h, pvw)
                bias = BT[:, bcol:bcol + 1] if bcol is not None else BT[:, 62:63]
                sc.activation(kb.PT[sw][:, 0:wid], psw[:, 0:wid], AF.Exp,
                              scale=scale, bias=bias).then_inc(kb.act.h, 1)
            kb.on("scalar", fexp)
            exp_ms[pi] = kb.act.inc()

            if pr["kind"] == "diag":
                def fmask(v, sw=sw, ev=exp_ms[pi]):
                    v.wait_ge(kb.act.h, ev)
                    v.tensor_mul(kb.PT[sw][:, 0:128], kb.PT[sw][:, 0:128], kb.TRI[:, :])
                    v.tensor_mul(kb.PT[sw][:, 512:640], kb.PT[sw][:, 512:640],
                                 kb.TRI[:, :]).then_inc(kb.dve.h, 1)
                kb.on("vector", fmask)
                mask_ms[pi] = kb.dve.inc()

            if os.environ.get("K_NOLOOK"):
                pv_ms[pi] = emit_pv(pi)
            elif pi >= 1:
                pv_ms[pi - 1] = emit_pv(pi - 1)
            if pi + 1 < NPAIRS:
                ensure_k(h, pair_rank(pi + 1))
                ensure_v(h, pair_rank(pi + 1))
        if not os.environ.get("K_NOLOOK"):
            pv_ms[NPAIRS - 1] = emit_pv(NPAIRS - 1)
        head_end = pv_ms[NPAIRS - 1]

        for grp in range(2):
            def fep_act(sc, grp=grp, he=head_end,
                        pw=epi_prev_mul[grp]):
                sc.wait_ge(kb.pe.h, he)
                if pw:
                    sc.wait_ge(pw[0].h, pw[1])
                sc.activation(kb.SCR[:, grp * 512:(grp + 1) * 512],
                              kb.PS[2 + grp][:, :], AF.Ln)
                sc.activation(kb.APL[:, grp * 512:(grp + 1) * 512],
                              kb.SCR[:, grp * 512:(grp + 1) * 512],
                              AF.Exp, scale=-1.0).then_inc(kb.act.h, 1)
            kb.on("scalar", fep_act)
            rec_ms = kb.act.inc()
            epi_cp[grp] = (kb.act, rec_ms)

            def fep_dve(v, grp=grp, h=h, rec_ms=rec_ms):
                v.wait_ge(kb.act.h, rec_ms)
                v.tensor_mul(kb.B[:, h * 1024 + grp * 512: h * 1024 + (grp + 1) * 512],
                             kb.PS[grp][:, :],
                             kb.APL[:, grp * 512:(grp + 1) * 512]).then_inc(kb.dve.h, 1)
            kb.on("vector", fep_dve)
            cp = kb.dve.inc()
            epi_dve[grp] = (kb.dve, cp)
            epi_prev_mul[grp] = (kb.dve, cp)
    for i in range(4):
        kb.ps_free[i] = epi_dve[i % 2] if i < 2 else epi_cp[i % 2]
    kb.psw0_free = (kb.act, exp_ms[NPAIRS - 2])
    kb.psw1_free = (kb.act, exp_ms[NPAIRS - 1])
    return kb.dve.n


def emit_outproj(kb, dr, L, ao_ms):
    wsrc = dr["l0_wo_t"] if L == 0 else dr["l1_wo_t"]
    bias = kb.BO if L == 0 else kb.BOUT
    res_ms = []
    for g in range(4):
        ev_last = None
        for fo in range(NCH):
            buf, wv = kb.wload(lambda fo=fo: wsrc[fo])
            p, free = kb.next_ps()

            def fmm(t, buf=buf, wv=wv, p=p, free=free, g=g, fo=fo,
                    am=ao_ms if (g == 0 and fo == 0) else None):
                if am:
                    t.wait_ge(kb.dve.h, am)
                t.wait_ge(wv[0].h, wv[1])
                if free is not None:
                    t.wait_ge(free[0].h, free[1])
                for c in range(NCH):
                    mm = t.matmul(kb.PS[p][:, 0:256], kb.W[buf][:, c * 128:(c + 1) * 128],
                                  kb.B[:, c * 1024 + g * 256: c * 1024 + (g + 1) * 256],
                                  start=(c == 0), stop=(c == NCH - 1))
                    if c == NCH - 1:
                        mm.then_inc(kb.pe.h, 1)
            kb.on("tensor", fmm)
            mm_ms = kb.pe.inc()
            kb.wconsumed(mm_ms)

            def fev(v, p=p, fo=fo, mm_ms=mm_ms):
                v.wait_ge(kb.pe.h, mm_ms)
                v.tensor_scalar_add(kb.SCR[:, fo * 256:(fo + 1) * 256],
                                    kb.PS[p][:, 0:256],
                                    bias[:, fo:fo + 1]).then_inc(kb.dve.h, 1)
            kb.on("vector", fev)
            ev_last = kb.dve.inc()
            kb.ps_free[p] = (kb.dve, ev_last)
        res_ms.append(emit_ln_post_res(
            kb, 256,
            lambda c, g=g: kb.X32[:, c * 1024 + g * 256: c * 1024 + (g + 1) * 256],
            (kb.dve, ev_last)))
    return res_ms


def emit_ffn_l0(kb, dr, xn_ms):
    gu, dw = dr["l0_gu_t"], dr["l0_dw_t"]
    res_ms = None
    for e in range(4):
        int_ms = {}
        for f in range(NCH):
            bufg, wvg = kb.wload(lambda e=e, f=f: gu[e, f])
            pg, freeg = kb.next_ps()

            def fmmg(t, buf=bufg, wv=wvg, p=pg, free=freeg, e=e,
                     xm=xn_ms if (e == 0 and f == 0) else None):
                if xm:
                    t.wait_ge(kb.dve.h, xm)
                t.wait_ge(wv[0].h, wv[1])
                if free is not None:
                    t.wait_ge(free[0].h, free[1])
                for c in range(NCH):
                    mm = t.matmul(kb.PS[p][:, 0:256], kb.W[buf][:, c * 128:(c + 1) * 128],
                                  kb.A[:, c * 1024 + e * 256: c * 1024 + (e + 1) * 256],
                                  start=(c == 0), stop=(c == NCH - 1))
                    if c == NCH - 1:
                        mm.then_inc(kb.pe.h, 1)
            kb.on("tensor", fmmg)
            msg = kb.pe.inc()
            kb.wconsumed(msg)

            bufu, wvu = kb.wload(lambda e=e, f=f: gu[e, 16 + f])
            pu, freeu = kb.next_ps()

            def fmmu(t, buf=bufu, wv=wvu, p=pu, free=freeu, e=e):
                t.wait_ge(wv[0].h, wv[1])
                if free is not None:
                    t.wait_ge(free[0].h, free[1])
                for c in range(NCH):
                    mm = t.matmul(kb.PS[p][:, 0:256], kb.W[buf][:, c * 128:(c + 1) * 128],
                                  kb.A[:, c * 1024 + e * 256: c * 1024 + (e + 1) * 256],
                                  start=(c == 0), stop=(c == NCH - 1))
                    if c == NCH - 1:
                        mm.then_inc(kb.pe.h, 1)
            kb.on("tensor", fmmu)
            msu = kb.pe.inc()
            kb.wconsumed(msu)

            sb_ = f % 2

            def fsil(sc, p=pg, sb_=sb_, msg=msg, w=int_ms.get(f - 2)):
                sc.wait_ge(kb.pe.h, msg)
                if w:
                    sc.wait_ge(kb.dve.h, w)
                sc.activation(kb.STG[:, sb_ * 256:(sb_ + 1) * 256], kb.PS[p][:, 0:256],
                              AF.Silu).then_inc(kb.act.h, 1)
            kb.on("scalar", fsil)
            sil_ms = kb.act.inc()
            kb.ps_free[pg] = (kb.act, sil_ms)

            def fint(v, p=pu, sb_=sb_, f=f, msu=msu, sil_ms=sil_ms):
                v.wait_ge(kb.pe.h, msu)
                v.wait_ge(kb.act.h, sil_ms)
                v.tensor_mul(kb.B[:, f * 256:(f + 1) * 256],
                             kb.STG[:, sb_ * 256:(sb_ + 1) * 256],
                             kb.PS[p][:, 0:256]).then_inc(kb.dve.h, 1)
            kb.on("vector", fint)
            mi = kb.dve.inc()
            int_ms[f] = mi
            kb.ps_free[pu] = (kb.dve, mi)
        last_int = kb.dve.n

        ev_last = None
        for hh in range(NCH):
            buf, wv = kb.wload(lambda e=e, hh=hh: dw[e, hh])
            p, free = kb.next_ps()

            def fmmd(t, buf=buf, wv=wv, p=p, free=free, last_int=last_int):
                t.wait_ge(kb.dve.h, last_int)
                t.wait_ge(wv[0].h, wv[1])
                if free is not None:
                    t.wait_ge(free[0].h, free[1])
                for i in range(NCH):
                    mm = t.matmul(kb.PS[p][:, 0:256], kb.W[buf][:, i * 128:(i + 1) * 128],
                                  kb.B[:, i * 256:(i + 1) * 256],
                                  start=(i == 0), stop=(i == NCH - 1))
                    if i == NCH - 1:
                        mm.then_inc(kb.pe.h, 1)
            kb.on("tensor", fmmd)
            md = kb.pe.inc()
            kb.wconsumed(md)

            def fev(v, p=p, hh=hh, md=md):
                v.wait_ge(kb.pe.h, md)
                v.tensor_copy(kb.SCR[:, hh * 256:(hh + 1) * 256],
                              kb.PS[p][:, 0:256]).then_inc(kb.dve.h, 1)
            kb.on("vector", fev)
            ev_last = kb.dve.inc()
            kb.ps_free[p] = (kb.dve, ev_last)

        res_ms = emit_ln_post_res(
            kb, 256,
            lambda c, e=e: kb.X32[:, c * 1024 + e: c * 1024 + 1024: 4],
            (kb.dve, ev_last))
    return res_ms


def emit_ffn_l1(kb, dr, xn_ms):
    wg, wu, wd = dr["l1_wg"], dr["l1_wu"], dr["l1_wd"]
    res_ms = []
    for g in range(4):
        for ihalf in range(2):
            int_ms = {}
            for f in range(32):
                gi = ihalf * 32 + f
                bufg, wvg = kb.wload(lambda gi=gi: wg[gi])
                pg, freeg = kb.next_ps()

                def fmmg(t, buf=bufg, wv=wvg, p=pg, free=freeg, g=g,
                         xm=xn_ms if (g == 0 and ihalf == 0 and f == 0) else None):
                    if xm:
                        t.wait_ge(kb.dve.h, xm)
                    t.wait_ge(wv[0].h, wv[1])
                    if free is not None:
                        t.wait_ge(free[0].h, free[1])
                    for c in range(NCH):
                        mm = t.matmul(kb.PS[p][:, 0:256], kb.W[buf][:, c * 128:(c + 1) * 128],
                                      kb.A[:, c * 1024 + g * 256: c * 1024 + (g + 1) * 256],
                                      start=(c == 0), stop=(c == NCH - 1))
                        if c == NCH - 1:
                            mm.then_inc(kb.pe.h, 1)
                kb.on("tensor", fmmg)
                msg = kb.pe.inc()
                kb.wconsumed(msg)

                bufu, wvu = kb.wload(lambda gi=gi: wu[gi])
                pu, freeu = kb.next_ps()

                def fmmu(t, buf=bufu, wv=wvu, p=pu, free=freeu, g=g):
                    t.wait_ge(wv[0].h, wv[1])
                    if free is not None:
                        t.wait_ge(free[0].h, free[1])
                    for c in range(NCH):
                        mm = t.matmul(kb.PS[p][:, 0:256], kb.W[buf][:, c * 128:(c + 1) * 128],
                                      kb.A[:, c * 1024 + g * 256: c * 1024 + (g + 1) * 256],
                                      start=(c == 0), stop=(c == NCH - 1))
                        if c == NCH - 1:
                            mm.then_inc(kb.pe.h, 1)
                kb.on("tensor", fmmu)
                msu = kb.pe.inc()
                kb.wconsumed(msu)

                sb_ = f % 2

                def fsil(sc, p=pg, sb_=sb_, gi=gi, msg=msg, w=int_ms.get(f - 2)):
                    sc.wait_ge(kb.pe.h, msg)
                    if w:
                        sc.wait_ge(kb.dve.h, w)
                    sc.activation(kb.STG[:, sb_ * 256:(sb_ + 1) * 256], kb.PS[p][:, 0:256],
                                  AF.Silu, scale=kb.SGT[:, gi:gi + 1]).then_inc(kb.act.h, 1)
                kb.on("scalar", fsil)
                sil_ms = kb.act.inc()
                kb.ps_free[pg] = (kb.act, sil_ms)

                def fint(v, p=pu, sb_=sb_, gi=gi, f=f, msu=msu, sil_ms=sil_ms):
                    v.wait_ge(kb.pe.h, msu)
                    v.wait_ge(kb.act.h, sil_ms)
                    v.scalar_tensor_tensor(kb.B[:, f * 256:(f + 1) * 256], kb.PS[p][:, 0:256],
                                           kb.SUT[:, gi:gi + 1],
                                           kb.STG[:, sb_ * 256:(sb_ + 1) * 256],
                                           op0=ALU.mult, op1=ALU.mult).then_inc(kb.dve.h, 1)
                kb.on("vector", fint)
                mi = kb.dve.inc()
                int_ms[f] = mi
                kb.ps_free[pu] = (kb.dve, mi)
            last_int = kb.dve.n

            ev_last = None
            for hh in range(NCH):
                slabs = [kb.wload(lambda hh=hh, j=j, ihalf=ihalf: wd[hh, ihalf * 2 + j])
                         for j in range(2)]
                p, free = kb.next_ps()

                def fmmd(t, slabs=slabs, p=p, free=free, last_int=last_int):
                    t.wait_ge(kb.dve.h, last_int)
                    if free is not None:
                        t.wait_ge(free[0].h, free[1])
                    for j, (buf, wv) in enumerate(slabs):
                        t.wait_ge(wv[0].h, wv[1])
                        for cc in range(16):
                            f_ = j * 16 + cc
                            mm = t.matmul(kb.PS[p][:, 0:256],
                                          kb.W[buf][:, cc * 128:(cc + 1) * 128],
                                          kb.B[:, f_ * 256:(f_ + 1) * 256],
                                          start=(j == 0 and cc == 0),
                                          stop=(j == 1 and cc == 15))
                            if j == 1 and cc == 15:
                                mm.then_inc(kb.pe.h, 1)
                kb.on("tensor", fmmd)
                md = kb.pe.inc()
                kb.wconsumed(md)
                kb.wconsumed(md)

                def fev(v, p=p, hh=hh, ihalf=ihalf, md=md):
                    v.wait_ge(kb.pe.h, md)
                    dst = kb.SCR[:, hh * 256:(hh + 1) * 256]
                    if ihalf == 0:
                        v.tensor_copy(dst, kb.PS[p][:, 0:256]).then_inc(kb.dve.h, 1)
                    else:
                        v.tensor_add(dst, dst, kb.PS[p][:, 0:256])
                        v.tensor_scalar_mul(dst, dst, kb.SDT[:, hh:hh + 1]).then_inc(kb.dve.h, 1)
                kb.on("vector", fev)
                ev_last = kb.dve.inc()
                kb.ps_free[p] = (kb.dve, ev_last)

        res_ms.append(emit_ln_post_res(
            kb, 256,
            lambda c, g=g: kb.X32[:, c * 1024 + g * 256: c * 1024 + (g + 1) * 256],
            (kb.dve, ev_last)))
    return res_ms


def emit_store(kb, y_d, res_ms):
    tgt = kb.out.inc(16)

    def f(sy, res_ms=res_ms, tgt=tgt):
        sy.wait_ge(kb.dve.h, res_ms)
        sy.dma_start(out=y_d[:, :], in_=kb.X32[:, :]).then_inc(kb.out.h, 16)
        sy.wait_ge(kb.out.h, tgt)
    kb.on("sync", f)


ALL_PHASES = ["l0attn", "l0ffn", "l1attn", "l1ffn"]


def build_program(phases, dbg=()):
    nc = bass.Bass("TRN2", target_bir_lowering=False, debug=False)
    stack = contextlib.ExitStack()
    dr = {}

    def din(name, shape, dtype):
        dr[name] = nc.dram_tensor(name, shape, dtype, kind="ExternalInput").ap()

    din("x", [128, 16 * 1024], F32)
    din("cos0", [128, 1024], BF16)
    din("sin0", [128, 1024], BF16)
    din("tri", [128, 128], BF16)
    din("r0t", [128, 128], BF16)
    din("onec32", [128, 1], F32)
    din("onecbf", [128, 1], BF16)
    din("oner32", [1, 128], F32)
    din("ones128f", [128, 128], F32)
    din("ones128b", [128, 128], BF16)
    din("onerbf", [1, 128], BF16)
    din("biastab0", [128, 64], F32)
    din("biastab1", [128, 64], F32)
    din("cosv1", [128, 16], F32)
    din("sinv1", [128, 16], F32)
    din("consts", [128, 4], F32)
    if "l0attn" in phases:
        for n in ("l0_wq_t", "l0_wk_t", "l0_wo_t"):
            din(n, [16, 128, 2048], F32)
        din("l0_wv_t", [4, 4, 128, 2048], F32)
        for n in ("l0_bq", "l0_bk", "l0_bo"):
            din(n, [128, 16], F32)
        din("l0_bv_flat", [1, 2048], F32)
    if "l0ffn" in phases:
        din("l0_gu_t", [4, 32, 128, 2048], F32)
        din("l0_dw_t", [4, 16, 128, 2048], F32)
    if "l1attn" in phases:
        din("l1_wq_t", [16, 128, 2048], F32)
        din("l1_wk_t", [16, 128, 2048], F32)
        din("l1_wv_t", [4, 4, 128, 2048], F32)
        din("l1_wo_t", [16, 128, 2048], F32)
        din("l1_bqkv", [128, 48], F32)
        din("l1_bout", [128, 16], F32)
        din("l1_bv_flat", [1, 2048], F32)
        din("l1rot", [4, 128, 128], BF16)
    if "l1ffn" in phases:
        din("l1_wg", [64, 128, 2048], I8)
        din("l1_wu", [64, 128, 2048], I8)
        din("l1_wd", [16, 4, 128, 2048], I8)
        din("l1_sg", [128, 64], F32)
        din("l1_su", [128, 64], F32)
        din("l1_sd", [128, 16], F32)
    y_d = nc.dram_tensor("y", [128, 16 * 1024], F32, kind="ExternalOutput").ap()

    with nc.Block() as block:
        kb = Kb(nc, stack, block, dbg)
        kb.phases = phases
        kb.X32 = kb.sb("X32", [128, 16 * 1024], F32)
        kb.A = kb.sb("Apool", [128, 16 * 1024], BF16)
        kb.B = kb.sb("Bpool", [128, 16 * 1024], BF16)
        kb.W = [kb.sb(f"W{i}", [128, 2048], BF16) for i in range(4)]
        kb.SCR = kb.sb("SCR", [128, 4096], F32)
        kb.STG = kb.sb("STG", [128, 4096], BF16)
        kb.KS = [kb.sb(f"KS{i}", [128, 1024], BF16) for i in range(2)]
        kb.VS = [kb.sb(f"VS{i}", [128, 1024], BF16) for i in range(2)]
        kb.PARK2 = kb.sb("PARK2", [128, 2048], BF16)
        kb.PT = [kb.sb(f"PT{i}", [128, 1024], BF16) for i in range(2)]
        kb.APL = kb.sb("APL", [128, 1024], F32)
        kb.LNS = kb.sb("LNS", [128, 2048], F32)
        kb.ROWSBF = kb.sb("ROWSBF", [1, 512], BF16)
        kb.COS0 = kb.sb("COS0", [128, 1024], BF16)
        kb.SIN0 = kb.sb("SIN0", [128, 1024], BF16)
        kb.TRI = kb.sb("TRI", [128, 128], BF16)
        kb.R0T = kb.sb("R0T", [128, 128], BF16)
        kb.ONEC32 = kb.sb("ONEC32", [128, 1], F32)
        kb.ONECBF = kb.sb("ONECBF", [128, 1], BF16)
        kb.ONER32 = kb.sb("ONER32", [1, 128], F32)
        kb.ONES128F = kb.sb("ONES128F", [128, 128], F32)
        kb.ONES128B = kb.sb("ONES128B", [128, 128], BF16)
        kb.ONERBF = kb.sb("ONERBF", [1, 128], BF16)
        kb.BT0 = kb.sb("BT0", [128, 64], F32)
        kb.BT1 = kb.sb("BT1", [128, 64], F32)
        kb.CV1 = kb.sb("CV1", [128, 16], F32)
        kb.SV1 = kb.sb("SV1", [128, 16], F32)
        kb.CONSTS = kb.sb("CONSTS", [128, 4], F32)
        if "l0attn" in phases:
            kb.BQ = kb.sb("BQ", [128, 16], F32)
            kb.BK = kb.sb("BK", [128, 16], F32)
            kb.BO = kb.sb("BO", [128, 16], F32)
        if "l1attn" in phases:
            kb.BQKV = kb.sb("BQKV", [128, 48], F32)
            kb.BOUT = kb.sb("BOUT", [128, 16], F32)
            kb.L1R = [kb.sb(f"L1R{i}", [128, 128], BF16) for i in range(4)]
        if "l1ffn" in phases:
            kb.SGT = kb.sb("SGT", [128, 64], F32)
            kb.SUT = kb.sb("SUT", [128, 64], F32)
            kb.SDT = kb.sb("SDT", [128, 16], F32)
        kb.PS = [kb.psum(f"PS{i}", [128, 512]) for i in range(4)]
        kb.PSW0 = kb.psum("PSW0", [128, 1024])
        kb.PSW1 = kb.psum("PSW1", [128, 1024])

        kb.kin, kb.kout, kb.vin, kb.vout = {}, {}, {}, {}
        for L, ph in ((0, "l0attn"), (1, "l1attn")):
            if ph in phases:
                kb.kin[L] = nc.dram_tensor(f"kin{L}", [2048, 1024], BF16).ap()
                kb.kout[L] = nc.dram_tensor(f"kout{L}", [16384, 1024], BF16).ap()
                kb.vin[L] = nc.dram_tensor(f"vin{L}", [1024, 2048], BF16).ap()
                kb.vout[L] = nc.dram_tensor(f"vout{L}", [8192, 2048], BF16).ap()

        kb.begin()
        emit_load(kb, dr)
        entry = [(kb.ldr, kb.loads_done)]
        res = 1
        for ph in phases:
            if ph in ("l0attn", "l1attn"):
                L = 0 if ph == "l0attn" else 1
                xn_ms = emit_ln_pre(kb, kb.X32, kb.A, entry)
                kb.dump(f"xn{L}", lambda: kb.A[:, :], [128, 16384], BF16,
                        [(kb.dve, xn_ms)])
                raw_ms = emit_kq_proj(kb, dr, L, "k", xn_ms=xn_ms)
                if L == 0:
                    emit_rope_l0(kb, raw_ms, None, True, kb.kin[0])
                    coll_k = emit_gather_one(kb, L, "k")
                    emit_v_proj(kb, dr, L, xn_ms)
                    coll_v = emit_gather_one(kb, L, "v")
                    rawq_ms = emit_kq_proj(kb, dr, L, "q")
                    q_ms = emit_rope_l0(kb, rawq_ms, kb.A, False)
                    q_ready = q_ms[(NCH - 1, 1)]
                else:
                    park = [kb.KS[0][:, :], kb.KS[1][:, :],
                            kb.VS[0][:, :], kb.VS[1][:, :],
                            kb.PARK2[:, 0:1024], kb.PARK2[:, 1024:2048],
                            kb.PT[0][:, :], kb.PT[1][:, :]]
                    km, kloc = emit_rope_l1(kb, raw_ms, False, park)
                    emit_l2_scale(kb, km, kloc, True, kb.kin[1])
                    coll_k = emit_gather_one(kb, L, "k")
                    emit_v_proj(kb, dr, L, xn_ms)
                    coll_v = emit_gather_one(kb, L, "v")
                    rawq_ms = emit_kq_proj(kb, dr, L, "q",
                                           b_free=[(kb.stg_s[0], kb.stg_s[0].n),
                                                   (kb.stg_s[1], kb.stg_s[1].n)])
                    qm, qloc = emit_rope_l1(kb, rawq_ms, True, None)
                    q_ready = emit_l2_scale(kb, qm, qloc, False, None)
                kb.dump(f"q{L}", lambda: kb.A[:, :], [128, 16384], BF16,
                        [(kb.dve, q_ready)])
                ao_ms = emit_attention(kb, L, coll_k, coll_v)
                kb.dump(f"ao{L}", lambda: kb.B[:, :], [128, 16384], BF16,
                        [(kb.dve, ao_ms)])
                res_g = emit_outproj(kb, dr, L, ao_ms)
                res = res_g[-1]
                entry = ([(kb.dve, res_g[1])], [(kb.dve, res_g[3])])
            else:
                xn_ms = emit_ln_pre(kb, kb.X32, kb.A, entry, pack4=(ph == "l0ffn"))
                if ph == "l0ffn":
                    res = emit_ffn_l0(kb, dr, xn_ms)
                    entry = [(kb.dve, res)]
                else:
                    res_g = emit_ffn_l1(kb, dr, xn_ms)
                    res = res_g[-1]
                    entry = ([(kb.dve, res_g[1])], [(kb.dve, res_g[3])])
        emit_store(kb, y_d, res)
        kb.end()
    stack.close()
    return nc, set(dr.keys()), kb.dbg_tensors


def stage_inputs(inputs):
    f32 = np.float32
    shared = {}
    x = np.asarray(inputs["L0_attn_x"][0], f32)

    def chunked_cols(a):
        return np.ascontiguousarray(np.asarray(a, f32).reshape(-1, 128).T)

    def tile16(w):
        return np.ascontiguousarray(
            np.asarray(w, f32).reshape(16, 128, 16, 128).transpose(2, 1, 0, 3)
            .reshape(16, 128, 2048))

    def tile16_T(w):
        return np.ascontiguousarray(
            np.asarray(w, f32).reshape(16, 128, 16, 128).transpose(0, 3, 2, 1)
            .reshape(16, 128, 2048))

    def tile_v(w):
        return np.ascontiguousarray(
            np.asarray(w, f32).reshape(4, 4, 128, 4, 512).transpose(0, 3, 2, 1, 4)
            .reshape(4, 4, 128, 2048))

    def tile_v_T(w):
        return np.ascontiguousarray(
            np.asarray(w, f32).reshape(4, 512, 4, 4, 128).transpose(2, 0, 4, 3, 1)
            .reshape(4, 4, 128, 2048))

    shared["l0_wq_t"] = tile16(inputs["L0_attn_Wq"])
    shared["l0_wk_t"] = tile16(inputs["L0_attn_Wk"])
    shared["l0_wv_t"] = tile_v(inputs["L0_attn_Wv"])
    shared["l0_wo_t"] = tile16(inputs["L0_attn_Wo"])
    shared["l0_bq"] = chunked_cols(inputs["L0_attn_bq"])
    shared["l0_bk"] = chunked_cols(inputs["L0_attn_bk"])
    shared["l0_bo"] = chunked_cols(inputs["L0_attn_bo"])
    shared["l0_bv_flat"] = np.asarray(inputs["L0_attn_bv"], f32).reshape(1, 2048)
    gu = np.asarray(inputs["L0_ffn_gate_up_proj"], f32)
    shared["l0_gu_t"] = np.ascontiguousarray(
        gu.reshape(4, 16, 128, 32, 128).transpose(0, 3, 2, 1, 4).reshape(4, 32, 128, 2048))
    dwn = np.asarray(inputs["L0_ffn_down_proj"], f32)
    shared["l0_dw_t"] = np.ascontiguousarray(
        dwn.reshape(4, 16, 128, 16, 128).transpose(0, 3, 2, 1, 4).reshape(4, 16, 128, 2048))
    wqkv = np.asarray(inputs["L1_attn_Wqkv"], f32)
    shared["l1_wq_t"] = tile16_T(wqkv[0:2048])
    shared["l1_wk_t"] = tile16_T(wqkv[2048:4096])
    shared["l1_wv_t"] = tile_v_T(wqkv[4096:6144])
    bqkv = np.asarray(inputs["L1_attn_bqkv"], f32)
    shared["l1_bqkv"] = chunked_cols(bqkv)
    shared["l1_bv_flat"] = np.ascontiguousarray(bqkv[4096:6144].reshape(1, 2048))
    shared["l1_wo_t"] = tile16_T(inputs["L1_attn_Wout"])
    shared["l1_bout"] = chunked_cols(inputs["L1_attn_bout"])
    wg = np.asarray(inputs["L1_ffn_w_gate"])
    wu = np.asarray(inputs["L1_ffn_w_up"])
    wd = np.asarray(inputs["L1_ffn_w_down"])
    shared["l1_wg"] = np.ascontiguousarray(
        wg.reshape(16, 128, 64, 128).transpose(2, 1, 0, 3).reshape(64, 128, 2048))
    shared["l1_wu"] = np.ascontiguousarray(
        wu.reshape(16, 128, 64, 128).transpose(2, 1, 0, 3).reshape(64, 128, 2048))
    shared["l1_wd"] = np.ascontiguousarray(
        wd.reshape(4, 16, 128, 16, 128).transpose(3, 0, 2, 1, 4).reshape(16, 4, 128, 2048))
    shared["l1_sg"] = chunked_cols(inputs["L1_ffn_s_gate"])
    shared["l1_su"] = chunked_cols(inputs["L1_ffn_s_up"])
    shared["l1_sd"] = chunked_cols(inputs["L1_ffn_s_down"])

    shared["tri"] = np.ascontiguousarray(np.triu(np.ones((128, 128), f32)).astype(BFD))
    r0t = np.zeros((128, 128), f32)
    for i in range(64):
        r0t[2 * i, 2 * i + 1] = 1.0
        r0t[2 * i + 1, 2 * i] = -1.0
    shared["r0t"] = r0t.astype(BFD)
    l1r = np.zeros((4, 128, 128), f32)
    for m in range(64):
        l1r[0, 2 * m + 1, m] = -1.0
        l1r[1, 2 * m + 1, m + 64] = -1.0
        l1r[2, 2 * m, m] = 1.0
        l1r[3, 2 * m, m + 64] = 1.0
    shared["l1rot"] = l1r.astype(BFD)
    shared["onec32"] = np.ones((128, 1), f32)
    shared["onecbf"] = np.ones((128, 1), BFD)
    shared["oner32"] = np.ones((1, 128), f32)
    shared["ones128f"] = np.ones((128, 128), f32)
    shared["ones128b"] = np.ones((128, 128), BFD)
    shared["onerbf"] = np.ones((1, 128), BFD)
    inv = np.power(f32(10000.0), -(np.arange(0, E, 2, dtype=f32) / E))
    emb = np.repeat(inv, 2)
    shared["cosv1"] = chunked_cols(np.cos(f32(S) * emb))
    shared["sinv1"] = chunked_cols(np.sin(f32(S) * emb))
    shared["consts"] = np.tile(np.array([[LN_EPS, L2_EPS, AQBIAS, 0.0]], f32), (128, 1))

    per_core = []
    for c in range(NCORES):
        d = dict(shared)
        tiles = RANK_TILES[c]
        rows = np.concatenate([x[g * 128:(g + 1) * 128] for g in tiles], 0)
        d["x"] = np.ascontiguousarray(
            rows.T.reshape(16, 128, 1024).transpose(1, 0, 2).reshape(128, 16 * 1024))
        pos = np.concatenate([np.arange(g * 128, (g + 1) * 128) for g in tiles]).astype(f32)
        exps = -np.arange(0, D, 2, dtype=f32) / D
        thetas = np.repeat(np.power(f32(10000.0), exps), 2)
        ang = thetas[:, None] * pos[None, :]
        d["cos0"] = np.ascontiguousarray(np.cos(ang).astype(BFD))
        d["sin0"] = np.ascontiguousarray(np.sin(ang).astype(BFD))
        for L, base in ((0, L0_SHIFT), (1, 0.0)):
            bt = np.full((128, 64), KILL, f32)
            for p in range(14):
                bt[:, p] = base if p < 2 * c else KILL
            for p in range(30):
                bt[:, 32 + p] = base if p < 30 - 2 * c else KILL
            bt[:, 62] = base
            d[f"biastab{L}"] = bt
        per_core.append(d)
    return per_core


LAUNCH_PLAN = [ALL_PHASES]


def kernel(**inputs):
    per_core = stage_inputs(inputs)
    for phases in LAUNCH_PLAN:
        nc, innames, _ = build_program(phases)
        in_maps = [{k: v for k, v in d.items() if k in innames} for d in per_core]
        res = run_bass_kernel_spmd(nc, in_maps, core_ids=list(range(NCORES)))
        for c in range(NCORES):
            per_core[c]["x"] = np.ascontiguousarray(res.results[c]["y"])
    out = np.zeros((S, E), np.float32)
    for c in range(NCORES):
        xc = per_core[c]["x"].reshape(128, 16, 1024).transpose(1, 0, 2).reshape(2048, 1024)
        for j, g in enumerate(RANK_TILES[c]):
            out[g * 128:(g + 1) * 128] = xc[:, j * 128:(j + 1) * 128].T
    return out[None]


# version 04 (final)
# speedup vs baseline: 1.1845x; 1.0488x over previous
import contextlib
import os
from collections import defaultdict

import numpy as np
import ml_dtypes
import concourse.bass as bass
import concourse.mybir as mybir
from concourse.bass_utils import run_bass_kernel_spmd

F32 = mybir.dt.float32
BF16 = mybir.dt.bfloat16
I8 = mybir.dt.int8
AF = mybir.ActivationFunctionType
ALU = mybir.AluOpType
BFD = ml_dtypes.bfloat16

ENGINES = ("sync", "gpsimd", "scalar", "vector", "tensor")

E = 2048
S = 8192
H = 16
D = 128
NCH = 16
NCORES = 8
LN_EPS = 1e-5
L2_EPS = 1e-24
L0_SHIFT = -8.0
KILL = -1e30
RSCALE = float(1.0 / np.sqrt(D))
AQBIAS = float(-0.25 * np.log(2048.0))


def core_tiles(c):
    return [4 * c + j for j in range(4)] + [60 - 4 * c + j for j in range(4)]


RANK_TILES = [core_tiles(c) for c in range(NCORES)]


class Sem:
    def __init__(self, nc, stack, name):
        self.h = stack.enter_context(nc.semaphore(name))
        self.n = 0

    def inc(self, k=1):
        self.n += k
        return self.n


class Kb:
    def __init__(self, nc, stack, block, dbg=()):
        self.nc = nc
        self.stack = stack
        self.block = block
        self.dbg = set(dbg)
        self.dbg_tensors = {}
        self.pe = Sem(nc, stack, "s_pe")
        self.act = Sem(nc, stack, "s_act")
        self.dve = Sem(nc, stack, "s_dve")
        self.wdma = Sem(nc, stack, "s_wdma")
        self.ldr = Sem(nc, stack, "s_ldr")
        self.stg = Sem(nc, stack, "s_stg")
        self.coll = Sem(nc, stack, "s_coll")
        self.out = Sem(nc, stack, "s_out")
        self.w_s = [Sem(nc, stack, f"s_w{i}") for i in range(4)]
        self.ks_s = [Sem(nc, stack, f"s_ks{i}") for i in range(2)]
        self.vs_s = [Sem(nc, stack, f"s_vs{i}") for i in range(2)]
        self.stg_s = [Sem(nc, stack, f"s_stg{i}") for i in range(2)]
        self.rb_s = Sem(nc, stack, "s_rb")
        self.wslab = 0
        self.wfree = []
        self.ops = None
        self.ps_free = [None] * 4
        self.ps_rot = 0
        self.psw0_free = None
        self.psw1_free = None
        self.half_gate = None

    def sb(self, name, shape, dtype):
        return self.stack.enter_context(self.nc.sbuf_tensor(name, shape, dtype))

    def psum(self, name, shape, dtype=F32):
        return self.stack.enter_context(self.nc.psum_tensor(name, shape, dtype))

    def begin(self):
        assert self.ops is None
        self.ops = defaultdict(list)

    def end(self):
        ops = self.ops
        self.ops = None
        for name in ENGINES:
            lst = ops.get(name)
            if lst:
                def run(eng, lst=lst):
                    for f in lst:
                        f(eng)
                getattr(self.block, name)(run)

    def on(self, engine, f):
        self.ops[engine].append(f)

    def wload(self, src_ap_fn, nsub=None):
        k = self.wslab
        self.wslab += 1
        buf = k % 4
        free_ms = self.wfree[k - 4] if k >= 4 else None
        sem = self.w_s[buf]
        wv = sem.inc(16)

        def f(g, buf=buf, free_ms=free_ms, src_ap_fn=src_ap_fn, nsub=nsub, sem=sem):
            if free_ms:
                g.wait_ge(self.pe.h, free_ms)
            dst = self.W[buf][:, :]
            if nsub:
                dst = dst.rearrange("p (c f) -> p c f", c=nsub)
            g.dma_start(out=dst, in_=src_ap_fn()).then_inc(sem.h, 16)
        self.on("gpsimd", f)
        return buf, (sem, wv)

    def wconsumed(self, pe_ms):
        self.wfree.append(pe_ms)

    def next_ps(self):
        i = self.ps_rot % 4
        self.ps_rot += 1
        return i, self.ps_free[i]

    def dump(self, tag, ap_fn, shape, dtype, waits):
        if tag not in self.dbg:
            return
        t = self.nc.dram_tensor("dbg_" + tag, shape, dtype, kind="ExternalOutput").ap()
        self.dbg_tensors[tag] = t
        tgt = self.out.inc(16)

        def f(sy, t=t, ap_fn=ap_fn, waits=list(waits)):
            for s, v in waits:
                sy.wait_ge(s.h, v)
            sy.dma_start(out=t[:, :], in_=ap_fn()).then_inc(self.out.h, 16)
        self.on("sync", f)


def emit_load(kb, dr):
    names = [("X32", "x"), ("COS0", "cos0"), ("SIN0", "sin0"),
             ("TRI", "tri"), ("R0T", "r0t"),
             ("ONEC32", "onec32"), ("ONECBF", "onecbf"),
             ("ONER32", "oner32"), ("ONERBF", "onerbf"),
             ("ONES128F", "ones128f"), ("ONES128B", "ones128b"),
             ("BT0", "biastab0"), ("BT1", "biastab1"),
             ("CV1", "cosv1"), ("SV1", "sinv1"), ("CONSTS", "consts")]
    small = {"l0attn": [("BQ", "l0_bq"), ("BK", "l0_bk"), ("BO", "l0_bo")],
             "l1attn": [("BQKV", "l1_bqkv"), ("BOUT", "l1_bout")],
             "l1ffn": [("SGT", "l1_sg"), ("SUT", "l1_su"), ("SDT", "l1_sd")]}
    for ph in kb.phases:
        names += small.get(ph, [])
    n_dma = len(names) + (4 if "l1attn" in kb.phases else 0)

    def f(sy, names=list(names)):
        for attr, key in names:
            sy.dma_start(out=getattr(kb, attr)[:, :], in_=dr[key][:, :]).then_inc(kb.ldr.h, 16)
        if "l1attn" in kb.phases:
            for i in range(4):
                sy.dma_start(out=kb.L1R[i][:, :], in_=dr["l1rot"][i]).then_inc(kb.ldr.h, 16)
    kb.on("sync", f)
    kb.loads_done = kb.ldr.inc(16 * n_dma)


def emit_ln_pre(kb, src, dst, entry, pack4=False):
    if isinstance(entry, tuple):
        entries = entry
    else:
        entries = (entry, entry)
    apply_ms = None
    for half in range(2):
        entry = entries[half]
        sq_ms = {}
        sumsq_ms = {}
        for c in range(NCH):
            slot = (c % 2) * 512

            def fsq(sc, c=c, slot=slot, half=half,
                    ent=list(entry) if c == 0 else None,
                    w=sumsq_ms.get(c - 2),
                    hg=kb.half_gate if c == 0 else None):
                if ent:
                    for s, v in ent:
                        sc.wait_ge(s.h, v)
                if hg:
                    sc.wait_ge(hg[0].h, hg[1])
                if w:
                    sc.wait_ge(kb.pe.h, w)
                sc.activation(kb.LNS[:, 1024 + slot:1024 + slot + 512],
                              src[:, c * 1024 + half * 512: c * 1024 + half * 512 + 512],
                              AF.Square).then_inc(kb.act.h, 1)
            kb.on("scalar", fsq)
            sq_ms[c] = kb.act.inc()

            def fmm(t, c=c, slot=slot, half=half, sqv=sq_ms[c],
                    ent=list(entry) if c == 0 else None,
                    pw=kb.psw0_free if c == 0 else None):
                if ent:
                    for s, v in ent:
                        t.wait_ge(s.h, v)
                    t.wait_ge(kb.ldr.h, kb.loads_done)
                if pw:
                    t.wait_ge(pw[0].h, pw[1])
                t.matmul(kb.PSW0[:, 0:512], kb.ONES128F[:, :],
                         src[:, c * 1024 + half * 512: c * 1024 + half * 512 + 512],
                         start=(c == 0), stop=(c == NCH - 1))
                t.wait_ge(kb.act.h, sqv)
                t.matmul(kb.PSW0[:, 512:1024], kb.ONES128F[:, :],
                         kb.LNS[:, 1024 + slot:1024 + slot + 512],
                         start=(c == 0), stop=(c == NCH - 1)).then_inc(kb.pe.h, 1)
            kb.on("tensor", fmm)
            sumsq_ms[c] = kb.pe.inc()
        stats_ms = sumsq_ms[NCH - 1]

        def frows(v, stats_ms=stats_ms):
            v.wait_ge(kb.pe.h, stats_ms)
            v.tensor_scalar_mul(kb.LNS[:, 0:512], kb.PSW0[:, 0:512], 1.0 / E)
            v.tensor_mul(kb.LNS[:, 1024:1536], kb.LNS[:, 0:512], kb.LNS[:, 0:512])
            v.scalar_tensor_tensor(kb.LNS[:, 512:1024], kb.PSW0[:, 512:1024], 1.0 / E,
                                   kb.LNS[:, 1024:1536],
                                   op0=ALU.mult, op1=ALU.subtract).then_inc(kb.dve.h, 1)
        kb.on("vector", frows)
        rows1_ms = kb.dve.inc()
        kb.psw0_free = (kb.dve, rows1_ms)
        kb.half_gate = (kb.dve, rows1_ms)

        def facts(sc, rows1_ms=rows1_ms):
            sc.wait_ge(kb.dve.h, rows1_ms)
            sc.activation(kb.LNS[:, 1024:1536], kb.LNS[:, 512:1024], AF.Ln,
                          bias=kb.CONSTS[:, 0:1])
            sc.activation(kb.LNS[:, 512:1024], kb.LNS[:, 1024:1536], AF.Exp,
                          scale=-0.5).then_inc(kb.act.h, 1)
        kb.on("scalar", facts)
        rstd_ms = kb.act.inc()

        def fc2(v, rstd_ms=rstd_ms):
            v.wait_ge(kb.act.h, rstd_ms)
            v.scalar_tensor_tensor(kb.LNS[:, 0:512], kb.LNS[:, 0:512], -1.0,
                                   kb.LNS[:, 512:1024],
                                   op0=ALU.mult, op1=ALU.mult).then_inc(kb.dve.h, 1)
        kb.on("vector", fc2)
        c2_ms = kb.dve.inc()

        for c in range(NCH):
            def fap(v, c=c, half=half):
                xs = src[:, c * 1024 + half * 512: c * 1024 + half * 512 + 512]
                if pack4:
                    out = dst[:, c * 1024:(c + 1) * 1024].rearrange(
                        "p (e g) -> p g e", e=4)[:, half * 128:(half + 1) * 128, :]
                    v.tensor_mul(out, xs.rearrange("p (g e) -> p g e", e=4),
                                 kb.LNS[:, 512:1024].rearrange("p (g e) -> p g e", e=4))
                    mm = v.tensor_add(out, out,
                                      kb.LNS[:, 0:512].rearrange("p (g e) -> p g e", e=4))
                else:
                    out = dst[:, c * 1024 + half * 512: c * 1024 + half * 512 + 512]
                    v.tensor_mul(out, xs, kb.LNS[:, 512:1024])
                    mm = v.tensor_add(out, out, kb.LNS[:, 0:512])
                if c == NCH - 1:
                    mm.then_inc(kb.dve.h, 1)
            kb.on("vector", fap)
        apply_ms = kb.dve.inc()
    return apply_ms


def emit_ln_post_res(kb, ntok, dst_ap_fn, pe_ready):
    mm_ms = {}
    for c in range(NCH):
        slot = 512 + (c % 2) * 256

        def fsq(sc, c=c, slot=slot, w=mm_ms.get(c - 2),
                pr=pe_ready if c == 0 else None):
            if pr:
                sc.wait_ge(pr[0].h, pr[1])
            if w:
                sc.wait_ge(kb.pe.h, w)
            sc.activation(kb.APL[:, slot:slot + 256],
                          kb.SCR[:, c * ntok: c * ntok + 256],
                          AF.Square).then_inc(kb.act.h, 1)
        kb.on("scalar", fsq)
        sqv = kb.act.inc()

        def fmm(t, c=c, slot=slot, sqv=sqv, pr=pe_ready if c == 0 else None,
                pw=kb.psw1_free if c == 0 else None):
            if pr:
                t.wait_ge(pr[0].h, pr[1])
            if pw:
                t.wait_ge(pw[0].h, pw[1])
            t.matmul(kb.PSW1[:, 0:256], kb.ONES128F[:, :],
                     kb.SCR[:, c * ntok: c * ntok + 256],
                     start=(c == 0), stop=(c == NCH - 1))
            t.wait_ge(kb.act.h, sqv)
            t.matmul(kb.PSW1[:, 512:768], kb.ONES128F[:, :],
                     kb.APL[:, slot:slot + 256],
                     start=(c == 0), stop=(c == NCH - 1)).then_inc(kb.pe.h, 1)
        kb.on("tensor", fmm)
        mm_ms[c] = kb.pe.inc()
    stats_ms = mm_ms[NCH - 1]

    def frows(v, stats_ms=stats_ms):
        v.wait_ge(kb.pe.h, stats_ms)
        v.tensor_scalar_mul(kb.APL[:, 0:256], kb.PSW1[:, 0:256], 1.0 / E)
        v.tensor_mul(kb.APL[:, 256:512], kb.APL[:, 0:256], kb.APL[:, 0:256])
        v.scalar_tensor_tensor(kb.APL[:, 256:512], kb.PSW1[:, 512:768], 1.0 / E,
                               kb.APL[:, 256:512],
                               op0=ALU.mult, op1=ALU.subtract).then_inc(kb.dve.h, 1)
    kb.on("vector", frows)
    rows1_ms = kb.dve.inc()
    kb.psw1_free = (kb.dve, rows1_ms)

    def facts(sc, rows1_ms=rows1_ms):
        sc.wait_ge(kb.dve.h, rows1_ms)
        sc.activation(kb.APL[:, 512:768], kb.APL[:, 256:512], AF.Ln,
                      bias=kb.CONSTS[:, 0:1])
        sc.activation(kb.APL[:, 256:512], kb.APL[:, 512:768], AF.Exp,
                      scale=-0.5).then_inc(kb.act.h, 1)
    kb.on("scalar", facts)
    rstd_ms = kb.act.inc()

    def fc2(v, rstd_ms=rstd_ms):
        v.wait_ge(kb.act.h, rstd_ms)
        v.scalar_tensor_tensor(kb.APL[:, 0:256], kb.APL[:, 0:256], -1.0,
                               kb.APL[:, 256:512],
                               op0=ALU.mult, op1=ALU.mult).then_inc(kb.dve.h, 1)
    kb.on("vector", fc2)
    c2_ms = kb.dve.inc()

    for c in range(NCH):
        def fap(v, c=c):
            s_ = kb.SCR[:, c * ntok: c * ntok + 256]
            v.tensor_mul(kb.APL[:, 512:768], s_, kb.APL[:, 256:512])
            v.tensor_add(kb.APL[:, 768:1024], kb.APL[:, 512:768], kb.APL[:, 0:256])
            dst = dst_ap_fn(c)
            mm = v.tensor_add(dst, dst, kb.APL[:, 768:1024])
            if c == NCH - 1:
                mm.then_inc(kb.dve.h, 1)
        kb.on("vector", fap)
    apply_ms = kb.dve.inc()
    return apply_ms


def emit_v_proj(kb, dr, L, xn_ms):
    wsrc = dr["l0_wv_t"] if L == 0 else dr["l1_wv_t"]
    bvflat = dr["l0_bv_flat"] if L == 0 else dr["l1_bv_flat"]
    vin = kb.vin[L]
    evac_ms_prev = None
    prev_mm = None
    for idx in range(8):
        half_t, foB = idx // 4, idx % 4
        rb = 0

        def fbias(g, rb=rb, foB=foB, pm=prev_mm):
            if pm:
                g.wait_ge(kb.pe.h, pm)
            g.dma_start(out=kb.ROWSBF[0:1, rb:rb + 512],
                        in_=bvflat[0:1, foB * 512:(foB + 1) * 512]).then_inc(kb.rb_s.h, 16)
        kb.on("gpsimd", fbias)
        bias_v = kb.rb_s.inc(16)

        slabs = [kb.wload(lambda cq=cq, foB=foB: wsrc[cq, foB]) for cq in range(4)]

        def fmm(t, slabs=slabs, half_t=half_t, foB=foB, bias_v=bias_v, rb=rb,
                xm=xn_ms if idx == 0 else None, ev_prev=evac_ms_prev,
                psf=list(kb.ps_free) if idx == 0 else None):
            if xm:
                t.wait_ge(kb.dve.h, xm)
            if psf:
                for w in psf:
                    if w:
                        t.wait_ge(w[0].h, w[1])
            if ev_prev:
                t.wait_ge(kb.dve.h, ev_prev)
            for cq, (buf, wv) in enumerate(slabs):
                t.wait_ge(wv[0].h, wv[1])
                for c4 in range(4):
                    for tt in range(4):
                        t.matmul(kb.PS[tt][:, :],
                                 kb.A[:, (4 * cq + c4) * 1024 + (half_t * 4 + tt) * 128:
                                      (4 * cq + c4) * 1024 + (half_t * 4 + tt) * 128 + 128],
                                 kb.W[buf][:, c4 * 512:(c4 + 1) * 512],
                                 start=(cq == 0 and c4 == 0), stop=False)
            t.wait_ge(kb.rb_s.h, bias_v)
            for tt in range(4):
                mm = t.matmul(kb.PS[tt][:, :], kb.ONERBF[0:1, :],
                              kb.ROWSBF[0:1, rb:rb + 512], start=False, stop=True)
                if tt == 3:
                    mm.then_inc(kb.pe.h, 1)
        kb.on("tensor", fmm)
        mm_ms = kb.pe.inc()
        for _ in range(4):
            kb.wconsumed(mm_ms)

        sp = idx % 2
        st_w = kb.stg_s[sp].n

        def fev(v, mm_ms=mm_ms, sp=sp, st_w=st_w):
            v.wait_ge(kb.pe.h, mm_ms)
            if st_w:
                v.wait_ge(kb.stg_s[sp].h, st_w)
            for tt in range(4):
                mm = v.tensor_copy(kb.STG[:, sp * 2048 + tt * 512: sp * 2048 + (tt + 1) * 512],
                                   kb.PS[tt][:, :])
                if tt == 3:
                    mm.then_inc(kb.dve.h, 1)
        kb.on("vector", fev)
        evac_ms = kb.dve.inc()
        evac_ms_prev = evac_ms
        prev_mm = mm_ms

        def fst(sc, evac_ms=evac_ms, half_t=half_t, foB=foB, sp=sp):
            sc.wait_ge(kb.dve.h, evac_ms)
            for tt in range(4):
                sc.dma_start(
                    out=vin[half_t * 512 + tt * 128: half_t * 512 + (tt + 1) * 128,
                            foB * 512:(foB + 1) * 512],
                    in_=kb.STG[:, sp * 2048 + tt * 512: sp * 2048 + (tt + 1) * 512]
                ).then_inc(kb.stg_s[sp].h, 16)
        kb.on("scalar", fst)
        kb.stg_s[sp].inc(64)
    for i in range(4):
        kb.ps_free[i] = (kb.dve, evac_ms_prev)


def emit_kq_proj(kb, dr, L, which, b_free=None, xn_ms=None):
    if L == 0:
        wsrc = dr["l0_wq_t"] if which == "q" else dr["l0_wk_t"]
        bias_t, bias_off = ("BQ" if which == "q" else "BK"), 0
    else:
        wsrc = dr["l1_wq_t"] if which == "q" else dr["l1_wk_t"]
        bias_t, bias_off = "BQKV", (0 if which == "q" else 16)
    bias = getattr(kb, bias_t)
    ev_ms = {}
    for fo in range(NCH):
        buf, wv = kb.wload(lambda fo=fo: wsrc[fo])
        for half in range(2):
            p, free = kb.next_ps()

            def fmm(t, buf=buf, wv=wv, p=p, free=free, fo=fo, half=half,
                    xm=xn_ms if (fo == 0 and half == 0) else None):
                if xm:
                    t.wait_ge(kb.dve.h, xm)
                t.wait_ge(wv[0].h, wv[1])
                if free is not None:
                    t.wait_ge(free[0].h, free[1])
                for c in range(NCH):
                    mm = t.matmul(kb.PS[p][:, :], kb.W[buf][:, c * 128:(c + 1) * 128],
                                  kb.A[:, c * 1024 + half * 512: c * 1024 + half * 512 + 512],
                                  start=(c == 0), stop=(c == NCH - 1))
                    if c == NCH - 1:
                        mm.then_inc(kb.pe.h, 1)
            kb.on("tensor", fmm)
            mm_ms = kb.pe.inc()
            if half == 1:
                kb.wconsumed(mm_ms)

            def fev(v, p=p, fo=fo, half=half, mm_ms=mm_ms,
                    bf=b_free if (fo == 0 and half == 0) else None):
                v.wait_ge(kb.pe.h, mm_ms)
                if bf:
                    for s, vv in bf:
                        v.wait_ge(s.h, vv)
                v.tensor_scalar_add(
                    kb.B[:, fo * 1024 + half * 512: fo * 1024 + half * 512 + 512],
                    kb.PS[p][:, :],
                    bias[:, bias_off + fo: bias_off + fo + 1]).then_inc(kb.dve.h, 1)
            kb.on("vector", fev)
            ev = kb.dve.inc()
            ev_ms[(fo, half)] = ev
            kb.ps_free[p] = (kb.dve, ev)
    return ev_ms


def emit_rope_l0(kb, raw_ms, dst, stage, kdma=None):
    out_ms = {}
    st_prev = {}
    for fo in range(NCH):
        sp = fo % 2
        for half in range(2):
            p, free = kb.next_ps()

            def frot(t, p=p, free=free, fo=fo, half=half, rms=raw_ms[(fo, half)]):
                t.wait_ge(kb.dve.h, rms)
                if free is not None:
                    t.wait_ge(free[0].h, free[1])
                t.matmul(kb.PS[p][:, :], kb.R0T[:, :],
                         kb.B[:, fo * 1024 + half * 512: fo * 1024 + half * 512 + 512],
                         start=True, stop=True).then_inc(kb.pe.h, 1)
            kb.on("tensor", frot)
            rot_ms = kb.pe.inc()

            def fcomb(v, p=p, fo=fo, half=half, rot_ms=rot_ms, sp=sp,
                      stw=st_prev.get(fo - 2)):
                v.wait_ge(kb.pe.h, rot_ms)
                if stw:
                    v.wait_ge(kb.stg_s[sp].h, stw)
                hs = half * 512
                v.tensor_mul(kb.APL[:, 0:512], kb.PS[p][:, :], kb.SIN0[:, hs:hs + 512])
                v.tensor_mul(kb.APL[:, 512:1024],
                             kb.B[:, fo * 1024 + hs: fo * 1024 + hs + 512],
                             kb.COS0[:, hs:hs + 512])
                if stage:
                    tgt = kb.STG[:, sp * 2048 + hs: sp * 2048 + hs + 512]
                else:
                    tgt = dst[:, fo * 1024 + hs: fo * 1024 + hs + 512]
                v.tensor_add(tgt, kb.APL[:, 0:512], kb.APL[:, 512:1024]).then_inc(kb.dve.h, 1)
            kb.on("vector", fcomb)
            cm = kb.dve.inc()
            out_ms[(fo, half)] = cm
            kb.ps_free[p] = (kb.dve, cm)
        if stage:
            def fst(sc, fo=fo, sp=sp, cm=out_ms[(fo, 1)]):
                sc.wait_ge(kb.dve.h, cm)
                sc.dma_start(out=kdma[fo * 128:(fo + 1) * 128, :],
                             in_=kb.STG[:, sp * 2048: sp * 2048 + 1024]).then_inc(kb.stg_s[sp].h, 16)
            kb.on("scalar", fst)
            st_prev[fo] = kb.stg_s[sp].inc(16)
    return out_ms


def l1_sources(fo):
    if fo < 8:
        return 2 * fo, 2 * fo + 1, 0, 1
    return 2 * (fo - 8), 2 * (fo - 8) + 1, 2, 3


def emit_rope_l1(kb, raw_ms, to_a, park_aps):
    out_ms = {}
    loc = {}
    for fo in range(NCH):
        s0, s1, p0, p1 = l1_sources(fo)
        p, free = kb.next_ps()
        qb = fo % 2

        def frot(t, p=p, free=free, s0=s0, s1=s1, p0=p0, p1=p1,
                 w=max(raw_ms[(s0, 1)], raw_ms[(s1, 1)])):
            t.wait_ge(kb.dve.h, w)
            if free is not None:
                t.wait_ge(free[0].h, free[1])
            t.matmul(kb.PS[p][:, :], kb.L1R[p0][:, :],
                     kb.B[:, s0 * 1024: s0 * 1024 + 512], start=True, stop=False)
            t.matmul(kb.PS[p][:, :], kb.L1R[p1][:, :],
                     kb.B[:, s1 * 1024: s1 * 1024 + 512], start=False, stop=True).then_inc(kb.pe.h, 1)
        kb.on("tensor", frot)
        rot1 = kb.pe.inc()
        p2, free2 = kb.next_ps()

        def frot2(t, p=p2, free=free2, s0=s0, s1=s1, p0=p0, p1=p1):
            if free is not None:
                t.wait_ge(free[0].h, free[1])
            t.matmul(kb.PS[p][:, :], kb.L1R[p0][:, :],
                     kb.B[:, s0 * 1024 + 512: (s0 + 1) * 1024], start=True, stop=False)
            t.matmul(kb.PS[p][:, :], kb.L1R[p1][:, :],
                     kb.B[:, s1 * 1024 + 512: (s1 + 1) * 1024],
                     start=False, stop=True).then_inc(kb.pe.h, 1)
        kb.on("tensor", frot2)
        rot2 = kb.pe.inc()

        def fqcos(sc, fo=fo, qb=qb, rm=raw_ms[(fo, 1)]):
            sc.wait_ge(kb.dve.h, rm)
            sc.activation(kb.STG[:, qb * 1024:(qb + 1) * 1024],
                          kb.B[:, fo * 1024:(fo + 1) * 1024],
                          AF.Copy, scale=kb.CV1[:, fo:fo + 1]).then_inc(kb.act.h, 1)
        kb.on("scalar", fqcos)
        qc_ms = kb.act.inc()

        if to_a:
            tgt = kb.A[:, fo * 1024:(fo + 1) * 1024]
        elif fo < 8:
            tgt = park_aps[fo]
        else:
            tgt = kb.B[:, 2 * (fo - 8) * 1024: (2 * (fo - 8) + 1) * 1024]
        loc[fo] = tgt

        def fcomb(v, p=p, p2=p2, fo=fo, qb=qb, rot2=rot2, qc_ms=qc_ms, tgt=tgt):
            v.wait_ge(kb.pe.h, rot2)
            v.wait_ge(kb.act.h, qc_ms)
            v.scalar_tensor_tensor(tgt[:, 0:512], kb.PS[p][:, :], kb.SV1[:, fo:fo + 1],
                                   kb.STG[:, qb * 1024: qb * 1024 + 512],
                                   op0=ALU.mult, op1=ALU.add)
            v.scalar_tensor_tensor(tgt[:, 512:1024], kb.PS[p2][:, :], kb.SV1[:, fo:fo + 1],
                                   kb.STG[:, qb * 1024 + 512: (qb + 1) * 1024],
                                   op0=ALU.mult, op1=ALU.add).then_inc(kb.dve.h, 1)
        kb.on("vector", fcomb)
        cm = kb.dve.inc()
        out_ms[fo] = cm
        kb.ps_free[p] = (kb.dve, cm)
        kb.ps_free[p2] = (kb.dve, cm)
    return out_ms, loc


def emit_l2_scale(kb, out_ms, loc, stage_dma, kdma):
    sq_pe = {}
    for fo in range(NCH):
        qb = 2 + (fo % 2)

        def fsq(sc, fo=fo, qb=qb, cm=out_ms[fo], w=sq_pe.get(fo - 2)):
            sc.wait_ge(kb.dve.h, cm)
            if w:
                sc.wait_ge(kb.pe.h, w)
            sc.activation(kb.STG[:, qb * 1024:(qb + 1) * 1024], loc[fo],
                          AF.Square).then_inc(kb.act.h, 1)
        kb.on("scalar", fsq)
        sqv = kb.act.inc()

        def fmm(t, fo=fo, qb=qb, sqv=sqv, pw=kb.psw1_free if fo == 0 else None):
            if pw:
                t.wait_ge(pw[0].h, pw[1])
            t.wait_ge(kb.act.h, sqv)
            t.matmul(kb.PSW1[:, 0:512], kb.ONES128B[:, :],
                     kb.STG[:, qb * 1024: qb * 1024 + 512],
                     start=(fo == 0), stop=(fo == NCH - 1))
            t.matmul(kb.PSW1[:, 512:1024], kb.ONES128B[:, :],
                     kb.STG[:, qb * 1024 + 512: (qb + 1) * 1024],
                     start=(fo == 0), stop=(fo == NCH - 1)).then_inc(kb.pe.h, 1)
        kb.on("tensor", fmm)
        sq_pe[fo] = kb.pe.inc()
    ss_ms = sq_pe[NCH - 1]

    def facts(sc, ss_ms=ss_ms):
        sc.wait_ge(kb.pe.h, ss_ms)
        sc.activation(kb.SCR[:, 0:512], kb.PSW1[:, 0:512], AF.Ln,
                      bias=kb.CONSTS[:, 1:2])
        sc.activation(kb.SCR[:, 1024:1536], kb.SCR[:, 0:512], AF.Exp,
                      scale=-0.5, bias=kb.CONSTS[:, 2:3])
        sc.activation(kb.SCR[:, 512:1024], kb.PSW1[:, 512:1024], AF.Ln,
                      bias=kb.CONSTS[:, 1:2])
        sc.activation(kb.SCR[:, 1536:2048], kb.SCR[:, 512:1024], AF.Exp,
                      scale=-0.5, bias=kb.CONSTS[:, 2:3]).then_inc(kb.act.h, 1)
    kb.on("scalar", facts)
    a_ms = kb.act.inc()
    kb.psw1_free = (kb.act, a_ms)

    last = None
    for fo in range(NCH):
        def fsc(v, fo=fo, a_ms=a_ms):
            if fo == 0:
                v.wait_ge(kb.act.h, a_ms)
            t = loc[fo]
            v.tensor_mul(t[:, 0:512], t[:, 0:512], kb.SCR[:, 1024:1536])
            v.tensor_mul(t[:, 512:1024], t[:, 512:1024],
                         kb.SCR[:, 1536:2048]).then_inc(kb.dve.h, 1)
        kb.on("vector", fsc)
        last = kb.dve.inc()
        if stage_dma:
            def fst(sc, fo=fo, last=last):
                sc.wait_ge(kb.dve.h, last)
                sc.dma_start(out=kdma[fo * 128:(fo + 1) * 128, :],
                             in_=loc[fo]).then_inc(kb.stg_s[fo % 2].h, 16)
            kb.on("scalar", fst)
            kb.stg_s[fo % 2].inc(16)
    return last


def emit_gather_one(kb, L, which):
    stg_v = (kb.stg_s[0].n, kb.stg_s[1].n)
    src_ap = kb.kin[L] if which == "k" else kb.vin[L]
    dst_ap = kb.kout[L] if which == "k" else kb.vout[L]

    def f(g, stg_v=stg_v, src_ap=src_ap, dst_ap=dst_ap):
        if stg_v[0]:
            g.wait_ge(kb.stg_s[0].h, stg_v[0])
        if stg_v[1]:
            g.wait_ge(kb.stg_s[1].h, stg_v[1])
        g.collective_compute("AllGather", ALU.bypass,
                             replica_groups=[list(range(NCORES))],
                             ins=[src_ap[:, :]], outs=[dst_ap[:, :]]).then_inc(kb.coll.h, 1)
    kb.on("gpsimd", f)
    return kb.coll.inc(1)


def build_head_pairs():
    pairs = []
    for r in range(8):
        tiles = RANK_TILES[r]
        a_items = [(loc, 0, tiles[loc]) for loc in range(4) if tiles[loc] < 28]
        b_items = [(loc, 1, tiles[loc]) for loc in range(8) if tiles[loc] < 60]
        for i in range(0, len(a_items), 2):
            pairs.append({"kind": "past", "r": r, "items": a_items[i:i + 2]})
        for i in range(0, len(b_items), 2):
            pairs.append({"kind": "past", "r": r, "items": b_items[i:i + 2]})
    for rho in range(4):
        pairs.append({"kind": "diag", "rho": rho})
    return pairs


HEAD_PAIRS = build_head_pairs()


def select_pairs():
    sel = os.environ.get("K_PAIRSEL", "all")
    if sel == "all":
        return HEAD_PAIRS
    if sel == "diag":
        return [p for p in HEAD_PAIRS if p["kind"] == "diag"]
    if sel.startswith("first:"):
        return HEAD_PAIRS[:int(sel.split(":")[1])] +             [p for p in HEAD_PAIRS if p["kind"] == "diag"]
    if sel.startswith("rank:"):
        rs = set(int(x) for x in sel.split(":")[1].split("-"))
        return [p for p in HEAD_PAIRS if p["kind"] == "diag" or p["r"] in rs]
    if sel == "grpA":
        return [p for p in HEAD_PAIRS if p["kind"] == "diag"
                or p["items"][0][1] == 0]
    if sel == "grpB":
        return [p for p in HEAD_PAIRS if p["kind"] == "diag"
                or p["items"][0][1] == 1]
    raise ValueError(sel)


SEL_PAIRS = select_pairs()
NPAIRS = len(SEL_PAIRS)


def emit_attention(kb, L, coll_k, coll_v):
    scale = RSCALE if L == 0 else 1.0
    biasc = L0_SHIFT if L == 0 else 0.0
    BT = kb.BT0 if L == 0 else kb.BT1
    kout, vout = kb.kout[L], kb.vout[L]
    kin, vin = kb.kin[L], kb.vin[L]

    ks_last = {}
    vs_last = {}
    ks_lds = {}
    vs_lds = {}
    ks_owner = {}
    vs_owner = {}
    epi_dve = {0: kb.ps_free[0], 1: kb.ps_free[1]}
    epi_cp = {0: kb.ps_free[2], 1: kb.ps_free[3]}
    epi_prev_mul = {0: None, 1: None}

    def ensure_k(h, r):
        key = h * 9 + r
        if key in ks_lds:
            return
        kbuf = key % 2
        prev = ks_owner.get(kbuf)
        ks_owner[kbuf] = key

        def fk(sy, r=r, h=h, kbuf=kbuf, w=ks_last.get(prev), ec=coll_k):
            sy.wait_ge(kb.coll.h, ec)
            if w:
                sy.wait_ge(kb.pe.h, w)
            if r < 8:
                src = kout[r * 2048 + h * 128: r * 2048 + (h + 1) * 128, :]
            else:
                src = kin[h * 128:(h + 1) * 128, :]
            sy.dma_start(out=kb.KS[kbuf][:, :], in_=src).then_inc(kb.ks_s[kbuf].h, 16)
        kb.on("sync", fk)
        ks_lds[key] = (kb.ks_s[kbuf], kb.ks_s[kbuf].inc(16))

    def ensure_v(h, r):
        key = h * 9 + r
        if key in vs_lds:
            return
        vbuf = key % 2
        prev = vs_owner.get(vbuf)
        vs_owner[vbuf] = key

        def fv(sy, r=r, h=h, vbuf=vbuf, w=vs_last.get(prev), ec=coll_v):
            sy.wait_ge(kb.coll.h, ec)
            if w:
                sy.wait_ge(kb.pe.h, w)
            if r < 8:
                src = vout[r * 1024:(r + 1) * 1024,
                           h * 128:(h + 1) * 128].rearrange("(j p) d -> p j d", p=128)
            else:
                src = vin[:, h * 128:(h + 1) * 128].rearrange("(j p) d -> p j d", p=128)
            sy.dma_start(out=kb.VS[vbuf][:, :].rearrange("p (j d) -> p j d", j=8),
                         in_=src).then_inc(kb.vs_s[vbuf].h, 16)
        kb.on("sync", fv)
        vs_lds[key] = (kb.vs_s[vbuf], kb.vs_s[vbuf].inc(16))

    for h in range(H):
        hp = h // 2
        exp_ms = {}
        qk_ms = {}
        pv_ms = {}
        mask_ms = {}
        pv_started = {0: False, 1: False}

        def pair_rank(pi):
            return SEL_PAIRS[pi].get("r", 8)

        def pair_items(pi):
            pr = SEL_PAIRS[pi]
            if pr["kind"] == "past":
                return [(it[1], it[2], it[0], 0) for it in pr["items"]]
            rho = pr["rho"]
            return [(0, None, rho, rho * 128), (1, None, 4 + rho, rho * 128)]

        def emit_pv(pi):
            items = pair_items(pi)
            r = pair_rank(pi)
            sw = pi % 2
            vbuf = (h * 9 + r) % 2
            em = exp_ms[pi]
            dm = mask_ms.get(pi)
            starts = []
            for (grp, kt, loc, q0) in items:
                starts.append(not pv_started[grp])
                pv_started[grp] = True

            def fpv(t, items=items, sw=sw, r=r, vbuf=vbuf, h=h, em=em, dm=dm,
                    starts=tuple(starts), vld=vs_lds[h * 9 + r],
                    _unused=None,
                    epi_d=(epi_dve[0], epi_dve[1]),
                    epi_c=(epi_cp[0], epi_cp[1])):
                t.wait_ge(kb.act.h, em)
                if dm:
                    t.wait_ge(kb.dve.h, dm)
                t.wait_ge(vld[0].h, vld[1])
                last_mm = None
                for ix, (grp, kt, loc, q0) in enumerate(items):
                    st = starts[ix]
                    if st and epi_d[grp]:
                        t.wait_ge(epi_d[grp][0].h, epi_d[grp][1])
                    if st and epi_c[grp]:
                        t.wait_ge(epi_c[grp][0].h, epi_c[grp][1])
                    n = 512 - q0
                    vch = kb.VS[vbuf][:, loc * 128:(loc + 1) * 128]
                    pt = kb.PT[sw][:, ix * 512: ix * 512 + n]
                    t.matmul(kb.PS[grp][:, q0:512], vch, pt, start=st, stop=False)
                    last_mm = t.matmul(kb.PS[2 + grp][:, q0:512], kb.ONES128B[:, :], pt,
                                       start=st, stop=False)
                last_mm.then_inc(kb.pe.h, 1)
            kb.on("tensor", fpv)
            ms = kb.pe.inc()
            vs_last[h * 9 + r] = ms
            return ms

        for pi in range(NPAIRS):
            pr = SEL_PAIRS[pi]
            r = pair_rank(pi)
            ensure_k(h, r)
            ensure_v(h, r)
            sw = pi % 2
            psw = kb.PSW0 if sw == 0 else kb.PSW1
            items = pair_items(pi)
            if pr["kind"] == "past":
                bcol = (0 if items[0][0] == 0 else 32) + items[0][1] // 2
            else:
                bcol = None
            kbuf = (h * 9 + r) % 2

            ser = (os.environ.get("K_SERIAL") and pi == 0 and h > 0)

            def fqk(t, items=items, sw=sw, psw=psw, kbuf=kbuf, h=h,
                    kld=ks_lds[h * 9 + r], expw=exp_ms.get(pi - 2),
                    pw=(kb.psw0_free if sw == 0 else kb.psw1_free)
                    if (h == 0 and pi < 2) else None,
                    serw=(epi_cp[0], epi_cp[1]) if ser else None):
                t.wait_ge(kld[0].h, kld[1])
                if serw:
                    for s in serw:
                        if s:
                            t.wait_ge(s[0].h, s[1])
                if expw:
                    t.wait_ge(kb.act.h, expw)
                if pw:
                    t.wait_ge(pw[0].h, pw[1])
                for ix, (grp, kt, loc, q0) in enumerate(items):
                    n = 512 - q0
                    kch = kb.KS[kbuf][:, loc * 128:(loc + 1) * 128]
                    q = kb.A[:, h * 1024 + grp * 512 + q0: h * 1024 + (grp + 1) * 512]
                    mm = t.matmul(psw[:, ix * 512: ix * 512 + n], kch, q,
                                  start=True, stop=True)
                    if ix == len(items) - 1:
                        mm.then_inc(kb.pe.h, 1)
            kb.on("tensor", fqk)
            qk_ms[pi] = kb.pe.inc()
            ks_last[h * 9 + r] = qk_ms[pi]

            wid = 512 + (512 - items[-1][3])

            def fexp(sc, sw=sw, psw=psw, wid=wid, bcol=bcol,
                     qv=qk_ms[pi], pvw=pv_ms.get(pi - 2)):
                sc.wait_ge(kb.pe.h, qv)
                if pvw:
                    sc.wait_ge(kb.pe.h, pvw)
                bias = BT[:, bcol:bcol + 1] if bcol is not None else BT[:, 62:63]
                sc.activation(kb.PT[sw][:, 0:wid], psw[:, 0:wid], AF.Exp,
                              scale=scale, bias=bias).then_inc(kb.act.h, 1)
            kb.on("scalar", fexp)
            exp_ms[pi] = kb.act.inc()

            if pr["kind"] == "diag":
                def fmask(v, sw=sw, ev=exp_ms[pi]):
                    v.wait_ge(kb.act.h, ev)
                    v.tensor_mul(kb.PT[sw][:, 0:128], kb.PT[sw][:, 0:128], kb.TRI[:, :])
                    v.tensor_mul(kb.PT[sw][:, 512:640], kb.PT[sw][:, 512:640],
                                 kb.TRI[:, :]).then_inc(kb.dve.h, 1)
                kb.on("vector", fmask)
                mask_ms[pi] = kb.dve.inc()

            if os.environ.get("K_NOLOOK"):
                pv_ms[pi] = emit_pv(pi)
            elif pi >= 1:
                pv_ms[pi - 1] = emit_pv(pi - 1)
            if pi + 1 < NPAIRS:
                ensure_k(h, pair_rank(pi + 1))
                ensure_v(h, pair_rank(pi + 1))
        if not os.environ.get("K_NOLOOK"):
            pv_ms[NPAIRS - 1] = emit_pv(NPAIRS - 1)
        head_end = pv_ms[NPAIRS - 1]

        for grp in range(2):
            def fep_act(sc, grp=grp, he=head_end,
                        pw=epi_prev_mul[grp]):
                sc.wait_ge(kb.pe.h, he)
                if pw:
                    sc.wait_ge(pw[0].h, pw[1])
                sc.activation(kb.SCR[:, grp * 512:(grp + 1) * 512],
                              kb.PS[2 + grp][:, :], AF.Ln)
                sc.activation(kb.APL[:, grp * 512:(grp + 1) * 512],
                              kb.SCR[:, grp * 512:(grp + 1) * 512],
                              AF.Exp, scale=-1.0).then_inc(kb.act.h, 1)
            kb.on("scalar", fep_act)
            rec_ms = kb.act.inc()
            epi_cp[grp] = (kb.act, rec_ms)

            def fep_dve(v, grp=grp, h=h, rec_ms=rec_ms):
                v.wait_ge(kb.act.h, rec_ms)
                v.tensor_mul(kb.B[:, h * 1024 + grp * 512: h * 1024 + (grp + 1) * 512],
                             kb.PS[grp][:, :],
                             kb.APL[:, grp * 512:(grp + 1) * 512]).then_inc(kb.dve.h, 1)
            kb.on("vector", fep_dve)
            cp = kb.dve.inc()
            epi_dve[grp] = (kb.dve, cp)
            epi_prev_mul[grp] = (kb.dve, cp)
    for i in range(4):
        kb.ps_free[i] = epi_dve[i % 2] if i < 2 else epi_cp[i % 2]
    kb.psw0_free = (kb.act, exp_ms[NPAIRS - 2])
    kb.psw1_free = (kb.act, exp_ms[NPAIRS - 1])
    return kb.dve.n


def emit_outproj(kb, dr, L, ao_ms):
    wsrc = dr["l0_wo_t"] if L == 0 else dr["l1_wo_t"]
    bias = kb.BO if L == 0 else kb.BOUT
    res_ms = []
    for g in range(4):
        ev_last = None
        for fo in range(NCH):
            buf, wv = kb.wload(lambda fo=fo: wsrc[fo])
            p, free = kb.next_ps()

            def fmm(t, buf=buf, wv=wv, p=p, free=free, g=g, fo=fo,
                    am=ao_ms if (g == 0 and fo == 0) else None):
                if am:
                    t.wait_ge(kb.dve.h, am)
                t.wait_ge(wv[0].h, wv[1])
                if free is not None:
                    t.wait_ge(free[0].h, free[1])
                for c in range(NCH):
                    mm = t.matmul(kb.PS[p][:, 0:256], kb.W[buf][:, c * 128:(c + 1) * 128],
                                  kb.B[:, c * 1024 + g * 256: c * 1024 + (g + 1) * 256],
                                  start=(c == 0), stop=(c == NCH - 1))
                    if c == NCH - 1:
                        mm.then_inc(kb.pe.h, 1)
            kb.on("tensor", fmm)
            mm_ms = kb.pe.inc()
            kb.wconsumed(mm_ms)

            def fev(v, p=p, fo=fo, mm_ms=mm_ms):
                v.wait_ge(kb.pe.h, mm_ms)
                v.tensor_scalar_add(kb.SCR[:, fo * 256:(fo + 1) * 256],
                                    kb.PS[p][:, 0:256],
                                    bias[:, fo:fo + 1]).then_inc(kb.dve.h, 1)
            kb.on("vector", fev)
            ev_last = kb.dve.inc()
            kb.ps_free[p] = (kb.dve, ev_last)
        res_ms.append(emit_ln_post_res(
            kb, 256,
            lambda c, g=g: kb.X32[:, c * 1024 + g * 256: c * 1024 + (g + 1) * 256],
            (kb.dve, ev_last)))
    return res_ms


def emit_ffn_l0(kb, dr, xn_ms):
    gu, dw = dr["l0_gu_t"], dr["l0_dw_t"]
    res_ms = None
    for e in range(4):
        int_ms = {}
        for f in range(NCH):
            bufg, wvg = kb.wload(lambda e=e, f=f: gu[e, f])
            pg, freeg = kb.next_ps()

            def fmmg(t, buf=bufg, wv=wvg, p=pg, free=freeg, e=e,
                     xm=xn_ms if (e == 0 and f == 0) else None):
                if xm:
                    t.wait_ge(kb.dve.h, xm)
                t.wait_ge(wv[0].h, wv[1])
                if free is not None:
                    t.wait_ge(free[0].h, free[1])
                for c in range(NCH):
                    mm = t.matmul(kb.PS[p][:, 0:256], kb.W[buf][:, c * 128:(c + 1) * 128],
                                  kb.A[:, c * 1024 + e * 256: c * 1024 + (e + 1) * 256],
                                  start=(c == 0), stop=(c == NCH - 1))
                    if c == NCH - 1:
                        mm.then_inc(kb.pe.h, 1)
            kb.on("tensor", fmmg)
            msg = kb.pe.inc()
            kb.wconsumed(msg)

            bufu, wvu = kb.wload(lambda e=e, f=f: gu[e, 16 + f])
            pu, freeu = kb.next_ps()

            def fmmu(t, buf=bufu, wv=wvu, p=pu, free=freeu, e=e):
                t.wait_ge(wv[0].h, wv[1])
                if free is not None:
                    t.wait_ge(free[0].h, free[1])
                for c in range(NCH):
                    mm = t.matmul(kb.PS[p][:, 0:256], kb.W[buf][:, c * 128:(c + 1) * 128],
                                  kb.A[:, c * 1024 + e * 256: c * 1024 + (e + 1) * 256],
                                  start=(c == 0), stop=(c == NCH - 1))
                    if c == NCH - 1:
                        mm.then_inc(kb.pe.h, 1)
            kb.on("tensor", fmmu)
            msu = kb.pe.inc()
            kb.wconsumed(msu)

            sb_ = f % 2

            def fsil(sc, p=pg, sb_=sb_, msg=msg, w=int_ms.get(f - 2)):
                sc.wait_ge(kb.pe.h, msg)
                if w:
                    sc.wait_ge(kb.dve.h, w)
                sc.activation(kb.STG[:, sb_ * 256:(sb_ + 1) * 256], kb.PS[p][:, 0:256],
                              AF.Silu).then_inc(kb.act.h, 1)
            kb.on("scalar", fsil)
            sil_ms = kb.act.inc()
            kb.ps_free[pg] = (kb.act, sil_ms)

            def fint(v, p=pu, sb_=sb_, f=f, msu=msu, sil_ms=sil_ms):
                v.wait_ge(kb.pe.h, msu)
                v.wait_ge(kb.act.h, sil_ms)
                v.tensor_mul(kb.B[:, f * 256:(f + 1) * 256],
                             kb.STG[:, sb_ * 256:(sb_ + 1) * 256],
                             kb.PS[p][:, 0:256]).then_inc(kb.dve.h, 1)
            kb.on("vector", fint)
            mi = kb.dve.inc()
            int_ms[f] = mi
            kb.ps_free[pu] = (kb.dve, mi)
        last_int = kb.dve.n

        ev_last = None
        for hh in range(NCH):
            buf, wv = kb.wload(lambda e=e, hh=hh: dw[e, hh])
            p, free = kb.next_ps()

            def fmmd(t, buf=buf, wv=wv, p=p, free=free, last_int=last_int):
                t.wait_ge(kb.dve.h, last_int)
                t.wait_ge(wv[0].h, wv[1])
                if free is not None:
                    t.wait_ge(free[0].h, free[1])
                for i in range(NCH):
                    mm = t.matmul(kb.PS[p][:, 0:256], kb.W[buf][:, i * 128:(i + 1) * 128],
                                  kb.B[:, i * 256:(i + 1) * 256],
                                  start=(i == 0), stop=(i == NCH - 1))
                    if i == NCH - 1:
                        mm.then_inc(kb.pe.h, 1)
            kb.on("tensor", fmmd)
            md = kb.pe.inc()
            kb.wconsumed(md)

            def fev(v, p=p, hh=hh, md=md):
                v.wait_ge(kb.pe.h, md)
                v.tensor_copy(kb.SCR[:, hh * 256:(hh + 1) * 256],
                              kb.PS[p][:, 0:256]).then_inc(kb.dve.h, 1)
            kb.on("vector", fev)
            ev_last = kb.dve.inc()
            kb.ps_free[p] = (kb.dve, ev_last)

        res_ms = emit_ln_post_res(
            kb, 256,
            lambda c, e=e: kb.X32[:, c * 1024 + e: c * 1024 + 1024: 4],
            (kb.dve, ev_last))
    return res_ms


def emit_ffn_l1(kb, dr, xn_ms):
    wg, wu, wd = dr["l1_wg"], dr["l1_wu"], dr["l1_wd"]
    res_ms = []
    for g in range(4):
        for ihalf in range(2):
            int_ms = {}
            for f in range(32):
                gi = ihalf * 32 + f
                bufg, wvg = kb.wload(lambda gi=gi: wg[gi])
                pg, freeg = kb.next_ps()

                def fmmg(t, buf=bufg, wv=wvg, p=pg, free=freeg, g=g,
                         xm=xn_ms if (g == 0 and ihalf == 0 and f == 0) else None):
                    if xm:
                        t.wait_ge(kb.dve.h, xm)
                    t.wait_ge(wv[0].h, wv[1])
                    if free is not None:
                        t.wait_ge(free[0].h, free[1])
                    for c in range(NCH):
                        mm = t.matmul(kb.PS[p][:, 0:256], kb.W[buf][:, c * 128:(c + 1) * 128],
                                      kb.A[:, c * 1024 + g * 256: c * 1024 + (g + 1) * 256],
                                      start=(c == 0), stop=(c == NCH - 1))
                        if c == NCH - 1:
                            mm.then_inc(kb.pe.h, 1)
                kb.on("tensor", fmmg)
                msg = kb.pe.inc()
                kb.wconsumed(msg)

                bufu, wvu = kb.wload(lambda gi=gi: wu[gi])
                pu, freeu = kb.next_ps()

                def fmmu(t, buf=bufu, wv=wvu, p=pu, free=freeu, g=g):
                    t.wait_ge(wv[0].h, wv[1])
                    if free is not None:
                        t.wait_ge(free[0].h, free[1])
                    for c in range(NCH):
                        mm = t.matmul(kb.PS[p][:, 0:256], kb.W[buf][:, c * 128:(c + 1) * 128],
                                      kb.A[:, c * 1024 + g * 256: c * 1024 + (g + 1) * 256],
                                      start=(c == 0), stop=(c == NCH - 1))
                        if c == NCH - 1:
                            mm.then_inc(kb.pe.h, 1)
                kb.on("tensor", fmmu)
                msu = kb.pe.inc()
                kb.wconsumed(msu)

                sb_ = f % 2

                def fsil(sc, p=pg, sb_=sb_, gi=gi, msg=msg, w=int_ms.get(f - 2)):
                    sc.wait_ge(kb.pe.h, msg)
                    if w:
                        sc.wait_ge(kb.dve.h, w)
                    sc.activation(kb.STG[:, sb_ * 256:(sb_ + 1) * 256], kb.PS[p][:, 0:256],
                                  AF.Silu, scale=kb.SGT[:, gi:gi + 1]).then_inc(kb.act.h, 1)
                kb.on("scalar", fsil)
                sil_ms = kb.act.inc()
                kb.ps_free[pg] = (kb.act, sil_ms)

                def fint(v, p=pu, sb_=sb_, gi=gi, f=f, msu=msu, sil_ms=sil_ms):
                    v.wait_ge(kb.pe.h, msu)
                    v.wait_ge(kb.act.h, sil_ms)
                    v.scalar_tensor_tensor(kb.B[:, f * 256:(f + 1) * 256], kb.PS[p][:, 0:256],
                                           kb.SUT[:, gi:gi + 1],
                                           kb.STG[:, sb_ * 256:(sb_ + 1) * 256],
                                           op0=ALU.mult, op1=ALU.mult).then_inc(kb.dve.h, 1)
                kb.on("vector", fint)
                mi = kb.dve.inc()
                int_ms[f] = mi
                kb.ps_free[pu] = (kb.dve, mi)
            last_int = kb.dve.n

            ev_last = None
            for hh in range(NCH):
                slabs = [kb.wload(lambda hh=hh, j=j, ihalf=ihalf: wd[hh, ihalf * 2 + j])
                         for j in range(2)]
                p, free = kb.next_ps()

                def fmmd(t, slabs=slabs, p=p, free=free, last_int=last_int):
                    t.wait_ge(kb.dve.h, last_int)
                    if free is not None:
                        t.wait_ge(free[0].h, free[1])
                    for j, (buf, wv) in enumerate(slabs):
                        t.wait_ge(wv[0].h, wv[1])
                        for cc in range(16):
                            f_ = j * 16 + cc
                            mm = t.matmul(kb.PS[p][:, 0:256],
                                          kb.W[buf][:, cc * 128:(cc + 1) * 128],
                                          kb.B[:, f_ * 256:(f_ + 1) * 256],
                                          start=(j == 0 and cc == 0),
                                          stop=(j == 1 and cc == 15))
                            if j == 1 and cc == 15:
                                mm.then_inc(kb.pe.h, 1)
                kb.on("tensor", fmmd)
                md = kb.pe.inc()
                kb.wconsumed(md)
                kb.wconsumed(md)

                def fev(v, p=p, hh=hh, ihalf=ihalf, md=md):
                    v.wait_ge(kb.pe.h, md)
                    dst = kb.SCR[:, hh * 256:(hh + 1) * 256]
                    if ihalf == 0:
                        v.tensor_copy(dst, kb.PS[p][:, 0:256]).then_inc(kb.dve.h, 1)
                    else:
                        v.tensor_add(dst, dst, kb.PS[p][:, 0:256])
                        v.tensor_scalar_mul(dst, dst, kb.SDT[:, hh:hh + 1]).then_inc(kb.dve.h, 1)
                kb.on("vector", fev)
                ev_last = kb.dve.inc()
                kb.ps_free[p] = (kb.dve, ev_last)

        res_ms.append(emit_ln_post_res(
            kb, 256,
            lambda c, g=g: kb.X32[:, c * 1024 + g * 256: c * 1024 + (g + 1) * 256],
            (kb.dve, ev_last)))
    return res_ms


def emit_store(kb, y_d, res_ms):
    tgt = kb.out.inc(16)

    def f(sy, res_ms=res_ms, tgt=tgt):
        sy.wait_ge(kb.dve.h, res_ms)
        sy.dma_start(out=y_d[:, :], in_=kb.X32[:, :]).then_inc(kb.out.h, 16)
        sy.wait_ge(kb.out.h, tgt)
    kb.on("sync", f)


ALL_PHASES = ["l0attn", "l0ffn", "l1attn", "l1ffn"]


def build_program(phases, dbg=()):
    nc = bass.Bass("TRN2", target_bir_lowering=False, debug=False)
    stack = contextlib.ExitStack()
    dr = {}

    def din(name, shape, dtype):
        dr[name] = nc.dram_tensor(name, shape, dtype, kind="ExternalInput").ap()

    din("x", [128, 16 * 1024], F32)
    din("cos0", [128, 1024], BF16)
    din("sin0", [128, 1024], BF16)
    din("tri", [128, 128], BF16)
    din("r0t", [128, 128], BF16)
    din("onec32", [128, 1], F32)
    din("onecbf", [128, 1], BF16)
    din("oner32", [1, 128], F32)
    din("ones128f", [128, 128], F32)
    din("ones128b", [128, 128], BF16)
    din("onerbf", [1, 128], BF16)
    din("biastab0", [128, 64], F32)
    din("biastab1", [128, 64], F32)
    din("cosv1", [128, 16], F32)
    din("sinv1", [128, 16], F32)
    din("consts", [128, 4], F32)
    if "l0attn" in phases:
        for n in ("l0_wq_t", "l0_wk_t", "l0_wo_t"):
            din(n, [16, 128, 2048], F32)
        din("l0_wv_t", [4, 4, 128, 2048], F32)
        for n in ("l0_bq", "l0_bk", "l0_bo"):
            din(n, [128, 16], F32)
        din("l0_bv_flat", [1, 2048], F32)
    if "l0ffn" in phases:
        din("l0_gu_t", [4, 32, 128, 2048], F32)
        din("l0_dw_t", [4, 16, 128, 2048], F32)
    if "l1attn" in phases:
        din("l1_wq_t", [16, 128, 2048], F32)
        din("l1_wk_t", [16, 128, 2048], F32)
        din("l1_wv_t", [4, 4, 128, 2048], F32)
        din("l1_wo_t", [16, 128, 2048], F32)
        din("l1_bqkv", [128, 48], F32)
        din("l1_bout", [128, 16], F32)
        din("l1_bv_flat", [1, 2048], F32)
        din("l1rot", [4, 128, 128], BF16)
    if "l1ffn" in phases:
        din("l1_wg", [64, 128, 2048], I8)
        din("l1_wu", [64, 128, 2048], I8)
        din("l1_wd", [16, 4, 128, 2048], I8)
        din("l1_sg", [128, 64], F32)
        din("l1_su", [128, 64], F32)
        din("l1_sd", [128, 16], F32)
    y_d = nc.dram_tensor("y", [128, 16 * 1024], F32, kind="ExternalOutput").ap()

    with nc.Block() as block:
        kb = Kb(nc, stack, block, dbg)
        kb.phases = phases
        kb.X32 = kb.sb("X32", [128, 16 * 1024], F32)
        kb.A = kb.sb("Apool", [128, 16 * 1024], BF16)
        kb.B = kb.sb("Bpool", [128, 16 * 1024], BF16)
        kb.W = [kb.sb(f"W{i}", [128, 2048], BF16) for i in range(4)]
        kb.SCR = kb.sb("SCR", [128, 4096], F32)
        kb.STG = kb.sb("STG", [128, 4096], BF16)
        kb.KS = [kb.sb(f"KS{i}", [128, 1024], BF16) for i in range(2)]
        kb.VS = [kb.sb(f"VS{i}", [128, 1024], BF16) for i in range(2)]
        kb.PARK2 = kb.sb("PARK2", [128, 2048], BF16)
        kb.PT = [kb.sb(f"PT{i}", [128, 1024], BF16) for i in range(2)]
        kb.APL = kb.sb("APL", [128, 1024], F32)
        kb.LNS = kb.sb("LNS", [128, 2048], F32)
        kb.ROWSBF = kb.sb("ROWSBF", [1, 512], BF16)
        kb.COS0 = kb.sb("COS0", [128, 1024], BF16)
        kb.SIN0 = kb.sb("SIN0", [128, 1024], BF16)
        kb.TRI = kb.sb("TRI", [128, 128], BF16)
        kb.R0T = kb.sb("R0T", [128, 128], BF16)
        kb.ONEC32 = kb.sb("ONEC32", [128, 1], F32)
        kb.ONECBF = kb.sb("ONECBF", [128, 1], BF16)
        kb.ONER32 = kb.sb("ONER32", [1, 128], F32)
        kb.ONES128F = kb.sb("ONES128F", [128, 128], F32)
        kb.ONES128B = kb.sb("ONES128B", [128, 128], BF16)
        kb.ONERBF = kb.sb("ONERBF", [1, 128], BF16)
        kb.BT0 = kb.sb("BT0", [128, 64], F32)
        kb.BT1 = kb.sb("BT1", [128, 64], F32)
        kb.CV1 = kb.sb("CV1", [128, 16], F32)
        kb.SV1 = kb.sb("SV1", [128, 16], F32)
        kb.CONSTS = kb.sb("CONSTS", [128, 4], F32)
        if "l0attn" in phases:
            kb.BQ = kb.sb("BQ", [128, 16], F32)
            kb.BK = kb.sb("BK", [128, 16], F32)
            kb.BO = kb.sb("BO", [128, 16], F32)
        if "l1attn" in phases:
            kb.BQKV = kb.sb("BQKV", [128, 48], F32)
            kb.BOUT = kb.sb("BOUT", [128, 16], F32)
            kb.L1R = [kb.sb(f"L1R{i}", [128, 128], BF16) for i in range(4)]
        if "l1ffn" in phases:
            kb.SGT = kb.sb("SGT", [128, 64], F32)
            kb.SUT = kb.sb("SUT", [128, 64], F32)
            kb.SDT = kb.sb("SDT", [128, 16], F32)
        kb.PS = [kb.psum(f"PS{i}", [128, 512]) for i in range(4)]
        kb.PSW0 = kb.psum("PSW0", [128, 1024])
        kb.PSW1 = kb.psum("PSW1", [128, 1024])

        kb.kin, kb.kout, kb.vin, kb.vout = {}, {}, {}, {}
        for L, ph in ((0, "l0attn"), (1, "l1attn")):
            if ph in phases:
                kb.kin[L] = nc.dram_tensor(f"kin{L}", [2048, 1024], BF16).ap()
                kb.kout[L] = nc.dram_tensor(f"kout{L}", [16384, 1024], BF16,
                                            addr_space="Shared").ap()
                kb.vin[L] = nc.dram_tensor(f"vin{L}", [1024, 2048], BF16).ap()
                kb.vout[L] = nc.dram_tensor(f"vout{L}", [8192, 2048], BF16,
                                            addr_space="Shared").ap()

        kb.begin()
        emit_load(kb, dr)
        entry = [(kb.ldr, kb.loads_done)]
        res = 1
        for ph in phases:
            if ph in ("l0attn", "l1attn"):
                L = 0 if ph == "l0attn" else 1
                xn_ms = emit_ln_pre(kb, kb.X32, kb.A, entry)
                kb.dump(f"xn{L}", lambda: kb.A[:, :], [128, 16384], BF16,
                        [(kb.dve, xn_ms)])
                raw_ms = emit_kq_proj(kb, dr, L, "k", xn_ms=xn_ms)
                if L == 0:
                    emit_rope_l0(kb, raw_ms, None, True, kb.kin[0])
                    coll_k = emit_gather_one(kb, L, "k")
                    emit_v_proj(kb, dr, L, xn_ms)
                    coll_v = emit_gather_one(kb, L, "v")
                    rawq_ms = emit_kq_proj(kb, dr, L, "q")
                    q_ms = emit_rope_l0(kb, rawq_ms, kb.A, False)
                    q_ready = q_ms[(NCH - 1, 1)]
                else:
                    park = [kb.KS[0][:, :], kb.KS[1][:, :],
                            kb.VS[0][:, :], kb.VS[1][:, :],
                            kb.PARK2[:, 0:1024], kb.PARK2[:, 1024:2048],
                            kb.PT[0][:, :], kb.PT[1][:, :]]
                    km, kloc = emit_rope_l1(kb, raw_ms, False, park)
                    emit_l2_scale(kb, km, kloc, True, kb.kin[1])
                    coll_k = emit_gather_one(kb, L, "k")
                    emit_v_proj(kb, dr, L, xn_ms)
                    coll_v = emit_gather_one(kb, L, "v")
                    rawq_ms = emit_kq_proj(kb, dr, L, "q",
                                           b_free=[(kb.stg_s[0], kb.stg_s[0].n),
                                                   (kb.stg_s[1], kb.stg_s[1].n)])
                    qm, qloc = emit_rope_l1(kb, rawq_ms, True, None)
                    q_ready = emit_l2_scale(kb, qm, qloc, False, None)
                kb.dump(f"q{L}", lambda: kb.A[:, :], [128, 16384], BF16,
                        [(kb.dve, q_ready)])
                ao_ms = emit_attention(kb, L, coll_k, coll_v)
                kb.dump(f"ao{L}", lambda: kb.B[:, :], [128, 16384], BF16,
                        [(kb.dve, ao_ms)])
                res_g = emit_outproj(kb, dr, L, ao_ms)
                res = res_g[-1]
                entry = ([(kb.dve, res_g[1])], [(kb.dve, res_g[3])])
            else:
                xn_ms = emit_ln_pre(kb, kb.X32, kb.A, entry, pack4=(ph == "l0ffn"))
                if ph == "l0ffn":
                    res = emit_ffn_l0(kb, dr, xn_ms)
                    entry = [(kb.dve, res)]
                else:
                    res_g = emit_ffn_l1(kb, dr, xn_ms)
                    res = res_g[-1]
                    entry = ([(kb.dve, res_g[1])], [(kb.dve, res_g[3])])
        emit_store(kb, y_d, res)
        kb.end()
    stack.close()
    return nc, set(dr.keys()), kb.dbg_tensors


def stage_inputs(inputs):
    f32 = np.float32
    shared = {}
    x = np.asarray(inputs["L0_attn_x"][0], f32)

    def chunked_cols(a):
        return np.ascontiguousarray(np.asarray(a, f32).reshape(-1, 128).T)

    def tile16(w):
        return np.ascontiguousarray(
            np.asarray(w, f32).reshape(16, 128, 16, 128).transpose(2, 1, 0, 3)
            .reshape(16, 128, 2048))

    def tile16_T(w):
        return np.ascontiguousarray(
            np.asarray(w, f32).reshape(16, 128, 16, 128).transpose(0, 3, 2, 1)
            .reshape(16, 128, 2048))

    def tile_v(w):
        return np.ascontiguousarray(
            np.asarray(w, f32).reshape(4, 4, 128, 4, 512).transpose(0, 3, 2, 1, 4)
            .reshape(4, 4, 128, 2048))

    def tile_v_T(w):
        return np.ascontiguousarray(
            np.asarray(w, f32).reshape(4, 512, 4, 4, 128).transpose(2, 0, 4, 3, 1)
            .reshape(4, 4, 128, 2048))

    shared["l0_wq_t"] = tile16(inputs["L0_attn_Wq"])
    shared["l0_wk_t"] = tile16(inputs["L0_attn_Wk"])
    shared["l0_wv_t"] = tile_v(inputs["L0_attn_Wv"])
    shared["l0_wo_t"] = tile16(inputs["L0_attn_Wo"])
    shared["l0_bq"] = chunked_cols(inputs["L0_attn_bq"])
    shared["l0_bk"] = chunked_cols(inputs["L0_attn_bk"])
    shared["l0_bo"] = chunked_cols(inputs["L0_attn_bo"])
    shared["l0_bv_flat"] = np.asarray(inputs["L0_attn_bv"], f32).reshape(1, 2048)
    gu = np.asarray(inputs["L0_ffn_gate_up_proj"], f32)
    shared["l0_gu_t"] = np.ascontiguousarray(
        gu.reshape(4, 16, 128, 32, 128).transpose(0, 3, 2, 1, 4).reshape(4, 32, 128, 2048))
    dwn = np.asarray(inputs["L0_ffn_down_proj"], f32)
    shared["l0_dw_t"] = np.ascontiguousarray(
        dwn.reshape(4, 16, 128, 16, 128).transpose(0, 3, 2, 1, 4).reshape(4, 16, 128, 2048))
    wqkv = np.asarray(inputs["L1_attn_Wqkv"], f32)
    shared["l1_wq_t"] = tile16_T(wqkv[0:2048])
    shared["l1_wk_t"] = tile16_T(wqkv[2048:4096])
    shared["l1_wv_t"] = tile_v_T(wqkv[4096:6144])
    bqkv = np.asarray(inputs["L1_attn_bqkv"], f32)
    shared["l1_bqkv"] = chunked_cols(bqkv)
    shared["l1_bv_flat"] = np.ascontiguousarray(bqkv[4096:6144].reshape(1, 2048))
    shared["l1_wo_t"] = tile16_T(inputs["L1_attn_Wout"])
    shared["l1_bout"] = chunked_cols(inputs["L1_attn_bout"])
    wg = np.asarray(inputs["L1_ffn_w_gate"])
    wu = np.asarray(inputs["L1_ffn_w_up"])
    wd = np.asarray(inputs["L1_ffn_w_down"])
    shared["l1_wg"] = np.ascontiguousarray(
        wg.reshape(16, 128, 64, 128).transpose(2, 1, 0, 3).reshape(64, 128, 2048))
    shared["l1_wu"] = np.ascontiguousarray(
        wu.reshape(16, 128, 64, 128).transpose(2, 1, 0, 3).reshape(64, 128, 2048))
    shared["l1_wd"] = np.ascontiguousarray(
        wd.reshape(4, 16, 128, 16, 128).transpose(3, 0, 2, 1, 4).reshape(16, 4, 128, 2048))
    shared["l1_sg"] = chunked_cols(inputs["L1_ffn_s_gate"])
    shared["l1_su"] = chunked_cols(inputs["L1_ffn_s_up"])
    shared["l1_sd"] = chunked_cols(inputs["L1_ffn_s_down"])

    shared["tri"] = np.ascontiguousarray(np.triu(np.ones((128, 128), f32)).astype(BFD))
    r0t = np.zeros((128, 128), f32)
    for i in range(64):
        r0t[2 * i, 2 * i + 1] = 1.0
        r0t[2 * i + 1, 2 * i] = -1.0
    shared["r0t"] = r0t.astype(BFD)
    l1r = np.zeros((4, 128, 128), f32)
    for m in range(64):
        l1r[0, 2 * m + 1, m] = -1.0
        l1r[1, 2 * m + 1, m + 64] = -1.0
        l1r[2, 2 * m, m] = 1.0
        l1r[3, 2 * m, m + 64] = 1.0
    shared["l1rot"] = l1r.astype(BFD)
    shared["onec32"] = np.ones((128, 1), f32)
    shared["onecbf"] = np.ones((128, 1), BFD)
    shared["oner32"] = np.ones((1, 128), f32)
    shared["ones128f"] = np.ones((128, 128), f32)
    shared["ones128b"] = np.ones((128, 128), BFD)
    shared["onerbf"] = np.ones((1, 128), BFD)
    inv = np.power(f32(10000.0), -(np.arange(0, E, 2, dtype=f32) / E))
    emb = np.repeat(inv, 2)
    shared["cosv1"] = chunked_cols(np.cos(f32(S) * emb))
    shared["sinv1"] = chunked_cols(np.sin(f32(S) * emb))
    shared["consts"] = np.tile(np.array([[LN_EPS, L2_EPS, AQBIAS, 0.0]], f32), (128, 1))

    per_core = []
    for c in range(NCORES):
        d = dict(shared)
        tiles = RANK_TILES[c]
        rows = np.concatenate([x[g * 128:(g + 1) * 128] for g in tiles], 0)
        d["x"] = np.ascontiguousarray(
            rows.T.reshape(16, 128, 1024).transpose(1, 0, 2).reshape(128, 16 * 1024))
        pos = np.concatenate([np.arange(g * 128, (g + 1) * 128) for g in tiles]).astype(f32)
        exps = -np.arange(0, D, 2, dtype=f32) / D
        thetas = np.repeat(np.power(f32(10000.0), exps), 2)
        ang = thetas[:, None] * pos[None, :]
        d["cos0"] = np.ascontiguousarray(np.cos(ang).astype(BFD))
        d["sin0"] = np.ascontiguousarray(np.sin(ang).astype(BFD))
        for L, base in ((0, L0_SHIFT), (1, 0.0)):
            bt = np.full((128, 64), KILL, f32)
            for p in range(14):
                bt[:, p] = base if p < 2 * c else KILL
            for p in range(30):
                bt[:, 32 + p] = base if p < 30 - 2 * c else KILL
            bt[:, 62] = base
            d[f"biastab{L}"] = bt
        per_core.append(d)
    return per_core


LAUNCH_PLAN = [ALL_PHASES]


def kernel(**inputs):
    per_core = stage_inputs(inputs)
    for phases in LAUNCH_PLAN:
        nc, innames, _ = build_program(phases)
        in_maps = [{k: v for k, v in d.items() if k in innames} for d in per_core]
        res = run_bass_kernel_spmd(nc, in_maps, core_ids=list(range(NCORES)))
        for c in range(NCORES):
            per_core[c]["x"] = np.ascontiguousarray(res.results[c]["y"])
    out = np.zeros((S, E), np.float32)
    for c in range(NCORES):
        xc = per_core[c]["x"].reshape(128, 16, 1024).transpose(1, 0, 2).reshape(2048, 1024)
        for j, g in enumerate(RANK_TILES[c]):
            out[g * 128:(g + 1) * 128] = xc[:, j * 128:(j + 1) * 128].T
    return out[None]
